# version 19
# baseline (speedup 1.0000x reference)
import numpy as np
from contextlib import ExitStack

import concourse.bass as bass
import concourse.mybir as mybir
from concourse.ap import AP
from concourse.bass_utils import run_bass_kernel_spmd
from concourse.alu_op_type import AluOpType
from concourse import library_config

F32 = mybir.dt.float32
BF16 = mybir.dt.bfloat16
I32 = mybir.dt.int32
AF = mybir.ActivationFunctionType
AX = mybir.AxisListType
OP = AluOpType

D = 1024
H = 16
INNER = 4096
NEG = -1.0e30
EXP_OFF = 40.0


class Sem:
    __slots__ = ("h", "n")
    def __init__(self, h):
        self.h = h
        self.n = 0


class RotSem:
    CAP = 1600

    def __init__(self, handles):
        self.hs = handles
        self.n = 0
        self.seg = 0
        self.base = 0

    def loc(self, v):
        if v == 0:
            return self.hs[0], 0
        s = (v - 1) // self.CAP
        return self.hs[s % len(self.hs)], v - s * self.CAP


class Prog:
    ENGS = ("tensor", "vector", "scalar", "gpsimd", "sync")

    def __init__(self, nc):
        self.nc = nc
        self.ev = {k: [] for k in self.ENGS}

    def wait(self, eng, sem, val):
        if not val:
            return
        if isinstance(sem, RotSem):
            h, v = sem.loc(int(val))
            self.ev[eng].append(lambda e, s=h, vv_=v: e.wait_ge(s, vv_))
        else:
            self.ev[eng].append(lambda e, s=sem.h, v=int(val): e.wait_ge(s, v))

    def op(self, eng, make, inc=None, k=1):
        if inc is not None:
            if isinstance(inc, RotSem):
                inc.n += k
                h, _ = inc.loc(inc.n)
                self.ev[eng].append(lambda e, m=make, s=h, kk=k: m(e).then_inc(s, kk))
                return inc.n
            inc.n += k
            self.ev[eng].append(lambda e, m=make, s=inc.h, kk=k: m(e).then_inc(s, kk))
            return inc.n
        self.ev[eng].append(make)
        return None

    def barrier(self, sems):
        snap = [(s, s.n) for s in sems if s.n > 0]
        for eng in self.ENGS:
            for s, n in snap:
                self.wait(eng, s, n)


def build(S=2048, b_dbg=False, pad_instr=0):
    NT = S // 128
    NP = NT // 2
    QT = max(1, NT // 8)
    R = 2 * S - 1
    WEXT = 384 + S
    nc = bass.Bass("TRN2", target_bir_lowering=False)

    x_in = nc.dram_tensor("x", [S, D], F32, kind="ExternalInput").ap()
    wqkvT = nc.dram_tensor("wqkvT", [D, 3 * D], F32, kind="ExternalInput").ap()
    wb_pc = nc.dram_tensor("wb_pc", [128, 24], F32, kind="ExternalInput").ap()
    wbv_row = nc.dram_tensor("wbv_row", [1, D], F32, kind="ExternalInput").ap()
    wprojT = nc.dram_tensor("wprojT", [D, D], F32, kind="ExternalInput").ap()
    projb_row = nc.dram_tensor("projb_row", [1, D], F32, kind="ExternalInput").ap()
    lng_row = nc.dram_tensor("lng_row", [1, D], F32, kind="ExternalInput").ap()
    lnb_row = nc.dram_tensor("lnb_row", [1, D], F32, kind="ExternalInput").ap()
    rpbmt = nc.dram_tensor("rpbmt", [H * R], F32, kind="ExternalInput").ap()
    wgT = nc.dram_tensor("wgT", [D, INNER], F32, kind="ExternalInput").ap()
    wuT = nc.dram_tensor("wuT", [D, INNER], F32, kind="ExternalInput").ap()
    wdT = nc.dram_tensor("wdT", [INNER, D], F32, kind="ExternalInput").ap()
    id_in = nc.dram_tensor("id_in", [128, 128], F32, kind="ExternalInput").ap()
    j_in = nc.dram_tensor("j_in", [128, 128], F32, kind="ExternalInput").ap()
    mask_in = nc.dram_tensor("mask_in", [128, 128], F32, kind="ExternalInput").ap()

    out = nc.dram_tensor("out", [S, D], F32, kind="ExternalOutput").ap()

    x1_dram = nc.dram_tensor("x1_scratch", [S, D], F32, kind="Internal").ap()
    gu_dram = nc.dram_tensor("gu_scratch", [S, INNER], F32, kind="Internal").ap()

    dbg = {}
    if b_dbg:
        for nm, shp, dt in (("dbg_q", [128, 8 * S], BF16), ("dbg_k", [128, 8 * S], BF16),
                            ("dbg_v", [128, NT * H * 65], BF16), ("dbg_o", [128, NT * D], BF16),
                            ("dbg_x1", [S, D], F32),
                            ("dbg_xn", [128, 8 * S], BF16),
                            ("dbg_wm", [1, 8], F32)):
            dbg[nm] = nc.dram_tensor(nm, shp, dt, kind="ExternalOutput").ap()

    st = ExitStack()
    with st:
        e = st.enter_context

        tt = RotSem([e(nc.semaphore(f"sem_t{i}")) for i in range(8)])
        vv = Sem(e(nc.semaphore("sem_v")))
        ss = Sem(e(nc.semaphore("sem_s")))
        gg = Sem(e(nc.semaphore("sem_g")))
        def _sem(nm):
            return Sem(e(nc.semaphore(nm)))
        sc_const = _sem("sc_const")
        sc_x = _sem("sc_x")
        sc_p1 = _sem("sc_p1")
        sc_p2l = _sem("sc_p2l")
        sc_p2s = _sem("sc_p2s")
        sc_xr = [_sem("sc_xr0"), _sem("sc_xr1")]
        sc_x1 = [_sem("sc_x10"), _sem("sc_x11")]
        sc_x1q = _sem("sc_x1q")
        sc_out = [_sem("sc_out0"), _sem("sc_out1")]
        gc_const = _sem("gc_const")
        gc_wq = [_sem("gc_wq0"), _sem("gc_wq1")]
        gc_wv = _sem("gc_wv")
        gc_we = [_sem("gc_we0"), _sem("gc_we1")]
        gc_wp = _sem("gc_wp")
        gc_gu = [_sem("gc_gu0"), _sem("gc_gu1")]
        gc_wd = [_sem("gc_wd0"), _sem("gc_wd1")]
        gc_dbg = _sem("gc_dbg")
        ALLSEMS = [tt, vv, ss, gg, sc_const, sc_x, sc_p1, sc_p2l, sc_p2s,
                   sc_xr[0], sc_xr[1], sc_x1[0], sc_x1[1], sc_x1q,
                   sc_out[0], sc_out[1], gc_const, gc_wq[0], gc_wq[1], gc_wv,
                   gc_we[0], gc_we[1], gc_wp, gc_gu[0], gc_gu[1],
                   gc_wd[0], gc_wd[1], gc_dbg]

        P = Prog(nc)

        ones_bf = e(nc.sbuf_tensor("ones_bf", [128, 128], BF16))
        ones_f32 = e(nc.sbuf_tensor("ones_f32", [1, 128], F32))
        onec_f32 = e(nc.sbuf_tensor("onec_f32", [128, 1], F32))
        id_bf = e(nc.sbuf_tensor("id_bf", [128, 128], BF16))
        j_bf = e(nc.sbuf_tensor("j_bf", [128, 128], BF16))
        mask_sb = e(nc.sbuf_tensor("mask_sb", [128, 128], F32))
        eps_sb = e(nc.sbuf_tensor("eps_sb", [128, 1], F32))
        wb_sb = e(nc.sbuf_tensor("wb_sb", [128, 24], F32))
        wbq_sb = e(nc.sbuf_tensor("wbq_sb", [128, 8], F32))
        wbv_bf = e(nc.sbuf_tensor("wbv_bf", [1, D], BF16))
        negmh = e(nc.sbuf_tensor("negmh", [128, 1], F32))
        wstat = e(nc.sbuf_tensor("wstat", [1, 8], F32))
        swg_b = e(nc.sbuf_tensor("swg_b", [128, 3], F32))
        mwg_b = e(nc.sbuf_tensor("mwg_b", [128, 3], F32))
        stO = ExitStack()
        stQKV = ExitStack()
        o_sb = stO.enter_context(nc.sbuf_tensor("o_sb", [128, NT, D], BF16))
        q_sb = stQKV.enter_context(nc.sbuf_tensor("q_sb", [128, 8, S], BF16))
        k_sb = stQKV.enter_context(nc.sbuf_tensor("k_sb", [128, 8, S], BF16))
        v_sb = stQKV.enter_context(nc.sbuf_tensor("v_sb", [128, NT, H * 65], BF16))

        P.op("sync", lambda E: E.dma_start(out=wb_sb[:], in_=wb_pc), sc_const, 16)
        a0_ds = P.op("sync", lambda E: E.dma_start(out=mask_sb[:], in_=mask_in), sc_const, 16)
        P.op("gpsimd", lambda E: E.dma_start(out=id_bf[:], in_=id_in), gc_const, 16)
        P.op("gpsimd", lambda E: E.dma_start(out=j_bf[:], in_=j_in), gc_const, 16)
        a0_dg = P.op("gpsimd", lambda E: E.dma_start(out=wbv_bf[:], in_=wbv_row), gc_const, 16)
        for _pi in range(pad_instr):
            P.op("vector", lambda E: E.memset(eps_sb[:], 1e-5))
        if pad_instr:
            with nc.psum_tensor("ps_pad", [128, 128], F32) as ps_pad:
                P.wait("tensor", gc_const, a0_dg)
                for _pi in range(pad_instr):
                    P.op("tensor", lambda E: E.matmul(ps_pad[:], id_bf[:], id_bf[:],
                                                      start=True, stop=True), tt)
        P.op("vector", lambda E: E.memset(eps_sb[:], 1e-5))
        P.op("vector", lambda E: E.memset(wstat[:], 0.0))
        P.op("vector", lambda E: E.memset(negmh[:], 0.0))
        P.op("vector", lambda E: E.memset(ones_bf[:], 1.0))
        P.op("vector", lambda E: E.memset(ones_f32[:], 1.0))
        P.op("vector", lambda E: E.memset(onec_f32[:], 1.0))
        P.op("vector", lambda E: E.memset(
            AP(v_sb[:].tensor, 64, [[NT * H * 65, 128], [H * 65, NT], [65, H]]), 1.0), vv)
        P.wait("vector", sc_const, a0_ds)
        v_wbq = P.op("vector", lambda E: E.tensor_scalar(
            wbq_sb[:], wb_sb[:, 0:8], 1.0 / 32.0, None, OP.mult), vv)

        with ExitStack() as stA:
            eA = stA.enter_context
            TBA = 2
            x_sb = eA(nc.sbuf_tensor("x_sb", [128, TBA, D], F32))
            stats = eA(nc.sbuf_tensor("stats", [128, TBA, 12], F32))
            mv = eA(nc.sbuf_tensor("mv", [128, TBA, 2], F32))
            lnv = eA(nc.sbuf_tensor("lnv", [128, TBA], F32))
            rstd = eA(nc.sbuf_tensor("rstd", [128, TBA], F32))
            xn_bf = eA(nc.sbuf_tensor("xn_bf", [128, TBA, D], BF16))
            xn_fm = eA(nc.sbuf_tensor("xn_fm", [128, 8, S], BF16))
            wqkv_o = eA(nc.sbuf_tensor("wqkv_o", [128, 2, 8, 128], BF16))
            wv_bf = eA(nc.sbuf_tensor("wv_bf", [128, 8, D], BF16))
            ps_tr = [eA(nc.psum_tensor(f"ps_tr{i}", [128, 512], BF16)) for i in range(2)]
            ps_qk = [eA(nc.psum_tensor(f"ps_qk{i}", [128, 512], F32)) for i in range(2)]
            ps_v = [eA(nc.psum_tensor(f"ps_v{i}", [128, D], F32)) for i in range(2)]

            P.wait("tensor", gc_const, a0_dg)

            NTB = NT // TBA
            tr_done = [0, 0]
            x_free = 0
            for tb in range(NTB):
                P.wait("sync", vv, x_free)
                d_x = P.op("sync", lambda E, tb=tb: E.dma_start(
                    out=x_sb[:],
                    in_=AP(x_in.tensor, tb * TBA * 128 * D, [[D, 128], [128 * D, TBA], [1, D]])), sc_x, 16)
                P.wait("vector", sc_x, d_x)
                for t in range(TBA):
                    P.op("vector", lambda E, t=t: E.bn_stats(stats[:, t, 0:6], x_sb[:, t, 0:512]))
                    last = P.op("vector", lambda E, t=t: E.bn_stats(stats[:, t, 6:12], x_sb[:, t, 512:1024]), vv)
                P.wait("vector", vv, last)
                for t in range(TBA):
                    agg = P.op("vector", lambda E, t=t: E.bn_aggr(mv[:, t, :], stats[:, t, :]), vv)
                P.wait("scalar", vv, agg)
                ln1 = P.op("scalar", lambda E: E.activation(lnv[:], mv[:, :, 1], AF.Ln, bias=eps_sb[:], scale=1.0), ss)
                P.wait("scalar", ss, ln1)
                rs = P.op("scalar", lambda E: E.activation(rstd[:], lnv[:], AF.Exp, scale=-0.5), ss)
                P.wait("vector", ss, rs)
                for t in range(TBA):
                    x_free = P.op("vector", lambda E, t=t: E.tensor_scalar(
                        xn_bf[:, t, :], x_sb[:, t, :], mv[:, t, 0:1], rstd[:, t:t + 1],
                        OP.subtract, OP.mult), vv)
                for c in range(8):
                    par = c % 2
                    P.wait("tensor", vv, x_free)
                    P.wait("tensor", vv, tr_done[par])
                    for t in range(TBA):
                        tp = P.op("tensor", lambda E, c=c, t=t, par=par: E.transpose(
                            ps_tr[par][:, t * 128:(t + 1) * 128],
                            xn_bf[:, t, c * 128:(c + 1) * 128], id_bf[:]), tt)
                    P.wait("vector", tt, tp)
                    tr_done[par] = P.op("vector", lambda E, c=c, tb=tb, par=par: E.tensor_copy(
                        xn_fm[:, c, tb * TBA * 128:(tb + 1) * TBA * 128], ps_tr[par][:, 0:TBA * 128]), vv)

            if b_dbg:
                P.wait("gpsimd", vv, max(tr_done))
                P.wait("gpsimd", gc_dbg, gc_dbg.n)
                _dmp = P.op("gpsimd", lambda E: E.dma_start(out=dbg["dbg_xn"], in_=xn_fm[:].rearrange("p a b -> p (a b)")), gc_dbg, 16)
                for _e in ("tensor", "vector", "scalar", "sync"):
                    P.wait(_e, gc_dbg, gc_dbg.n)
            xfm_all = max(tr_done)
            P.wait("tensor", vv, xfm_all)
            P.wait("scalar", sc_const, a0_ds)
            P.wait("scalar", vv, v_wbq)
            w_free = [0, 0]
            qk_ev = [0, 0]
            NF = S // 512
            for o in range(16):
                par = o % 2
                P.wait("gpsimd", tt, w_free[par])
                d_w = P.op("gpsimd", lambda E, o=o, par=par: E.dma_start(
                    out=wqkv_o[:, par, :, :],
                    in_=AP(wqkvT.tensor, o * 128, [[3 * D, 128], [128 * 3 * D, 8], [1, 128]])), gc_wq[par], 16)
                P.wait("tensor", gc_wq[par], d_w)
                for f in range(NF):
                    fpar = (o * NF + f) % 2
                    P.wait("tensor", ss, qk_ev[fpar])
                    for kc in range(8):
                        mm = P.op("tensor", lambda E, par=par, kc=kc, f=f, fpar=fpar: E.matmul(
                            ps_qk[fpar][:], wqkv_o[:, par, kc, :],
                            xn_fm[:, kc, f * 512:(f + 1) * 512],
                            start=(kc == 0), stop=(kc == 7)), tt)
                    if f == NF - 1:
                        w_free[par] = mm
                    P.wait("scalar", tt, mm)
                    if o < 8:
                        qk_ev[fpar] = P.op("scalar", lambda E, o=o, f=f, fpar=fpar: E.activation(
                            q_sb[:, o, f * 512:(f + 1) * 512], ps_qk[fpar][:], AF.Identity,
                            bias=wbq_sb[:, o:o + 1], scale=1.0 / 32.0), ss)
                    else:
                        qk_ev[fpar] = P.op("scalar", lambda E, o=o, f=f, fpar=fpar: E.activation(
                            k_sb[:, o - 8, f * 512:(f + 1) * 512], ps_qk[fpar][:], AF.Identity,
                            bias=wb_sb[:, o:o + 1], scale=1.0), ss)

            P.wait("gpsimd", tt, max(w_free))
            d_wv = P.op("gpsimd", lambda E: E.dma_start(
                out=wv_bf[:],
                in_=AP(wqkvT.tensor, 2048, [[3 * D, 128], [128 * 3 * D, 8], [1, D]])), gc_wv, 16)
            P.wait("tensor", gc_wv, d_wv)
            v_ev = [0, 0]
            for t in range(NT):
                par = t % 2
                P.wait("tensor", vv, v_ev[par])
                for fh in range(2):
                    P.op("tensor", lambda E, fh=fh, par=par: E.matmul(
                        ps_v[par][:, fh * 512:(fh + 1) * 512], ones_bf[0:1, :],
                        wbv_bf[:, fh * 512:(fh + 1) * 512], start=True, stop=False))
                for kc in range(8):
                    for fh in range(2):
                        mm = P.op("tensor", lambda E, t=t, kc=kc, fh=fh, par=par: E.matmul(
                            ps_v[par][:, fh * 512:(fh + 1) * 512],
                            xn_fm[:, kc, t * 128:(t + 1) * 128],
                            wv_bf[:, kc, fh * 512:(fh + 1) * 512],
                            start=False, stop=(kc == 7)), tt)
                P.wait("vector", tt, mm)
                v_ev[par] = P.op("vector", lambda E, t=t, par=par: E.tensor_copy(
                    AP(v_sb[:].tensor, t * (H * 65), [[NT * H * 65, 128], [65, H], [1, 64]]),
                    AP(ps_v[par][:].tensor, 0, [[D, 128], [64, H], [1, 64]])), vv)

        P.barrier(ALLSEMS)
        P.barrier(ALLSEMS)
        if b_dbg:
            P.wait("gpsimd", gc_dbg, gc_dbg.n)
            _dmp = P.op("gpsimd", lambda E: E.dma_start(out=dbg["dbg_q"], in_=q_sb[:].rearrange("p a b -> p (a b)")), gc_dbg, 16)
            for _e in ("tensor", "vector", "scalar", "sync"):
                P.wait(_e, gc_dbg, gc_dbg.n)
            P.wait("gpsimd", gc_dbg, gc_dbg.n)
            _dmp = P.op("gpsimd", lambda E: E.dma_start(out=dbg["dbg_k"], in_=k_sb[:].rearrange("p a b -> p (a b)")), gc_dbg, 16)
            for _e in ("tensor", "vector", "scalar", "sync"):
                P.wait(_e, gc_dbg, gc_dbg.n)
            P.wait("gpsimd", gc_dbg, gc_dbg.n)
            _dmp = P.op("gpsimd", lambda E: E.dma_start(out=dbg["dbg_v"], in_=v_sb[:].rearrange("p a b -> p (a b)")), gc_dbg, 16)
            for _e in ("tensor", "vector", "scalar", "sync"):
                P.wait(_e, gc_dbg, gc_dbg.n)
        with ExitStack() as stB:
            eB = stB.enter_context
            wext = eB(nc.sbuf_tensor("wext", [128, 2, WEXT], BF16))
            e_sb = eB(nc.sbuf_tensor("e_sb", [128, 2, WEXT], BF16))
            p_raw = eB(nc.sbuf_tensor("p_raw", [128, 4, 1024], F32))
            p_sb = eB(nc.sbuf_tensor("p_sb", [128, 4, 1024], BF16))
            recl = eB(nc.sbuf_tensor("recl", [128, 2, 4], F32))
            wchunk = eB(nc.sbuf_tensor("wchunk", [128, 2, 1024], F32))
            wabs = eB(nc.sbuf_tensor("wabs", [128, 2, 1024], F32))
            wsum_sb = eB(nc.sbuf_tensor("wsum_sb", [1, 512], F32))
            ps_st = [eB(nc.psum_tensor(f"ps_st{i}", [128, 1024], F32)) for i in range(2)]
            ps_o = [eB(nc.psum_tensor(f"ps_o{i}", [128, 260], F32)) for i in range(2)]
            ps_ws = eB(nc.psum_tensor("ps_ws", [1, 512], F32))

            P.wait("scalar", vv, v_wbq)

            w_srcs = [wgT, wuT, wdT]
            p1_units = [(wi, cc, hf) for wi in range(3) for cc in range(8) for hf in range(4)]
            p1 = {"dma": [0, 0], "abs": [0, 0], "mm": 0, "mm2": [0, 0], "ev": 0, "wsum_ev": 0}
            p1sems = [sc_p1, sc_x]

            def p1_issue_dma(u):
                wi, cc, hf = p1_units[u]
                wsrc = w_srcs[wi]
                upar = u % 2
                P.wait("sync", vv, p1["abs"][upar])

                def mk(E, wsrc=wsrc, cc=cc, hf=hf, upar=upar):
                    if wsrc is wdT:
                        src = AP(wsrc.tensor, (cc * 512 + hf * 128) * D, [[D, 128], [1, D]])
                        return E.dma_start(out=wchunk[:, upar, :], in_=src)
                    src = AP(wsrc.tensor, cc * 128 * INNER + hf * 1024, [[INNER, 128], [1, 1024]])
                    return E.dma_start(out=wchunk[:, upar, :], in_=src)
                p1["dma"][upar] = P.op("sync", mk, p1sems[upar], 16)

            def p1_abs(u):
                upar = u % 2
                P.wait("vector", p1sems[upar], p1["dma"][upar])
                P.wait("vector", tt, p1["mm2"][upar])
                ab = P.op("vector", lambda E, upar=upar: E.scalar_tensor_tensor(
                    wabs[:, upar, :], wchunk[:, upar, :], -1.0, wchunk[:, upar, :],
                    OP.mult, OP.max), vv)
                p1["abs"][upar] = ab

            def p1_unit_compute(u):
                wi = u // 32
                upar = u % 2
                P.wait("tensor", vv, p1["abs"][upar])
                if u % 32 == 0:
                    P.wait("tensor", vv, p1["wsum_ev"])
                for f in range(2):
                    mm = P.op("tensor", lambda E, f=f, u=u, upar=upar: E.matmul(
                        ps_ws[:], onec_f32[:], wabs[:, upar, f * 512:(f + 1) * 512],
                        start=(u % 32 == 0 and f == 0), stop=(u % 32 == 31 and f == 1)), tt)
                p1["mm"] = mm
                p1["mm2"][upar] = mm
                if u % 32 == 31:
                    P.wait("vector", tt, p1["mm"])
                    ev = P.op("vector", lambda E: E.tensor_copy(wsum_sb[:], ps_ws[:]), vv)
                    p1["wsum_ev"] = ev
                    P.wait("vector", vv, ev)
                    wdt2 = 256
                    while wdt2 >= 1:
                        sm = P.op("vector", lambda E, w2=wdt2: E.tensor_tensor(
                            wsum_sb[:, 0:w2], wsum_sb[:, 0:w2], wsum_sb[:, w2:2 * w2], OP.add), vv)
                        P.wait("vector", vv, sm)
                        wdt2 //= 2
                    sm = P.op("vector", lambda E, wi=wi: E.tensor_copy(
                        wstat[:, wi:wi + 1], wsum_sb[:, 0:1]), vv)
                    P.wait("vector", vv, sm)
                    mc = P.op("vector", lambda E, wi=wi: E.tensor_scalar(
                        wstat[:, wi:wi + 1], wstat[:, wi:wi + 1], 1.0 / float(D * INNER), 1e-5,
                        OP.mult, OP.max), vv)
                    P.wait("vector", vv, mc)
                    p1["ev"] = P.op("vector", lambda E, wi=wi: E.reciprocal(
                        wstat[:, 3 + wi:4 + wi], wstat[:, wi:wi + 1]), vv)

            p1_prog = {"mm": 0, "abs": 0, "issued": 0, "quads": 0}
            TOTQ = H * (NP // 2)

            def p1_tick():
                p1_prog["quads"] += 1
                base = p1_prog["quads"] * 96 // TOTQ
                while p1_prog["mm"] < base:
                    while p1_prog["abs"] <= p1_prog["mm"] + 1 and p1_prog["abs"] < 96:
                        p1_abs(p1_prog["abs"])
                        p1_prog["abs"] += 1
                        if p1_prog["issued"] < 96:
                            p1_issue_dma(p1_prog["issued"])
                            p1_prog["issued"] += 1
                    p1_unit_compute(p1_prog["mm"])
                    p1_prog["mm"] += 1

            p1_issue_dma(0)
            p1_issue_dma(1)
            p1_prog["issued"] = 2

            wx_free = [0, 0]
            wx_ready = [0, 0]
            st_free = [0, 0]
            pv_done = [0, 0, 0, 0]
            pmul = [0, 0, 0, 0]
            e_used = [0, 0]
            e_ready = [0, 0]
            fifo = []
            o_ev = [0, 0]
            gpar = [0]
            b_ss, b_vv = ss.n, vv.n

            def emit_head(h):
                par = h % 2
                P.wait("gpsimd", tt, wx_free[par])
                P.wait("gpsimd", vv, wx_ready[par])
                d_w = P.op("gpsimd", lambda E, h=h, par=par: E.dma_start(
                    out=wext[:, par, :],
                    in_=AP(rpbmt.tensor, h * R + (S - 512), [[1, 128], [1, WEXT]])), gc_we[par], 16)
                P.wait("vector", gc_we[par], d_w)
                P.wait("vector", tt, wx_free[par])
                P.op("vector", lambda E, par=par: E.memset(wext[:, par, 0:384], NEG))
                wx_ready[par] = P.op("vector", lambda E, par=par: E.tensor_tensor(
                    wext[:, par, 384:512], wext[:, par, 384:512], mask_sb[:], OP.add), vv)
                nE = (WEXT + 511) // 512
                for ei in range(nE):
                    w0 = ei * 512
                    w1 = min(WEXT, w0 + 512)
                    mypar = gpar[0] % 2
                    gpar[0] += 1
                    P.wait("tensor", ss, st_free[mypar])
                    if ei == 0:
                        P.wait("tensor", vv, wx_ready[par])
                    jm = P.op("tensor", lambda E, w0=w0, w1=w1, par=par, mypar=mypar: E.matmul(
                        ps_st[mypar][:, 0:w1 - w0], j_bf[:], wext[:, par, w0:w1],
                        start=True, stop=True), tt)
                    if ei == nE - 1:
                        wx_free[par] = jm
                    P.wait("scalar", tt, jm)
                    if ei == 0:
                        P.wait("scalar", vv, e_used[par])
                    st_free[mypar] = P.op("scalar", lambda E, w0=w0, w1=w1, par=par,
                                          mypar=mypar: E.activation(
                        e_sb[:, par, w0:w1], ps_st[mypar][:, 0:w1 - w0], AF.Exp,
                        bias=negmh[:], scale=1.0), ss)
                e_ready[par] = st_free[(gpar[0] - 1) % 2]

                first_use = True
                hh, hc = 64 * (h % 2), h // 2
                for qd in range(NP // 2):
                    nblocks = 4 * qd + 4
                    groups = [[b0, b0 + 1] for b0 in range(0, nblocks, 2)]
                    qdpar = qd % 2
                    pv_first = True
                    for grp in groups:
                        m2 = gpar[0] % 2
                        m3 = gpar[0] % 4
                        gpar[0] += 1
                        if first_use:
                            P.wait("tensor", ss, b_ss)
                            P.wait("tensor", vv, b_vv)
                            first_use = False
                        P.wait("tensor", ss, st_free[m2])
                        for bi, kc in enumerate(grp):
                            qm = P.op("tensor", lambda E, kc=kc, bi=bi, m2=m2, qd=qd,
                                      hh=hh, hc=hc: E.matmul(
                                ps_st[m2][:, bi * 512:(bi + 1) * 512],
                                k_sb[hh:hh + 64, hc, kc * 128:(kc + 1) * 128],
                                q_sb[hh:hh + 64, hc, qd * 512:(qd + 1) * 512],
                                start=True, stop=True), tt)
                        P.wait("scalar", tt, qm)
                        P.wait("scalar", vv, pmul[m3])
                        st_free[m2] = P.op("scalar", lambda E, m2=m2, m3=m3: E.activation(
                            p_raw[:, m3, :], ps_st[m2][:], AF.Exp,
                            bias=negmh[:], scale=1.0), ss)
                        while len(fifo) >= 3:
                            fifo.pop(0)()
                        P.wait("vector", ss, st_free[m2])
                        P.wait("vector", ss, e_ready[par])
                        P.wait("vector", tt, pv_done[m3])
                        for bi, kc in enumerate(grp):
                            off = 384 + (4 * qd - kc) * 128
                            pmul[m3] = P.op("vector", lambda E, off=off, bi=bi, par=par,
                                            m3=m3: E.tensor_tensor(
                                p_sb[:, m3, bi * 512:(bi + 1) * 512],
                                p_raw[:, m3, bi * 512:(bi + 1) * 512],
                                e_sb[:, par, off:off + 512], OP.mult), vv)
                        e_used[par] = pmul[m3]

                        def mk_pv(grp=grp, m3=m3, qdpar=qdpar, h=h, stt=pv_first,
                                  nblocks=nblocks, pm_val=pmul[m3]):
                            P.wait("tensor", vv, pm_val)
                            if stt:
                                P.wait("tensor", ss, o_ev[qdpar])
                            pm = 0
                            for bi, kc in enumerate(grp):
                                for qs in range(4):
                                    pm = P.op("tensor", lambda E, kc=kc, bi=bi, qs=qs, h=h,
                                              m3=m3, qdpar=qdpar,
                                              stt=(stt and bi == 0 and qs == 0),
                                              spp=(kc == nblocks - 1 and qs == 3): E.matmul(
                                        ps_o[qdpar][:, qs * 65:qs * 65 + 65],
                                        p_sb[:, m3, bi * 512 + qs * 128:bi * 512 + qs * 128 + 128],
                                        v_sb[:, kc, h * 65:h * 65 + 65],
                                        start=stt, stop=spp), tt)
                            pv_done[m3] = pm

                        fifo.append(mk_pv)
                        pv_first = False

                    def mk_evac(qd=qd, qdpar=qdpar, h=h, m3_last=(gpar[0] - 1) % 4):
                        P.wait("vector", tt, pv_done[m3_last])
                        P.wait("vector", ss, o_ev[qdpar])
                        rr = P.op("vector", lambda E, qdpar=qdpar: E.reciprocal(
                            recl[:, qdpar, :],
                            AP(ps_o[qdpar][:].tensor, 64, [[260, 128], [65, 4]])), vv)
                        P.wait("scalar", vv, rr)
                        for qs in range(4):
                            o_ev[qdpar] = P.op("scalar", lambda E, qs=qs, qd=qd, h=h,
                                               qdpar=qdpar: E.activation(
                                o_sb[:, 4 * qd + qs, h * 64:h * 64 + 64],
                                ps_o[qdpar][:, qs * 65:qs * 65 + 64],
                                AF.Identity, scale=recl[:, qdpar, qs:qs + 1]), ss)

                    fifo.append(mk_evac)
                    p1_tick()

            for h in range(H):
                emit_head(h)
            while fifo:
                fifo.pop(0)()

            b_end_ss = ss.n
            if b_dbg:
                P.wait("gpsimd", ss, b_end_ss)
                P.wait("gpsimd", gc_dbg, gc_dbg.n)
                _dmp = P.op("gpsimd", lambda E: E.dma_start(out=dbg["dbg_o"], in_=o_sb[:].rearrange("p a b -> p (a b)")), gc_dbg, 16)
                for _e in ("tensor", "vector", "scalar", "sync"):
                    P.wait(_e, gc_dbg, gc_dbg.n)

        stQKV.close()
        P.barrier(ALLSEMS)
        with ExitStack() as stB2:
            eB2 = stB2.enter_context
            wproj_bf = eB2(nc.sbuf_tensor("wproj_bf", [128, 8, D], BF16))
            o_fm = eB2(nc.sbuf_tensor("o_fm", [128, 8, 128], BF16))
            x_rld = eB2(nc.sbuf_tensor("x_rld", [128, 2, D], F32))
            xn1 = eB2(nc.sbuf_tensor("xn1", [128, 2, D], F32))
            ybuf = eB2(nc.sbuf_tensor("ybuf", [128, D], F32))
            zbuf = eB2(nc.sbuf_tensor("zbuf", [128, D], F32))
            x1buf = eB2(nc.sbuf_tensor("x1buf", [128, 2, D], F32))
            st2 = eB2(nc.sbuf_tensor("st2", [128, 12], F32))
            mv2 = eB2(nc.sbuf_tensor("mv2", [128, 2], F32))
            rs2 = eB2(nc.sbuf_tensor("rs2", [128, 1], F32))
            nb2 = eB2(nc.sbuf_tensor("nb2", [128, 1], F32))
            mva = eB2(nc.sbuf_tensor("mva", [128, 2, 2], F32))
            rsa = eB2(nc.sbuf_tensor("rsa", [128, 2], F32))
            nba = eB2(nc.sbuf_tensor("nba", [128, 2], F32))
            g_bc = eB2(nc.sbuf_tensor("g_bc", [128, D], F32))
            b_bc = eB2(nc.sbuf_tensor("b_bc", [128, D], F32))
            pb_bc = eB2(nc.sbuf_tensor("pb_bc", [128, D], F32))
            rows = eB2(nc.sbuf_tensor("rows", [1, 3 * D], F32))
            ps_tr2 = [eB2(nc.psum_tensor(f"ps_tr2{i}", [128, 512], BF16)) for i in range(2)]
            ps_pj = [eB2(nc.psum_tensor(f"ps_pj{i}", [128, 512], F32)) for i in range(2)]

            P.op("sync", lambda E: E.dma_start(out=rows[:, 0:D], in_=projb_row), sc_const, 16)
            P.op("sync", lambda E: E.dma_start(out=rows[:, D:2 * D], in_=lng_row), sc_const, 16)
            d_r2 = P.op("sync", lambda E: E.dma_start(out=rows[:, 2 * D:3 * D], in_=lnb_row), sc_const, 16)
            d_wp = P.op("gpsimd", lambda E: E.dma_start(
                out=wproj_bf[:],
                in_=AP(wprojT.tensor, 0, [[D, 128], [128 * D, 8], [1, D]])), gc_wp, 16)
            P.wait("tensor", sc_const, d_r2)
            P.wait("tensor", vv, p1["ev"])
            bc_dsts = [pb_bc, g_bc, b_bc]
            bc_ev = [0, 0]
            for bi3 in range(3):
                for fh in range(2):
                    P.wait("tensor", vv, bc_ev[fh])
                    bm = P.op("tensor", lambda E, bi3=bi3, fh=fh: E.matmul(
                        ps_pj[fh][:], ones_f32[:], rows[:, bi3 * D + fh * 512:bi3 * D + (fh + 1) * 512],
                        start=True, stop=True), tt)
                    P.wait("vector", tt, bm)
                    bcd = P.op("vector", lambda E, bi3=bi3, fh=fh: E.tensor_copy(
                        bc_dsts[bi3][:, fh * 512:(fh + 1) * 512], ps_pj[fh][:]), vv)
                    bc_ev[fh] = bcd
            P.wait("tensor", vv, bc_ev[0])
            bm = P.op("tensor", lambda E: E.matmul(ps_pj[0][:, 0:8], ones_f32[:], wstat[:],
                                                   start=True, stop=True), tt)
            P.wait("vector", tt, bm)
            P.op("vector", lambda E: E.tensor_copy(mwg_b[:], ps_pj[0][:, 0:3]))
            swb = P.op("vector", lambda E: E.tensor_copy(swg_b[:], ps_pj[0][:, 3:6]), vv)
            if b_dbg:
                P.wait("gpsimd", vv, swb)
                P.wait("gpsimd", gc_dbg, gc_dbg.n)
                _dmp = P.op("gpsimd", lambda E: E.dma_start(out=dbg["dbg_wm"], in_=wstat[:]), gc_dbg, 16)
                for _e in ("tensor", "vector", "scalar", "sync"):
                    P.wait(_e, gc_dbg, gc_dbg.n)


            tr2_ev = [0, 0]
            ypj = [0, 0]
            x1_st = [0, 0]
            x_free = [0, 0]
            x_free_s = [0, 0]
            s1_apply = [0, 0]
            ytt = [0, 0]
            zap = 0
            x1ap = 0
            ztt2 = 0
            pm_box = [0]
            d_x = [0, 0]

            def b2_dma(qi):
                par = qi % 2
                P.wait("sync", vv, x_free[par])
                P.wait("sync", ss, x_free_s[par])
                d_x[par] = P.op("sync", lambda E, qi=qi, par=par: E.dma_start(
                    out=x_rld[:, par, :], in_=x_in[qi * 128:(qi + 1) * 128, :]), sc_xr[par], 16)

            def b2_s1(qi):
                par = qi % 2
                P.wait("vector", sc_xr[par], d_x[par])
                if qi == 0:
                    P.wait("vector", vv, bcd)
                P.op("vector", lambda E, par=par: E.bn_stats(st2[:, 0:6], x_rld[:, par, 0:512]))
                sb = P.op("vector", lambda E, par=par: E.bn_stats(st2[:, 6:12], x_rld[:, par, 512:1024]), vv)
                P.wait("vector", vv, sb)
                ag = P.op("vector", lambda E, par=par: E.bn_aggr(mva[:, par, :], st2[:]), vv)
                P.wait("scalar", vv, ag)
                lnx = P.op("scalar", lambda E, par=par: E.activation(
                    rsa[:, par:par + 1], mva[:, par, 1:2], AF.Ln, bias=eps_sb[:], scale=1.0), ss)
                P.wait("scalar", ss, lnx)
                rr = P.op("scalar", lambda E, par=par: E.activation(
                    rsa[:, par:par + 1], rsa[:, par:par + 1], AF.Exp, scale=-0.5), ss)
                P.wait("vector", ss, rr)
                nb = P.op("vector", lambda E, par=par: E.scalar_tensor_tensor(
                    nba[:, par:par + 1], mva[:, par, 0:1], -1.0, rsa[:, par:par + 1],
                    OP.mult, OP.mult), vv)
                P.wait("scalar", vv, nb)
                P.wait("scalar", vv, ytt[par])
                s1_apply[par] = P.op("scalar", lambda E, par=par: E.activation(
                    xn1[:, par, :], x_rld[:, par, :], AF.Identity,
                    bias=nba[:, par:par + 1], scale=rsa[:, par:par + 1]), ss)
                x_free_s[par] = s1_apply[par]

            def b2_pe(qi):
                P.wait("tensor", ss, b_end_ss)
                for cg in range(2):
                    tpp = cg % 2
                    P.wait("tensor", ss, tr2_ev[tpp])
                    P.wait("scalar", tt, pm_box[0])
                    for c4 in range(4):
                        c = cg * 4 + c4
                        tp = P.op("tensor", lambda E, qi=qi, c=c, c4=c4, tpp=tpp: E.transpose(
                            ps_tr2[tpp][:, c4 * 128:(c4 + 1) * 128],
                            o_sb[:, qi, c * 128:(c + 1) * 128], id_bf[:]), tt)
                    P.wait("scalar", tt, tp)
                    tr2_ev[tpp] = P.op("scalar", lambda E, cg=cg, tpp=tpp: E.activation(
                        o_fm[:].rearrange("p a b -> p (a b)")[:, cg * 512:(cg + 1) * 512],
                        ps_tr2[tpp][:], AF.Identity, scale=1.0), ss)
                P.wait("tensor", gc_wp, d_wp)
                P.wait("tensor", ss, max(tr2_ev))
                mm = 0
                for fh in range(2):
                    P.wait("tensor", vv, ypj[fh])
                    for kc in range(8):
                        mm = P.op("tensor", lambda E, fh=fh, kc=kc: E.matmul(
                            ps_pj[fh][:], o_fm[:, kc, :],
                            wproj_bf[:, kc, fh * 512:(fh + 1) * 512],
                            start=(kc == 0), stop=(kc == 7)), tt)
                pm_box[0] = mm
                return mm

            def b2_s2(qi, mm):
                nonlocal_ = {}
                par = qi % 2
                P.wait("vector", tt, mm)
                P.wait("vector", ss, zap)
                P.wait("vector", ss, s1_apply[par])
                for fh in range(2):
                    st_ = P.op("vector", lambda E, fh=fh: E.scalar_tensor_tensor(
                        ybuf[:, fh * 512:(fh + 1) * 512], ps_pj[fh][:], 0.0,
                        pb_bc[:, fh * 512:(fh + 1) * 512], OP.add, OP.add), vv)
                    P.wait("vector", vv, st_)
                    yt = P.op("vector", lambda E, fh=fh, par=par: E.tensor_tensor(
                        ybuf[:, fh * 512:(fh + 1) * 512], ybuf[:, fh * 512:(fh + 1) * 512],
                        xn1[:, par, fh * 512:(fh + 1) * 512], OP.add), vv)
                    ypj[fh] = yt
                ytt[par] = yt
                P.wait("vector", vv, yt)
                P.op("vector", lambda E: E.bn_stats(st2[:, 0:6], ybuf[:, 0:512]))
                sb = P.op("vector", lambda E: E.bn_stats(st2[:, 6:12], ybuf[:, 512:1024]), vv)
                P.wait("vector", vv, sb)
                ag = P.op("vector", lambda E: E.bn_aggr(mv2[:], st2[:]), vv)
                P.wait("scalar", vv, ag)
                lnx = P.op("scalar", lambda E: E.activation(rs2[:], mv2[:, 1:2], AF.Ln, bias=eps_sb[:], scale=1.0), ss)
                P.wait("scalar", ss, lnx)
                rr = P.op("scalar", lambda E: E.activation(rs2[:], rs2[:], AF.Exp, scale=-0.5), ss)
                P.wait("vector", ss, rr)
                nb = P.op("vector", lambda E: E.scalar_tensor_tensor(
                    nb2[:], mv2[:, 0:1], -1.0, rs2[:], OP.mult, OP.mult), vv)
                P.wait("scalar", vv, nb)
                P.wait("scalar", gg, ztt2)
                zap_ = P.op("scalar", lambda E: E.activation(
                    zbuf[:], ybuf[:], AF.Identity, bias=nb2[:], scale=rs2[:]), ss)
                P.wait("gpsimd", ss, zap_)
                zt1 = P.op("gpsimd", lambda E: E.tensor_tensor(zbuf[:], zbuf[:], g_bc[:], OP.mult), gg)
                P.wait("gpsimd", gg, zt1)
                zt2 = P.op("gpsimd", lambda E: E.tensor_tensor(zbuf[:], zbuf[:], b_bc[:], OP.add), gg)
                P.wait("vector", gg, zt2)
                P.op("vector", lambda E: E.bn_stats(st2[:, 0:6], zbuf[:, 0:512]))
                sb = P.op("vector", lambda E: E.bn_stats(st2[:, 6:12], zbuf[:, 512:1024]), vv)
                P.wait("vector", vv, sb)
                ag = P.op("vector", lambda E: E.bn_aggr(mv2[:], st2[:]), vv)
                P.wait("scalar", vv, ag)
                lnx = P.op("scalar", lambda E: E.activation(rs2[:], mv2[:, 1:2], AF.Ln, bias=eps_sb[:], scale=1.0), ss)
                P.wait("scalar", ss, lnx)
                rr = P.op("scalar", lambda E: E.activation(rs2[:], rs2[:], AF.Exp, scale=-0.5), ss)
                P.wait("vector", ss, rr)
                nb = P.op("vector", lambda E: E.scalar_tensor_tensor(
                    nb2[:], mv2[:, 0:1], -1.0, rs2[:], OP.mult, OP.mult), vv)
                P.wait("scalar", vv, nb)
                P.wait("scalar", sc_x1[par], x1_st[par])
                x1a = P.op("scalar", lambda E, par=par: E.activation(
                    x1buf[:, par, :], zbuf[:], AF.Identity, bias=nb2[:], scale=rs2[:]), ss)
                P.wait("vector", ss, x1a)
                xa = P.op("vector", lambda E, par=par: E.tensor_tensor(
                    x1buf[:, par, :], x1buf[:, par, :], x_rld[:, par, :], OP.add), vv)
                x_free[par] = xa
                P.wait("sync", vv, xa)
                x1_st[par] = P.op("sync", lambda E, qi=qi, par=par: E.dma_start(
                    out=x1_dram[qi * 128:(qi + 1) * 128, :], in_=x1buf[:, par, :]), sc_x1[par], 16)
                return zap_, x1a

            b2_dma(0)
            if NT > 1:
                b2_dma(1)
            b2_s1(0)
            for qi in range(NT):
                mm = b2_pe(qi)
                if qi + 1 < NT:
                    b2_s1(qi + 1)
                zap, x1ap = b2_s2(qi, mm)
                if qi + 2 < NT:
                    b2_dma(qi + 2)

            if b_dbg:
                P.wait("gpsimd", sc_x1[0], x1_st[0])
                P.wait("gpsimd", sc_x1[1], x1_st[1])
                P.wait("gpsimd", gc_dbg, gc_dbg.n)
                _dmp = P.op("gpsimd", lambda E: E.dma_start(out=dbg["dbg_x1"], in_=x1_dram), gc_dbg, 16)
                for _e in ("tensor", "vector", "scalar", "sync"):
                    P.wait(_e, gc_dbg, gc_dbg.n)

        stO.close()
        P.barrier(ALLSEMS)
        with ExitStack() as stC:
            eC = stC.enter_context
            wdq_sb = eC(nc.sbuf_tensor("wdq_sb", [128, 32, D], BF16))
            st3 = eC(nc.sbuf_tensor("st3", [128, 12], F32))
            mv3 = eC(nc.sbuf_tensor("mv3", [128, 2], F32))
            rs3 = eC(nc.sbuf_tensor("rs3", [128, 1], F32))
            sc1 = eC(nc.sbuf_tensor("sc1", [128, NT], F32))
            rsg = eC(nc.sbuf_tensor("rsg", [128, NT], F32))
            rsu = eC(nc.sbuf_tensor("rsu", [128, NT], F32))
            rsd = eC(nc.sbuf_tensor("rsd", [128, NT], F32))
            magic_c = eC(nc.sbuf_tensor("magic_c", [128, 1], F32))
            nmagic_c = eC(nc.sbuf_tensor("nmagic_c", [128, 1], F32))
            ps_t3 = [eC(nc.psum_tensor(f"ps_t3{i}", [128, 512], BF16)) for i in range(2)]
            stQ3 = ExitStack()
            stCG = ExitStack()
            xq3_fm = stQ3.enter_context(nc.sbuf_tensor("xq3_fm", [128, 8, S], BF16))
            wgf = stCG.enter_context(nc.sbuf_tensor("wgf", [128, 4, 512], F32))
            wuf = stCG.enter_context(nc.sbuf_tensor("wuf", [128, 4, 512], F32))
            wdf = stCG.enter_context(nc.sbuf_tensor("wdf", [128, 2, D], F32))
            wq_t = stCG.enter_context(nc.sbuf_tensor("wq_t", [128, 2048], F32))
            wq_c1 = stCG.enter_context(nc.sbuf_tensor("wq_c1", [128, 2048], BF16))
            wq_c2 = stCG.enter_context(nc.sbuf_tensor("wq_c2", [128, 2048], BF16))
            wgq_sb = stCG.enter_context(nc.sbuf_tensor("wgq_sb", [128, 2, 8, 512], BF16))
            wuq_sb = stCG.enter_context(nc.sbuf_tensor("wuq_sb", [128, 2, 8, 512], BF16))
            g_act = stCG.enter_context(nc.sbuf_tensor("g_act", [128, 2, 512], BF16))
            prod_sb = stCG.enter_context(nc.sbuf_tensor("prod_sb", [128, 2, 512], F32))
            ps_g = [stCG.enter_context(nc.psum_tensor(f"ps_g{i}", [128, 512], F32)) for i in range(2)]
            ps_u = [stCG.enter_context(nc.psum_tensor(f"ps_u{i}", [128, 512], F32)) for i in range(2)]

            P.op("vector", lambda E: E.memset(magic_c[:], 6755399441055744.0))
            mgd = P.op("vector", lambda E: E.memset(nmagic_c[:], -6755399441055744.0), vv)

            sc_ps = [sc_p2l, sc_p2s]
            wsems = {"g": gc_gu, "u": gc_wq, "d": gc_wd}
            wsrcs = {"g": wgT, "u": wuT}
            wbufs = {"g": wgf, "u": wuf}
            wqsb = {"g": wgq_sb, "u": wuq_sb}
            widx = {"g": 0, "u": 1, "d": 2}
            wbuf_free = {"g": 0, "u": 0, "d": 0}
            wdma = {("g", 0): 0, ("g", 1): 0, ("u", 0): 0, ("u", 1): 0,
                    ("d", 0): 0, ("d", 1): 0}
            qfree = [0]
            qcons = [0]
            quv = {"g": [0, 0], "u": [0, 0]}
            gu_used = [0, 0]

            def emit_wdma(ic):
                for mat in ("g", "u"):
                    P.wait("gpsimd", ss, wbuf_free[mat])
                    wdma[(mat, 0)] = P.op(
                        "gpsimd", lambda E, mat=mat, ic=ic: E.dma_start(
                            out=wbufs[mat][:],
                            in_=AP(wsrcs[mat].tensor, ic * 512,
                                   [[INNER, 128], [128 * INNER, 4], [1, 512]])),
                        wsems[mat][0], 16)
                P.wait("gpsimd", ss, wbuf_free["d"])
                wdma[("d", 0)] = P.op("gpsimd", lambda E, ic=ic: E.dma_start(
                    out=wdf[:],
                    in_=AP(wdT.tensor, ic * 512 * D, [[D, 128], [128 * D, 2], [1, D]])),
                    gc_wd[0], 16)

            def emit_qunit(mat, ic, hf):
                icpar = ic % 2
                src = wdf[:] if mat == "d" else wbufs[mat][:]
                dsem, dval = wsems[mat][hf], wdma[(mat, hf)]
                P.wait("scalar", dsem, dval)
                P.wait("scalar", vv, qfree[0])
                P.wait("scalar", vv, swb)
                tw = P.op("scalar", lambda E, src=src, mat=mat: E.activation(
                    wq_t[:], src, AF.Identity, scale=swg_b[:, widx[mat]:widx[mat] + 1]), ss)
                if hf == 0:
                    P.wait("gpsimd", ss, tw)
                    if mat == "d":
                        wdma[("d", 1)] = P.op("gpsimd", lambda E, ic=ic: E.dma_start(
                            out=wdf[:],
                            in_=AP(wdT.tensor, (ic * 512 + 256) * D,
                                   [[D, 128], [128 * D, 2], [1, D]])),
                            gc_wd[1], 16)
                    else:
                        wdma[(mat, 1)] = P.op(
                            "gpsimd", lambda E, mat=mat, ic=ic: E.dma_start(
                                out=wbufs[mat][:],
                                in_=AP(wsrcs[mat].tensor, 512 * INNER + ic * 512,
                                       [[INNER, 128], [128 * INNER, 4], [1, 512]])),
                            wsems[mat][1], 16)
                else:
                    wbuf_free[mat] = tw
                P.wait("vector", ss, tw)
                P.wait("vector", vv, qcons[0])
                P.op("vector", lambda E: E.tensor_scalar(wq_c1[:], wq_t[:], 0.5, None, OP.is_gt))
                cmp2 = P.op("vector", lambda E: E.tensor_scalar(wq_c2[:], wq_t[:], -0.5, None, OP.is_lt), vv)
                qfree[0] = cmp2
                P.wait("vector", vv, cmp2)
                if mat == "d":
                    dst = wdq_sb[:, 4 * ic + 2 * hf:4 * ic + 2 * hf + 2, :]
                else:
                    if hf == 0:
                        P.wait("vector", tt, gu_used[icpar])
                    dst = wqsb[mat][:, icpar, 4 * hf:4 * hf + 4, :]
                qq_ = P.op("vector", lambda E, dst=dst: E.tensor_tensor(
                    dst.rearrange("p a b -> p (a b)"), wq_c1[:], wq_c2[:], OP.subtract), vv)
                qcons[0] = qq_
                if mat != "d":
                    quv[mat][icpar] = qq_

            UNITS = [("g", 0), ("g", 1), ("u", 0), ("u", 1), ("d", 0), ("d", 1)]
            unit_ts = [1 + i * max(1, (NT - 2) // 6) for i in range(6)]

            x1t_used = [0, 0]
            xq3_done = [0] * NT
            i32_free = 0
            tr3_ev = [0, 0]
            emit_wdma(0)
            with ExitStack() as stC0:
                e0 = stC0.enter_context
                x1t = e0(nc.sbuf_tensor("x1t", [128, 2, D], F32))
                xn3 = e0(nc.sbuf_tensor("xn3", [128, D], F32))
                xq_i32a = e0(nc.sbuf_tensor("xq_i32a", [128, D], I32))
                d_bfa = e0(nc.sbuf_tensor("d_bfa", [128, D], F32))
                xq3_tm = e0(nc.sbuf_tensor("xq3_tm", [128, D], BF16))
                for t in range(NT):
                    par = t % 2
                    P.wait("sync", vv, x1t_used[par])
                    d_x1 = P.op("sync", lambda E, t=t, par=par: E.dma_start(
                        out=x1t[:, par, :], in_=x1_dram[t * 128:(t + 1) * 128, :]), sc_xr[par], 16)
                    P.wait("vector", sc_xr[par], d_x1)
                    if t == 0:
                        P.wait("vector", vv, swb)
                        P.wait("vector", vv, mgd)
                    P.op("vector", lambda E, par=par: E.bn_stats(st3[:, 0:6], x1t[:, par, 0:512]))
                    sb = P.op("vector", lambda E, par=par: E.bn_stats(st3[:, 6:12], x1t[:, par, 512:1024]), vv)
                    P.wait("vector", vv, sb)
                    ag = P.op("vector", lambda E: E.bn_aggr(mv3[:], st3[:]), vv)
                    P.wait("scalar", vv, ag)
                    lnx = P.op("scalar", lambda E: E.activation(rs3[:], mv3[:, 1:2], AF.Ln, bias=eps_sb[:], scale=1.0), ss)
                    P.wait("scalar", ss, lnx)
                    rr = P.op("scalar", lambda E: E.activation(rs3[:], rs3[:], AF.Exp, scale=-0.5), ss)
                    P.wait("vector", ss, rr)
                    xnd = P.op("vector", lambda E, par=par: E.tensor_scalar(
                        xn3[:], x1t[:, par, :], mv3[:, 0:1], rs3[:], OP.subtract, OP.mult), vv)
                    x1t_used[par] = xnd
                    P.wait("vector", vv, xnd)
                    am = P.op("vector", lambda E, t=t: E.reduce_max(
                        sc1[:, t:t + 1], xn3[:], axis=AX.X, apply_absolute_value=True), vv)
                    P.wait("vector", vv, am)
                    mc1 = P.op("vector", lambda E, t=t: E.tensor_scalar(
                        sc1[:, t:t + 1], sc1[:, t:t + 1], 1e-5, None, OP.max), vv)
                    P.wait("vector", vv, mc1)
                    P.op("vector", lambda E, t=t: E.scalar_tensor_tensor(
                        rsg[:, t:t + 1], sc1[:, t:t + 1], 1.0 / 127.0, mwg_b[:, 0:1], OP.mult, OP.mult))
                    ru = P.op("vector", lambda E, t=t: E.scalar_tensor_tensor(
                        rsu[:, t:t + 1], sc1[:, t:t + 1], 1.0 / 127.0, mwg_b[:, 1:2], OP.mult, OP.mult), vv)
                    P.wait("vector", vv, ru)
                    rc = P.op("vector", lambda E, t=t: E.reciprocal(sc1[:, t:t + 1], sc1[:, t:t + 1]), vv)
                    P.wait("vector", vv, rc)
                    s127 = P.op("vector", lambda E, t=t: E.tensor_scalar(
                        sc1[:, t:t + 1], sc1[:, t:t + 1], 127.0, None, OP.mult), vv)
                    P.wait("scalar", vv, s127)
                    P.wait("scalar", vv, i32_free)
                    qi32 = P.op("scalar", lambda E, t=t: E.activation(
                        xq_i32a[:], xn3[:], AF.Identity, scale=sc1[:, t:t + 1]), ss)
                    P.wait("vector", ss, qi32)
                    dd1 = P.op("vector", lambda E, t=t: E.scalar_tensor_tensor(
                        d_bfa[:], xn3[:], sc1[:, t:t + 1], xq_i32a[:], OP.mult, OP.subtract), vv)
                    P.wait("scalar", vv, dd1)
                    da = P.op("scalar", lambda E: E.activation(
                        d_bfa[:], d_bfa[:], AF.Identity, bias=magic_c[:], scale=1.0), ss)
                    P.wait("scalar", ss, da)
                    dd2b = P.op("scalar", lambda E: E.activation(
                        d_bfa[:], d_bfa[:], AF.Identity, bias=nmagic_c[:], scale=1.0), ss)
                    P.wait("vector", ss, dd2b)
                    i32_free = P.op("vector", lambda E: E.scalar_tensor_tensor(
                        xq3_tm[:], d_bfa[:], 1.0, xq_i32a[:], OP.mult, OP.add), vv)
                    P.wait("tensor", vv, i32_free)
                    for cg in range(2):
                        tpp = (2 * t + cg) % 2
                        P.wait("tensor", vv, tr3_ev[tpp])
                        for c4 in range(4):
                            c = cg * 4 + c4
                            tp = P.op("tensor", lambda E, c=c, c4=c4, tpp=tpp: E.transpose(
                                ps_t3[tpp][:, c4 * 128:(c4 + 1) * 128],
                                xq3_tm[:, c * 128:(c + 1) * 128], id_bf[:]), tt)
                        P.wait("vector", tt, tp)
                        tr3_ev[tpp] = P.op("vector", lambda E, t=t, cg=cg, tpp=tpp: E.tensor_copy(
                            AP(xq3_fm[:].tensor, cg * 4 * S + t * 128,
                               [[8 * S, 128], [S, 4], [1, 128]]),
                            ps_t3[tpp][:].rearrange("p (a b) -> p a b", a=4)), vv)
                    xq3_done[t] = tr3_ev[(2 * t + 1) % 2]
                    for ui, (mat, hf) in enumerate(UNITS):
                        if unit_ts[ui] == t + 1 or (t == NT - 1 and unit_ts[ui] > NT - 1):
                            emit_qunit(mat, 0, hf)

            silu_done = [0, 0]
            prod_done = [0, 0]
            prod_st = [0, 0]
            for ic in range(8):
                icpar = ic % 2
                if ic + 1 < 8:
                    emit_wdma(ic + 1)
                P.wait("tensor", vv, quv["g"][icpar])
                P.wait("tensor", vv, quv["u"][icpar])
                for t in range(NT):
                    tpar = t % 2
                    if ic == 0:
                        P.wait("tensor", vv, xq3_done[t])
                    P.wait("tensor", ss, silu_done[tpar])
                    P.wait("tensor", vv, prod_done[tpar])
                    for kc in range(8):
                        P.op("tensor", lambda E, kc=kc, t=t, icpar=icpar, tpar=tpar: E.matmul(
                            ps_g[tpar][:], xq3_fm[:, kc, t * 128:(t + 1) * 128],
                            wgq_sb[:, icpar, kc, :], start=(kc == 0), stop=(kc == 7)))
                    for kc in range(8):
                        mmu = P.op("tensor", lambda E, kc=kc, t=t, icpar=icpar, tpar=tpar: E.matmul(
                            ps_u[tpar][:], xq3_fm[:, kc, t * 128:(t + 1) * 128],
                            wuq_sb[:, icpar, kc, :], start=(kc == 0), stop=(kc == 7)), tt)
                    gu_used[icpar] = mmu
                    P.wait("scalar", tt, mmu)
                    P.wait("scalar", vv, prod_done[tpar])
                    silu_done[tpar] = P.op("scalar", lambda E, t=t, tpar=tpar: E.activation(
                        g_act[:, tpar, :], ps_g[tpar][:], AF.Silu, scale=rsg[:, t:t + 1]), ss)
                    P.wait("vector", ss, silu_done[tpar])
                    P.wait("vector", sc_ps[tpar], prod_st[tpar])
                    prod_done[tpar] = P.op("vector", lambda E, tpar=tpar: E.tensor_tensor(
                        prod_sb[:, tpar, :], g_act[:, tpar, :], ps_u[tpar][:], OP.mult), vv)
                    P.wait("sync", vv, prod_done[tpar])
                    prod_st[tpar] = P.op("sync", lambda E, ic=ic, t=t, tpar=tpar: E.dma_start(
                        out=AP(gu_dram.tensor, t * 128 * INNER + ic * 512, [[INNER, 128], [1, 512]]),
                        in_=prod_sb[:, tpar, :]), sc_ps[tpar], 16)
                    if ic + 1 < 8:
                        for ui, (mat, hf) in enumerate(UNITS):
                            if unit_ts[ui] == t + 1 or (t == NT - 1 and unit_ts[ui] > NT - 1):
                                emit_qunit(mat, ic + 1, hf)

            stCG.close()
            stQ3.close()
            P.barrier(ALLSEMS)
            with ExitStack() as stCD:
                eD = stCD.enter_context
                gu_t = eD(nc.sbuf_tensor("gu_t", [128, 2, INNER], F32))
                x1t2 = eD(nc.sbuf_tensor("x1t2", [128, 2, D], F32))
                xqgu_tm = eD(nc.sbuf_tensor("xqgu_tm", [128, 2, INNER], BF16))
                xqgu_fm = eD(nc.sbuf_tensor("xqgu_fm", [128, 2, 32, 128], BF16))
                xq_i32b = eD(nc.sbuf_tensor("xq_i32b", [128, INNER], I32))
                d_bfb = eD(nc.sbuf_tensor("d_bfb", [128, INNER], F32))
                mpt = eD(nc.sbuf_tensor("mpt", [128, NT, 3], F32))
                ffo = eD(nc.sbuf_tensor("ffo", [128, 2, D], F32))
                outb = eD(nc.sbuf_tensor("outb", [128, 2, D], F32))
                ps_d = [eD(nc.psum_tensor(f"ps_d{i}", [128, D], F32)) for i in range(2)]

                gu_free = [0, 0]
                gu_free_s = [0, 0]
                gu_free_g = [0, 0]
                x1_free = [0, 0]
                ffo_used = [0, 0]
                ffo_ev = [0, 0]
                out_st = [0, 0]
                d_gu = [0, 0]
                d_x1 = [0, 0]
                dd1_d = [0, 0]
                dd2_d = [0, 0]
                comb_d = [0, 0]
                qi32_d = [0, 0]
                copy_d = {}
                tr_d = {}
                mm_d = [0, 0]

                def cd_dma_gu(t):
                    par = t % 2
                    P.wait("sync", vv, gu_free[par])
                    P.wait("sync", ss, gu_free_s[par])
                    d_gu[par] = P.op("sync", lambda E, t=t, par=par: E.dma_start(
                        out=gu_t[:, par, :],
                        in_=AP(gu_dram.tensor, t * 128 * INNER, [[INNER, 128], [1, INNER]])), sc_x1[par], 16)

                def cd_dma_x1(t):
                    par = t % 2
                    P.wait("sync", vv, x1_free[par])
                    d_x1[par] = P.op("sync", lambda E, t=t, par=par: E.dma_start(
                        out=x1t2[:, par, :], in_=x1_dram[t * 128:(t + 1) * 128, :]), sc_xr[par], 16)

                def cd_quant(t):
                    par = t % 2
                    P.wait("vector", sc_x1[par], d_gu[par])
                    P.wait("vector", vv, mgd)
                    am = P.op("vector", lambda E, t=t, par=par: E.reduce_max(
                        mpt[:, t, 0:1], gu_t[:, par, :], axis=AX.X, apply_absolute_value=True), vv)
                    gu_free_g[par] = 0
                    P.wait("vector", vv, am)
                    mc2 = P.op("vector", lambda E, t=t: E.tensor_scalar(
                        mpt[:, t, 1:2], mpt[:, t, 0:1], rsu[:, t:t + 1], 1e-5, OP.mult, OP.max), vv)
                    P.wait("vector", vv, mc2)
                    P.op("vector", lambda E, t=t: E.scalar_tensor_tensor(
                        rsd[:, t:t + 1], mpt[:, t, 1:2], 1.0 / 127.0, mwg_b[:, 2:3], OP.mult, OP.mult))
                    rc = P.op("vector", lambda E, t=t: E.reciprocal(mpt[:, t, 2:3], mpt[:, t, 1:2]), vv)
                    P.wait("vector", vv, rc)
                    v2d = P.op("vector", lambda E, t=t: E.scalar_tensor_tensor(
                        mpt[:, t, 2:3], mpt[:, t, 2:3], 127.0, rsu[:, t:t + 1], OP.mult, OP.mult), vv)
                    P.wait("scalar", vv, v2d)
                    P.wait("scalar", vv, max(comb_d))
                    qi32_d[par] = P.op("scalar", lambda E, t=t, par=par: E.activation(
                        xq_i32b[:], gu_t[:, par, :], AF.Identity, scale=mpt[:, t, 2:3]), ss)
                    gu_free_s[par] = qi32_d[par]
                    P.wait("vector", ss, qi32_d[par])
                    P.wait("vector", vv, max(comb_d))
                    dd1_d[par] = P.op("vector", lambda E, t=t, par=par: E.scalar_tensor_tensor(
                        d_bfb[:], gu_t[:, par, :], mpt[:, t, 2:3], xq_i32b[:],
                        OP.mult, OP.subtract), vv)
                    gu_free[par] = dd1_d[par]
                    P.wait("scalar", vv, dd1_d[par])
                    da = P.op("scalar", lambda E: E.activation(
                        d_bfb[:], d_bfb[:], AF.Identity, bias=magic_c[:], scale=1.0), ss)
                    P.wait("scalar", ss, da)
                    dd2_d[par] = P.op("scalar", lambda E: E.activation(
                        d_bfb[:], d_bfb[:], AF.Identity, bias=nmagic_c[:], scale=1.0), ss)
                    P.wait("vector", ss, dd2_d[par])
                    P.wait("vector", tt, tr_d.get((t - 2, 7), 0))
                    comb_d[par] = P.op("vector", lambda E, par=par: E.scalar_tensor_tensor(
                        xqgu_tm[:, par, :], d_bfb[:], 1.0, xq_i32b[:],
                        OP.mult, OP.add), vv)

                def cd_transp(t):
                    par = t % 2
                    P.wait("tensor", vv, comb_d[par])
                    P.wait("scalar", tt, mm_d[par])
                    for cg in range(8):
                        tpp = cg % 2
                        prev = copy_d[(t, cg - 2)] if cg >= 2 else copy_d.get((t - 1, 6 + tpp), 0)
                        P.wait("tensor", ss, prev)
                        for c4 in range(4):
                            c = cg * 4 + c4
                            tp = P.op("tensor", lambda E, c=c, c4=c4, tpp=tpp, par=par: E.transpose(
                                ps_t3[tpp][:, c4 * 128:(c4 + 1) * 128],
                                xqgu_tm[:, par, c * 128:(c + 1) * 128], id_bf[:]), tt)
                        tr_d[(t, cg)] = tp
                        P.wait("scalar", tt, tp)
                        copy_d[(t, cg)] = P.op("scalar", lambda E, cg=cg, tpp=tpp, par=par: E.activation(
                            xqgu_fm[:, par, cg * 4:(cg + 1) * 4, :].rearrange("p a b -> p (a b)"),
                            ps_t3[tpp][:], AF.Identity, scale=1.0), ss)

                def cd_mm(t):
                    par = t % 2
                    P.wait("tensor", ss, copy_d[(t, 7)])
                    P.wait("tensor", ss, ffo_ev[par])
                    for fh in range(2):
                        for kc in range(32):
                            mmd = P.op("tensor", lambda E, kc=kc, fh=fh, par=par: E.matmul(
                                ps_d[par][:, fh * 512:(fh + 1) * 512], xqgu_fm[:, par, kc, :],
                                wdq_sb[:, kc, fh * 512:(fh + 1) * 512],
                                start=(kc == 0), stop=(kc == 31)), tt)
                        P.wait("scalar", tt, mmd)
                        if fh == 0:
                            P.wait("scalar", vv, ffo_used[par])
                        ffo_ev[par] = P.op("scalar", lambda E, t=t, fh=fh, par=par: E.activation(
                            ffo[:, par, fh * 512:(fh + 1) * 512],
                            ps_d[par][:, fh * 512:(fh + 1) * 512],
                            AF.Identity, scale=rsd[:, t:t + 1]), ss)
                    mm_d[par] = mmd

                def cd_ln4(t):
                    par = t % 2
                    P.wait("vector", ss, ffo_ev[par])
                    P.op("vector", lambda E, par=par: E.bn_stats(st3[:, 0:6], ffo[:, par, 0:512]))
                    sb4 = P.op("vector", lambda E, par=par: E.bn_stats(st3[:, 6:12], ffo[:, par, 512:1024]), vv)
                    P.wait("vector", vv, sb4)
                    ag4 = P.op("vector", lambda E: E.bn_aggr(mv3[:], st3[:]), vv)
                    P.wait("scalar", vv, ag4)
                    lnx4 = P.op("scalar", lambda E: E.activation(rs3[:], mv3[:, 1:2], AF.Ln, bias=eps_sb[:], scale=1.0), ss)
                    P.wait("scalar", ss, lnx4)
                    rr4 = P.op("scalar", lambda E: E.activation(rs3[:], rs3[:], AF.Exp, scale=-0.5), ss)
                    P.wait("vector", ss, rr4)
                    P.wait("vector", sc_out[par], out_st[par])
                    ow = P.op("vector", lambda E, par=par: E.tensor_scalar(
                        outb[:, par, :], ffo[:, par, :], mv3[:, 0:1], rs3[:], OP.subtract, OP.mult), vv)
                    ffo_used[par] = ow
                    P.wait("vector", vv, ow)
                    P.wait("vector", sc_xr[par], d_x1[par])
                    ox = P.op("vector", lambda E, par=par: E.tensor_tensor(
                        outb[:, par, :], outb[:, par, :], x1t2[:, par, :], OP.add), vv)
                    x1_free[par] = ox
                    P.wait("sync", vv, ox)
                    out_st[par] = P.op("sync", lambda E, t=t, par=par: E.dma_start(
                        out=out[t * 128:(t + 1) * 128, :], in_=outb[:, par, :]), sc_out[par], 16)

                cd_dma_gu(0)
                if NT > 1:
                    cd_dma_gu(1)
                cd_dma_x1(0)
                cd_quant(0)
                cd_transp(0)
                for i in range(NT):
                    if i + 2 < NT:
                        cd_dma_gu(i + 2)
                    if i + 1 < NT:
                        cd_dma_x1(i + 1)
                    if i + 1 < NT:
                        cd_quant(i + 1)
                    cd_mm(i)
                    if i + 1 < NT:
                        cd_transp(i + 1)
                    cd_ln4(i)

                P.wait("sync", sc_out[0], out_st[0])
                P.wait("sync", sc_out[1], out_st[1])
                P.wait("gpsimd", sc_out[0], out_st[0])
                P.wait("gpsimd", sc_out[1], out_st[1])

        with nc.Block() as block:
            @block.tensor
            def _(E):
                for f in P.ev["tensor"]:
                    f(E)

            @block.vector
            def _(E):
                for f in P.ev["vector"]:
                    f(E)

            @block.scalar
            def _(E):
                for f in P.ev["scalar"]:
                    f(E)

            @block.gpsimd
            def _(E):
                for f in P.ev["gpsimd"]:
                    f(E)

            @block.sync
            def _(E):
                for f in P.ev["sync"]:
                    f(E)

    return nc, dbg


_CACHE = {}


def _host_inputs(L0_attn_W_w, L0_attn_W_b, L0_attn_proj_w, L0_attn_proj_b,
                 L0_attn_ln_g, L0_attn_ln_b, L0_attn_rpb,
                 L0_ffn_w_gate, L0_ffn_w_up, L0_ffn_w_down):
    f = np.asarray
    Ww = f(L0_attn_W_w, dtype=np.float32)
    Wb = f(L0_attn_W_b, dtype=np.float32)
    rpb = f(L0_attn_rpb, dtype=np.float32)
    shared = {
        "wqkvT": np.ascontiguousarray(Ww.T),
        "wb_pc": np.ascontiguousarray(Wb.reshape(24, 128).T),
        "wbv_row": np.ascontiguousarray(Wb[2048:3072].reshape(1, 1024)),
        "wprojT": np.ascontiguousarray(f(L0_attn_proj_w, dtype=np.float32).T),
        "projb_row": np.ascontiguousarray(f(L0_attn_proj_b, dtype=np.float32).reshape(1, 1024)),
        "lng_row": np.ascontiguousarray(f(L0_attn_ln_g, dtype=np.float32).reshape(1, 1024)),
        "lnb_row": np.ascontiguousarray(f(L0_attn_ln_b, dtype=np.float32).reshape(1, 1024)),
        "rpbmt": np.ascontiguousarray(rpb[::-1, :].T).reshape(-1),
        "wgT": np.ascontiguousarray(f(L0_ffn_w_gate, dtype=np.float32).T),
        "wuT": np.ascontiguousarray(f(L0_ffn_w_up, dtype=np.float32).T),
        "wdT": np.ascontiguousarray(f(L0_ffn_w_down, dtype=np.float32).T),
        "id_in": np.eye(128, dtype=np.float32),
        "j_in": np.eye(128, dtype=np.float32)[::-1].copy(),
    }
    m = np.arange(128)[:, None] + np.arange(128)[None, :]
    shared["mask_in"] = np.where(m <= 126, np.float32(NEG), np.float32(0.0))
    return shared


def kernel(L0_attn_x, L0_attn_W_w, L0_attn_W_b, L0_attn_proj_w, L0_attn_proj_b,
           L0_attn_ln_g, L0_attn_ln_b, L0_attn_rpb, L0_ffn_w_gate, L0_ffn_w_up,
           L0_ffn_w_down):
    x = np.asarray(L0_attn_x, dtype=np.float32)
    B = x.shape[0]
    if "nc" not in _CACHE:
        _CACHE["nc"] = build(S=2048, b_dbg=False)[0]
    nc = _CACHE["nc"]
    shared = _host_inputs(L0_attn_W_w, L0_attn_W_b, L0_attn_proj_w, L0_attn_proj_b,
                          L0_attn_ln_g, L0_attn_ln_b, L0_attn_rpb,
                          L0_ffn_w_gate, L0_ffn_w_up, L0_ffn_w_down)
    in_maps = []
    for b in range(B):
        m = dict(shared)
        m["x"] = np.ascontiguousarray(x[b])
        in_maps.append(m)
    res = run_bass_kernel_spmd(nc, in_maps, core_ids=list(range(B)))
    return np.stack([res.results[b]["out"] for b in range(B)], axis=0)


# version 20
# speedup vs baseline: 1.0123x; 1.0123x over previous
import numpy as np
from contextlib import ExitStack

import concourse.bass as bass
import concourse.mybir as mybir
from concourse.ap import AP
from concourse.bass_utils import run_bass_kernel_spmd
from concourse.alu_op_type import AluOpType
from concourse import library_config

F32 = mybir.dt.float32
BF16 = mybir.dt.bfloat16
I32 = mybir.dt.int32
AF = mybir.ActivationFunctionType
AX = mybir.AxisListType
OP = AluOpType

D = 1024
H = 16
INNER = 4096
NEG = -1.0e30
EXP_OFF = 40.0


class Sem:
    __slots__ = ("h", "n")
    def __init__(self, h):
        self.h = h
        self.n = 0


class RotSem:
    CAP = 1600

    def __init__(self, handles):
        self.hs = handles
        self.n = 0
        self.seg = 0
        self.base = 0

    def loc(self, v):
        if v == 0:
            return self.hs[0], 0
        s = (v - 1) // self.CAP
        return self.hs[s % len(self.hs)], v - s * self.CAP


class Prog:
    ENGS = ("tensor", "vector", "scalar", "gpsimd", "sync")

    def __init__(self, nc):
        self.nc = nc
        self.ev = {k: [] for k in self.ENGS}

    def wait(self, eng, sem, val):
        if not val:
            return
        if isinstance(sem, RotSem):
            h, v = sem.loc(int(val))
            self.ev[eng].append(lambda e, s=h, vv_=v: e.wait_ge(s, vv_))
        else:
            self.ev[eng].append(lambda e, s=sem.h, v=int(val): e.wait_ge(s, v))

    def op(self, eng, make, inc=None, k=1):
        if inc is not None:
            if isinstance(inc, RotSem):
                inc.n += k
                h, _ = inc.loc(inc.n)
                self.ev[eng].append(lambda e, m=make, s=h, kk=k: m(e).then_inc(s, kk))
                return inc.n
            inc.n += k
            self.ev[eng].append(lambda e, m=make, s=inc.h, kk=k: m(e).then_inc(s, kk))
            return inc.n
        self.ev[eng].append(make)
        return None

    def barrier(self, sems):
        snap = [(s, s.n) for s in sems if s.n > 0]
        for eng in self.ENGS:
            for s, n in snap:
                self.wait(eng, s, n)


def build(S=2048, b_dbg=False, pad_instr=0):
    NT = S // 128
    NP = NT // 2
    QT = max(1, NT // 8)
    R = 2 * S - 1
    WEXT = 384 + S
    nc = bass.Bass("TRN2", target_bir_lowering=False)

    x_in = nc.dram_tensor("x", [S, D], F32, kind="ExternalInput").ap()
    wqkvT = nc.dram_tensor("wqkvT", [D, 3 * D], F32, kind="ExternalInput").ap()
    wb_pc = nc.dram_tensor("wb_pc", [128, 24], F32, kind="ExternalInput").ap()
    wbv_row = nc.dram_tensor("wbv_row", [1, D], F32, kind="ExternalInput").ap()
    wprojT = nc.dram_tensor("wprojT", [D, D], F32, kind="ExternalInput").ap()
    projb_row = nc.dram_tensor("projb_row", [1, D], F32, kind="ExternalInput").ap()
    lng_row = nc.dram_tensor("lng_row", [1, D], F32, kind="ExternalInput").ap()
    lnb_row = nc.dram_tensor("lnb_row", [1, D], F32, kind="ExternalInput").ap()
    rpbmt = nc.dram_tensor("rpbmt", [H * R], F32, kind="ExternalInput").ap()
    wgT = nc.dram_tensor("wgT", [D, INNER], F32, kind="ExternalInput").ap()
    wuT = nc.dram_tensor("wuT", [D, INNER], F32, kind="ExternalInput").ap()
    wdT = nc.dram_tensor("wdT", [INNER, D], F32, kind="ExternalInput").ap()
    id_in = nc.dram_tensor("id_in", [128, 128], F32, kind="ExternalInput").ap()
    j_in = nc.dram_tensor("j_in", [128, 128], F32, kind="ExternalInput").ap()
    mask_in = nc.dram_tensor("mask_in", [128, 128], F32, kind="ExternalInput").ap()

    out = nc.dram_tensor("out", [S, D], F32, kind="ExternalOutput").ap()

    x1_dram = nc.dram_tensor("x1_scratch", [S, D], F32, kind="Internal").ap()
    gu_dram = nc.dram_tensor("gu_scratch", [S, INNER], F32, kind="Internal").ap()

    dbg = {}
    if b_dbg:
        for nm, shp, dt in (("dbg_q", [128, 8 * S], BF16), ("dbg_k", [128, 8 * S], BF16),
                            ("dbg_v", [128, NT * H * 65], BF16), ("dbg_o", [128, NT * D], BF16),
                            ("dbg_x1", [S, D], F32),
                            ("dbg_xn", [128, 8 * S], BF16),
                            ("dbg_wm", [1, 8], F32)):
            dbg[nm] = nc.dram_tensor(nm, shp, dt, kind="ExternalOutput").ap()

    st = ExitStack()
    with st:
        e = st.enter_context

        tt = RotSem([e(nc.semaphore(f"sem_t{i}")) for i in range(8)])
        vv = Sem(e(nc.semaphore("sem_v")))
        ss = Sem(e(nc.semaphore("sem_s")))
        gg = Sem(e(nc.semaphore("sem_g")))
        def _sem(nm):
            return Sem(e(nc.semaphore(nm)))
        sc_const = _sem("sc_const")
        sc_x = _sem("sc_x")
        sc_p1 = _sem("sc_p1")
        sc_p2l = _sem("sc_p2l")
        sc_p2s = _sem("sc_p2s")
        sc_xr = [_sem("sc_xr0"), _sem("sc_xr1")]
        sc_x1 = [_sem("sc_x10"), _sem("sc_x11")]
        sc_x1q = _sem("sc_x1q")
        sc_out = [_sem("sc_out0"), _sem("sc_out1")]
        gc_const = _sem("gc_const")
        gc_wq = [_sem("gc_wq0"), _sem("gc_wq1")]
        gc_wv = _sem("gc_wv")
        gc_we = [_sem("gc_we0"), _sem("gc_we1")]
        gc_wp = _sem("gc_wp")
        gc_gu = [_sem("gc_gu0"), _sem("gc_gu1")]
        gc_wd = [_sem("gc_wd0"), _sem("gc_wd1")]
        gc_dbg = _sem("gc_dbg")
        ALLSEMS = [tt, vv, ss, gg, sc_const, sc_x, sc_p1, sc_p2l, sc_p2s,
                   sc_xr[0], sc_xr[1], sc_x1[0], sc_x1[1], sc_x1q,
                   sc_out[0], sc_out[1], gc_const, gc_wq[0], gc_wq[1], gc_wv,
                   gc_we[0], gc_we[1], gc_wp, gc_gu[0], gc_gu[1],
                   gc_wd[0], gc_wd[1], gc_dbg]

        P = Prog(nc)

        ones_bf = e(nc.sbuf_tensor("ones_bf", [128, 128], BF16))
        ones_f32 = e(nc.sbuf_tensor("ones_f32", [1, 128], F32))
        onec_f32 = e(nc.sbuf_tensor("onec_f32", [128, 1], F32))
        id_bf = e(nc.sbuf_tensor("id_bf", [128, 128], BF16))
        j_bf = e(nc.sbuf_tensor("j_bf", [128, 128], BF16))
        mask_sb = e(nc.sbuf_tensor("mask_sb", [128, 128], F32))
        eps_sb = e(nc.sbuf_tensor("eps_sb", [128, 1], F32))
        wb_sb = e(nc.sbuf_tensor("wb_sb", [128, 24], F32))
        wbq_sb = e(nc.sbuf_tensor("wbq_sb", [128, 8], F32))
        wbv_bf = e(nc.sbuf_tensor("wbv_bf", [1, D], BF16))
        negmh = e(nc.sbuf_tensor("negmh", [128, 1], F32))
        wstat = e(nc.sbuf_tensor("wstat", [1, 8], F32))
        swg_b = e(nc.sbuf_tensor("swg_b", [128, 3], F32))
        mwg_b = e(nc.sbuf_tensor("mwg_b", [128, 3], F32))
        stO = ExitStack()
        stQKV = ExitStack()
        o_sb = stO.enter_context(nc.sbuf_tensor("o_sb", [128, NT, D], BF16))
        q_sb = stQKV.enter_context(nc.sbuf_tensor("q_sb", [128, 8, S], BF16))
        k_sb = stQKV.enter_context(nc.sbuf_tensor("k_sb", [128, 8, S], BF16))
        v_sb = stQKV.enter_context(nc.sbuf_tensor("v_sb", [128, NT, H * 65], BF16))

        P.op("sync", lambda E: E.dma_start(out=wb_sb[:], in_=wb_pc), sc_const, 16)
        a0_ds = P.op("sync", lambda E: E.dma_start(out=mask_sb[:], in_=mask_in), sc_const, 16)
        P.op("gpsimd", lambda E: E.dma_start(out=id_bf[:], in_=id_in), gc_const, 16)
        P.op("gpsimd", lambda E: E.dma_start(out=j_bf[:], in_=j_in), gc_const, 16)
        a0_dg = P.op("gpsimd", lambda E: E.dma_start(out=wbv_bf[:], in_=wbv_row), gc_const, 16)
        for _pi in range(pad_instr):
            P.op("vector", lambda E: E.memset(eps_sb[:], 1e-5))
        if pad_instr:
            with nc.psum_tensor("ps_pad", [128, 128], F32) as ps_pad:
                P.wait("tensor", gc_const, a0_dg)
                for _pi in range(pad_instr):
                    P.op("tensor", lambda E: E.matmul(ps_pad[:], id_bf[:], id_bf[:],
                                                      start=True, stop=True), tt)
        P.op("vector", lambda E: E.memset(eps_sb[:], 1e-5))
        P.op("vector", lambda E: E.memset(wstat[:], 0.0))
        P.op("vector", lambda E: E.memset(negmh[:], 0.0))
        P.op("vector", lambda E: E.memset(ones_bf[:], 1.0))
        P.op("vector", lambda E: E.memset(ones_f32[:], 1.0))
        P.op("vector", lambda E: E.memset(onec_f32[:], 1.0))
        P.op("vector", lambda E: E.memset(
            AP(v_sb[:].tensor, 64, [[NT * H * 65, 128], [H * 65, NT], [65, H]]), 1.0), vv)
        P.wait("vector", sc_const, a0_ds)
        v_wbq = P.op("vector", lambda E: E.tensor_scalar(
            wbq_sb[:], wb_sb[:, 0:8], 1.0 / 32.0, None, OP.mult), vv)

        with ExitStack() as stA:
            eA = stA.enter_context
            TBA = 2
            x_sb = eA(nc.sbuf_tensor("x_sb", [128, TBA, D], F32))
            stats = eA(nc.sbuf_tensor("stats", [128, TBA, 12], F32))
            mv = eA(nc.sbuf_tensor("mv", [128, TBA, 2], F32))
            lnv = eA(nc.sbuf_tensor("lnv", [128, TBA], F32))
            rstd = eA(nc.sbuf_tensor("rstd", [128, TBA], F32))
            xn_bf = eA(nc.sbuf_tensor("xn_bf", [128, TBA, D], BF16))
            xn_fm = eA(nc.sbuf_tensor("xn_fm", [128, 8, S], BF16))
            wqkv_o = eA(nc.sbuf_tensor("wqkv_o", [128, 2, 8, 128], BF16))
            wv_bf = eA(nc.sbuf_tensor("wv_bf", [128, 8, D], BF16))
            ps_tr = [eA(nc.psum_tensor(f"ps_tr{i}", [128, 512], BF16)) for i in range(2)]
            ps_qk = [eA(nc.psum_tensor(f"ps_qk{i}", [128, 512], F32)) for i in range(2)]
            ps_v = [eA(nc.psum_tensor(f"ps_v{i}", [128, D], F32)) for i in range(2)]

            P.wait("tensor", gc_const, a0_dg)

            NTB = NT // TBA
            tr_done = [0, 0]
            x_free = 0
            for tb in range(NTB):
                P.wait("sync", vv, x_free)
                d_x = P.op("sync", lambda E, tb=tb: E.dma_start(
                    out=x_sb[:],
                    in_=AP(x_in.tensor, tb * TBA * 128 * D, [[D, 128], [128 * D, TBA], [1, D]])), sc_x, 16)
                P.wait("vector", sc_x, d_x)
                for t in range(TBA):
                    P.op("vector", lambda E, t=t: E.bn_stats(stats[:, t, 0:6], x_sb[:, t, 0:512]))
                    last = P.op("vector", lambda E, t=t: E.bn_stats(stats[:, t, 6:12], x_sb[:, t, 512:1024]), vv)
                P.wait("vector", vv, last)
                for t in range(TBA):
                    agg = P.op("vector", lambda E, t=t: E.bn_aggr(mv[:, t, :], stats[:, t, :]), vv)
                P.wait("scalar", vv, agg)
                ln1 = P.op("scalar", lambda E: E.activation(lnv[:], mv[:, :, 1], AF.Ln, bias=eps_sb[:], scale=1.0), ss)
                P.wait("scalar", ss, ln1)
                rs = P.op("scalar", lambda E: E.activation(rstd[:], lnv[:], AF.Exp, scale=-0.5), ss)
                P.wait("vector", ss, rs)
                for t in range(TBA):
                    x_free = P.op("vector", lambda E, t=t: E.tensor_scalar(
                        xn_bf[:, t, :], x_sb[:, t, :], mv[:, t, 0:1], rstd[:, t:t + 1],
                        OP.subtract, OP.mult), vv)
                for c in range(8):
                    par = c % 2
                    P.wait("tensor", vv, x_free)
                    P.wait("tensor", vv, tr_done[par])
                    for t in range(TBA):
                        tp = P.op("tensor", lambda E, c=c, t=t, par=par: E.transpose(
                            ps_tr[par][:, t * 128:(t + 1) * 128],
                            xn_bf[:, t, c * 128:(c + 1) * 128], id_bf[:]), tt)
                    P.wait("vector", tt, tp)
                    tr_done[par] = P.op("vector", lambda E, c=c, tb=tb, par=par: E.tensor_copy(
                        xn_fm[:, c, tb * TBA * 128:(tb + 1) * TBA * 128], ps_tr[par][:, 0:TBA * 128]), vv)

            if b_dbg:
                P.wait("gpsimd", vv, max(tr_done))
                P.wait("gpsimd", gc_dbg, gc_dbg.n)
                _dmp = P.op("gpsimd", lambda E: E.dma_start(out=dbg["dbg_xn"], in_=xn_fm[:].rearrange("p a b -> p (a b)")), gc_dbg, 16)
                for _e in ("tensor", "vector", "scalar", "sync"):
                    P.wait(_e, gc_dbg, gc_dbg.n)
            xfm_all = max(tr_done)
            P.wait("tensor", vv, xfm_all)
            P.wait("scalar", sc_const, a0_ds)
            P.wait("scalar", vv, v_wbq)
            w_free = [0, 0]
            qk_ev = [0, 0]
            NF = S // 512
            for o in range(16):
                par = o % 2
                P.wait("gpsimd", tt, w_free[par])
                d_w = P.op("gpsimd", lambda E, o=o, par=par: E.dma_start(
                    out=wqkv_o[:, par, :, :],
                    in_=AP(wqkvT.tensor, o * 128, [[3 * D, 128], [128 * 3 * D, 8], [1, 128]])), gc_wq[par], 16)
                P.wait("tensor", gc_wq[par], d_w)
                for f in range(NF):
                    fpar = (o * NF + f) % 2
                    P.wait("tensor", ss, qk_ev[fpar])
                    for kc in range(8):
                        mm = P.op("tensor", lambda E, par=par, kc=kc, f=f, fpar=fpar: E.matmul(
                            ps_qk[fpar][:], wqkv_o[:, par, kc, :],
                            xn_fm[:, kc, f * 512:(f + 1) * 512],
                            start=(kc == 0), stop=(kc == 7)), tt)
                    if f == NF - 1:
                        w_free[par] = mm
                    P.wait("scalar", tt, mm)
                    if o < 8:
                        qk_ev[fpar] = P.op("scalar", lambda E, o=o, f=f, fpar=fpar: E.activation(
                            q_sb[:, o, f * 512:(f + 1) * 512], ps_qk[fpar][:], AF.Identity,
                            bias=wbq_sb[:, o:o + 1], scale=1.0 / 32.0), ss)
                    else:
                        qk_ev[fpar] = P.op("scalar", lambda E, o=o, f=f, fpar=fpar: E.activation(
                            k_sb[:, o - 8, f * 512:(f + 1) * 512], ps_qk[fpar][:], AF.Identity,
                            bias=wb_sb[:, o:o + 1], scale=1.0), ss)

            P.wait("gpsimd", tt, max(w_free))
            d_wv = P.op("gpsimd", lambda E: E.dma_start(
                out=wv_bf[:],
                in_=AP(wqkvT.tensor, 2048, [[3 * D, 128], [128 * 3 * D, 8], [1, D]])), gc_wv, 16)
            P.wait("tensor", gc_wv, d_wv)
            v_ev = [0, 0]
            for t in range(NT):
                par = t % 2
                P.wait("tensor", vv, v_ev[par])
                for fh in range(2):
                    P.op("tensor", lambda E, fh=fh, par=par: E.matmul(
                        ps_v[par][:, fh * 512:(fh + 1) * 512], ones_bf[0:1, :],
                        wbv_bf[:, fh * 512:(fh + 1) * 512], start=True, stop=False))
                for kc in range(8):
                    for fh in range(2):
                        mm = P.op("tensor", lambda E, t=t, kc=kc, fh=fh, par=par: E.matmul(
                            ps_v[par][:, fh * 512:(fh + 1) * 512],
                            xn_fm[:, kc, t * 128:(t + 1) * 128],
                            wv_bf[:, kc, fh * 512:(fh + 1) * 512],
                            start=False, stop=(kc == 7)), tt)
                P.wait("vector", tt, mm)
                v_ev[par] = P.op("vector", lambda E, t=t, par=par: E.tensor_copy(
                    AP(v_sb[:].tensor, t * (H * 65), [[NT * H * 65, 128], [65, H], [1, 64]]),
                    AP(ps_v[par][:].tensor, 0, [[D, 128], [64, H], [1, 64]])), vv)

        P.barrier(ALLSEMS)
        P.barrier(ALLSEMS)
        if b_dbg:
            P.wait("gpsimd", gc_dbg, gc_dbg.n)
            _dmp = P.op("gpsimd", lambda E: E.dma_start(out=dbg["dbg_q"], in_=q_sb[:].rearrange("p a b -> p (a b)")), gc_dbg, 16)
            for _e in ("tensor", "vector", "scalar", "sync"):
                P.wait(_e, gc_dbg, gc_dbg.n)
            P.wait("gpsimd", gc_dbg, gc_dbg.n)
            _dmp = P.op("gpsimd", lambda E: E.dma_start(out=dbg["dbg_k"], in_=k_sb[:].rearrange("p a b -> p (a b)")), gc_dbg, 16)
            for _e in ("tensor", "vector", "scalar", "sync"):
                P.wait(_e, gc_dbg, gc_dbg.n)
            P.wait("gpsimd", gc_dbg, gc_dbg.n)
            _dmp = P.op("gpsimd", lambda E: E.dma_start(out=dbg["dbg_v"], in_=v_sb[:].rearrange("p a b -> p (a b)")), gc_dbg, 16)
            for _e in ("tensor", "vector", "scalar", "sync"):
                P.wait(_e, gc_dbg, gc_dbg.n)
        with ExitStack() as stB:
            eB = stB.enter_context
            wext = eB(nc.sbuf_tensor("wext", [128, 2, WEXT], BF16))
            e_sb = eB(nc.sbuf_tensor("e_sb", [128, 2, WEXT], BF16))
            p_raw = eB(nc.sbuf_tensor("p_raw", [128, 4, 1024], F32))
            p_sb = eB(nc.sbuf_tensor("p_sb", [128, 4, 1024], BF16))
            recl = eB(nc.sbuf_tensor("recl", [128, 2, 4], F32))
            wchunk = eB(nc.sbuf_tensor("wchunk", [128, 2, 1024], F32))
            wabs = eB(nc.sbuf_tensor("wabs", [128, 2, 1024], F32))
            wsum_sb = eB(nc.sbuf_tensor("wsum_sb", [1, 512], F32))
            ps_st = [eB(nc.psum_tensor(f"ps_st{i}", [128, 1024], F32)) for i in range(2)]
            ps_o = [eB(nc.psum_tensor(f"ps_o{i}", [128, 260], F32)) for i in range(2)]
            ps_ws = eB(nc.psum_tensor("ps_ws", [1, 512], F32))

            P.wait("scalar", vv, v_wbq)

            w_srcs = [wgT, wuT, wdT]
            p1_units = [(wi, cc, hf) for wi in range(3) for cc in range(8) for hf in range(4)]
            p1 = {"dma": [0, 0], "abs": [0, 0], "mm": 0, "mm2": [0, 0], "ev": 0, "wsum_ev": 0}
            p1sems = [sc_p1, sc_x]

            def p1_issue_dma(u):
                wi, cc, hf = p1_units[u]
                wsrc = w_srcs[wi]
                upar = u % 2
                P.wait("sync", vv, p1["abs"][upar])

                def mk(E, wsrc=wsrc, cc=cc, hf=hf, upar=upar):
                    if wsrc is wdT:
                        src = AP(wsrc.tensor, (cc * 512 + hf * 128) * D, [[D, 128], [1, D]])
                        return E.dma_start(out=wchunk[:, upar, :], in_=src)
                    src = AP(wsrc.tensor, cc * 128 * INNER + hf * 1024, [[INNER, 128], [1, 1024]])
                    return E.dma_start(out=wchunk[:, upar, :], in_=src)
                p1["dma"][upar] = P.op("sync", mk, p1sems[upar], 16)

            def p1_abs(u):
                upar = u % 2
                P.wait("vector", p1sems[upar], p1["dma"][upar])
                P.wait("vector", tt, p1["mm2"][upar])
                ab = P.op("vector", lambda E, upar=upar: E.scalar_tensor_tensor(
                    wabs[:, upar, :], wchunk[:, upar, :], -1.0, wchunk[:, upar, :],
                    OP.mult, OP.max), vv)
                p1["abs"][upar] = ab

            def p1_unit_compute(u):
                wi = u // 32
                upar = u % 2
                P.wait("tensor", vv, p1["abs"][upar])
                if u % 32 == 0:
                    P.wait("tensor", vv, p1["wsum_ev"])
                for f in range(2):
                    mm = P.op("tensor", lambda E, f=f, u=u, upar=upar: E.matmul(
                        ps_ws[:], onec_f32[:], wabs[:, upar, f * 512:(f + 1) * 512],
                        start=(u % 32 == 0 and f == 0), stop=(u % 32 == 31 and f == 1)), tt)
                p1["mm"] = mm
                p1["mm2"][upar] = mm
                if u % 32 == 31:
                    P.wait("vector", tt, p1["mm"])
                    ev = P.op("vector", lambda E: E.tensor_copy(wsum_sb[:], ps_ws[:]), vv)
                    p1["wsum_ev"] = ev
                    P.wait("vector", vv, ev)
                    wdt2 = 256
                    while wdt2 >= 1:
                        sm = P.op("vector", lambda E, w2=wdt2: E.tensor_tensor(
                            wsum_sb[:, 0:w2], wsum_sb[:, 0:w2], wsum_sb[:, w2:2 * w2], OP.add), vv)
                        P.wait("vector", vv, sm)
                        wdt2 //= 2
                    sm = P.op("vector", lambda E, wi=wi: E.tensor_copy(
                        wstat[:, wi:wi + 1], wsum_sb[:, 0:1]), vv)
                    P.wait("vector", vv, sm)
                    mc = P.op("vector", lambda E, wi=wi: E.tensor_scalar(
                        wstat[:, wi:wi + 1], wstat[:, wi:wi + 1], 1.0 / float(D * INNER), 1e-5,
                        OP.mult, OP.max), vv)
                    P.wait("vector", vv, mc)
                    p1["ev"] = P.op("vector", lambda E, wi=wi: E.reciprocal(
                        wstat[:, 3 + wi:4 + wi], wstat[:, wi:wi + 1]), vv)

            p1_prog = {"mm": 0, "abs": 0, "issued": 0, "quads": 0}
            TOTQ = H * (NP // 2)

            def p1_tick():
                p1_prog["quads"] += 1
                base = p1_prog["quads"] * 96 // TOTQ
                while p1_prog["mm"] < base:
                    while p1_prog["abs"] <= p1_prog["mm"] + 1 and p1_prog["abs"] < 96:
                        p1_abs(p1_prog["abs"])
                        p1_prog["abs"] += 1
                        if p1_prog["issued"] < 96:
                            p1_issue_dma(p1_prog["issued"])
                            p1_prog["issued"] += 1
                    p1_unit_compute(p1_prog["mm"])
                    p1_prog["mm"] += 1

            p1_issue_dma(0)
            p1_issue_dma(1)
            p1_prog["issued"] = 2

            wx_free = [0, 0]
            wx_ready = [0, 0]
            st_free = [0, 0]
            pv_done = [0, 0, 0, 0]
            pmul = [0, 0, 0, 0]
            e_used = [0, 0]
            e_ready = [0, 0]
            fifo = []
            o_ev = [0, 0]
            gpar = [0]
            b_ss, b_vv = ss.n, vv.n

            def emit_head(h):
                par = h % 2
                P.wait("gpsimd", tt, wx_free[par])
                P.wait("gpsimd", vv, wx_ready[par])
                d_w = P.op("gpsimd", lambda E, h=h, par=par: E.dma_start(
                    out=wext[:, par, :],
                    in_=AP(rpbmt.tensor, h * R + (S - 512), [[1, 128], [1, WEXT]])), gc_we[par], 16)
                P.wait("vector", gc_we[par], d_w)
                P.wait("vector", tt, wx_free[par])
                P.op("vector", lambda E, par=par: E.memset(wext[:, par, 0:384], NEG))
                wx_ready[par] = P.op("vector", lambda E, par=par: E.tensor_tensor(
                    wext[:, par, 384:512], wext[:, par, 384:512], mask_sb[:], OP.add), vv)
                nE = (WEXT + 511) // 512
                for ei in range(nE):
                    w0 = ei * 512
                    w1 = min(WEXT, w0 + 512)
                    mypar = gpar[0] % 2
                    gpar[0] += 1
                    P.wait("tensor", ss, st_free[mypar])
                    if ei == 0:
                        P.wait("tensor", vv, wx_ready[par])
                    jm = P.op("tensor", lambda E, w0=w0, w1=w1, par=par, mypar=mypar: E.matmul(
                        ps_st[mypar][:, 0:w1 - w0], j_bf[:], wext[:, par, w0:w1],
                        start=True, stop=True), tt)
                    if ei == nE - 1:
                        wx_free[par] = jm
                    P.wait("scalar", tt, jm)
                    if ei == 0:
                        P.wait("scalar", vv, e_used[par])
                    st_free[mypar] = P.op("scalar", lambda E, w0=w0, w1=w1, par=par,
                                          mypar=mypar: E.activation(
                        e_sb[:, par, w0:w1], ps_st[mypar][:, 0:w1 - w0], AF.Exp,
                        bias=negmh[:], scale=1.0), ss)
                e_ready[par] = st_free[(gpar[0] - 1) % 2]

                first_use = True
                hh, hc = 64 * (h % 2), h // 2
                for qd in range(NP // 2):
                    nblocks = 4 * qd + 4
                    groups = [[b0, b0 + 1] for b0 in range(0, nblocks, 2)]
                    qdpar = qd % 2
                    pv_first = True
                    for grp in groups:
                        m2 = gpar[0] % 2
                        m3 = gpar[0] % 4
                        gpar[0] += 1
                        if first_use:
                            P.wait("tensor", ss, b_ss)
                            P.wait("tensor", vv, b_vv)
                            first_use = False
                        P.wait("tensor", ss, st_free[m2])
                        for bi, kc in enumerate(grp):
                            qm = P.op("tensor", lambda E, kc=kc, bi=bi, m2=m2, qd=qd,
                                      hh=hh, hc=hc: E.matmul(
                                ps_st[m2][:, bi * 512:(bi + 1) * 512],
                                k_sb[hh:hh + 64, hc, kc * 128:(kc + 1) * 128],
                                q_sb[hh:hh + 64, hc, qd * 512:(qd + 1) * 512],
                                start=True, stop=True), tt)
                        P.wait("scalar", tt, qm)
                        P.wait("scalar", vv, pmul[m3])
                        st_free[m2] = P.op("scalar", lambda E, m2=m2, m3=m3: E.activation(
                            p_raw[:, m3, :], ps_st[m2][:], AF.Exp,
                            bias=negmh[:], scale=1.0), ss)
                        while len(fifo) >= 3:
                            fifo.pop(0)()
                        P.wait("vector", ss, st_free[m2])
                        P.wait("vector", ss, e_ready[par])
                        P.wait("vector", tt, pv_done[m3])
                        for bi, kc in enumerate(grp):
                            off = 384 + (4 * qd - kc) * 128
                            pmul[m3] = P.op("vector", lambda E, off=off, bi=bi, par=par,
                                            m3=m3: E.tensor_tensor(
                                p_sb[:, m3, bi * 512:(bi + 1) * 512],
                                p_raw[:, m3, bi * 512:(bi + 1) * 512],
                                e_sb[:, par, off:off + 512], OP.mult), vv)
                        e_used[par] = pmul[m3]

                        def mk_pv(grp=grp, m3=m3, qdpar=qdpar, h=h, stt=pv_first,
                                  nblocks=nblocks, pm_val=pmul[m3]):
                            P.wait("tensor", vv, pm_val)
                            if stt:
                                P.wait("tensor", ss, o_ev[qdpar])
                            pm = 0
                            for bi, kc in enumerate(grp):
                                for qs in range(4):
                                    pm = P.op("tensor", lambda E, kc=kc, bi=bi, qs=qs, h=h,
                                              m3=m3, qdpar=qdpar,
                                              stt=(stt and bi == 0 and qs == 0),
                                              spp=(kc == nblocks - 1 and qs == 3): E.matmul(
                                        ps_o[qdpar][:, qs * 65:qs * 65 + 65],
                                        p_sb[:, m3, bi * 512 + qs * 128:bi * 512 + qs * 128 + 128],
                                        v_sb[:, kc, h * 65:h * 65 + 65],
                                        start=stt, stop=spp), tt)
                            pv_done[m3] = pm

                        fifo.append(mk_pv)
                        pv_first = False

                    def mk_evac(qd=qd, qdpar=qdpar, h=h, m3_last=(gpar[0] - 1) % 4):
                        P.wait("vector", tt, pv_done[m3_last])
                        P.wait("vector", ss, o_ev[qdpar])
                        rr = P.op("vector", lambda E, qdpar=qdpar: E.reciprocal(
                            recl[:, qdpar, :],
                            AP(ps_o[qdpar][:].tensor, 64, [[260, 128], [65, 4]])), vv)
                        P.wait("scalar", vv, rr)
                        for qs in range(4):
                            o_ev[qdpar] = P.op("scalar", lambda E, qs=qs, qd=qd, h=h,
                                               qdpar=qdpar: E.activation(
                                o_sb[:, 4 * qd + qs, h * 64:h * 64 + 64],
                                ps_o[qdpar][:, qs * 65:qs * 65 + 64],
                                AF.Identity, scale=recl[:, qdpar, qs:qs + 1]), ss)

                    fifo.append(mk_evac)
                    p1_tick()

            for h in range(H):
                emit_head(h)
            while fifo:
                fifo.pop(0)()

            b_end_ss = ss.n
            if b_dbg:
                P.wait("gpsimd", ss, b_end_ss)
                P.wait("gpsimd", gc_dbg, gc_dbg.n)
                _dmp = P.op("gpsimd", lambda E: E.dma_start(out=dbg["dbg_o"], in_=o_sb[:].rearrange("p a b -> p (a b)")), gc_dbg, 16)
                for _e in ("tensor", "vector", "scalar", "sync"):
                    P.wait(_e, gc_dbg, gc_dbg.n)

        stQKV.close()
        P.barrier(ALLSEMS)
        with ExitStack() as stB2:
            eB2 = stB2.enter_context
            wproj_bf = eB2(nc.sbuf_tensor("wproj_bf", [128, 8, D], BF16))
            o_fm = eB2(nc.sbuf_tensor("o_fm", [128, 8, 128], BF16))
            x_rld = eB2(nc.sbuf_tensor("x_rld", [128, 2, D], F32))
            xn1 = eB2(nc.sbuf_tensor("xn1", [128, 2, D], F32))
            ybuf = eB2(nc.sbuf_tensor("ybuf", [128, D], F32))
            zbuf = eB2(nc.sbuf_tensor("zbuf", [128, D], F32))
            x1buf = eB2(nc.sbuf_tensor("x1buf", [128, 2, D], F32))
            st2 = eB2(nc.sbuf_tensor("st2", [128, 12], F32))
            mv2 = eB2(nc.sbuf_tensor("mv2", [128, 2], F32))
            rs2 = eB2(nc.sbuf_tensor("rs2", [128, 1], F32))
            nb2 = eB2(nc.sbuf_tensor("nb2", [128, 1], F32))
            mva = eB2(nc.sbuf_tensor("mva", [128, 2, 2], F32))
            rsa = eB2(nc.sbuf_tensor("rsa", [128, 2], F32))
            nba = eB2(nc.sbuf_tensor("nba", [128, 2], F32))
            g_bc = eB2(nc.sbuf_tensor("g_bc", [128, D], F32))
            b_bc = eB2(nc.sbuf_tensor("b_bc", [128, D], F32))
            pb_bc = eB2(nc.sbuf_tensor("pb_bc", [128, D], F32))
            rows = eB2(nc.sbuf_tensor("rows", [1, 3 * D], F32))
            ps_tr2 = [eB2(nc.psum_tensor(f"ps_tr2{i}", [128, 512], BF16)) for i in range(2)]
            ps_pj = [eB2(nc.psum_tensor(f"ps_pj{i}", [128, 512], F32)) for i in range(2)]

            P.op("sync", lambda E: E.dma_start(out=rows[:, 0:D], in_=projb_row), sc_const, 16)
            P.op("sync", lambda E: E.dma_start(out=rows[:, D:2 * D], in_=lng_row), sc_const, 16)
            d_r2 = P.op("sync", lambda E: E.dma_start(out=rows[:, 2 * D:3 * D], in_=lnb_row), sc_const, 16)
            d_wp = P.op("gpsimd", lambda E: E.dma_start(
                out=wproj_bf[:],
                in_=AP(wprojT.tensor, 0, [[D, 128], [128 * D, 8], [1, D]])), gc_wp, 16)
            P.wait("tensor", sc_const, d_r2)
            P.wait("tensor", vv, p1["ev"])
            bc_dsts = [pb_bc, g_bc, b_bc]
            bc_ev = [0, 0]
            for bi3 in range(3):
                for fh in range(2):
                    P.wait("tensor", vv, bc_ev[fh])
                    bm = P.op("tensor", lambda E, bi3=bi3, fh=fh: E.matmul(
                        ps_pj[fh][:], ones_f32[:], rows[:, bi3 * D + fh * 512:bi3 * D + (fh + 1) * 512],
                        start=True, stop=True), tt)
                    P.wait("vector", tt, bm)
                    bcd = P.op("vector", lambda E, bi3=bi3, fh=fh: E.tensor_copy(
                        bc_dsts[bi3][:, fh * 512:(fh + 1) * 512], ps_pj[fh][:]), vv)
                    bc_ev[fh] = bcd
            P.wait("tensor", vv, bc_ev[0])
            bm = P.op("tensor", lambda E: E.matmul(ps_pj[0][:, 0:8], ones_f32[:], wstat[:],
                                                   start=True, stop=True), tt)
            P.wait("vector", tt, bm)
            P.op("vector", lambda E: E.tensor_copy(mwg_b[:], ps_pj[0][:, 0:3]))
            swb = P.op("vector", lambda E: E.tensor_copy(swg_b[:], ps_pj[0][:, 3:6]), vv)
            if b_dbg:
                P.wait("gpsimd", vv, swb)
                P.wait("gpsimd", gc_dbg, gc_dbg.n)
                _dmp = P.op("gpsimd", lambda E: E.dma_start(out=dbg["dbg_wm"], in_=wstat[:]), gc_dbg, 16)
                for _e in ("tensor", "vector", "scalar", "sync"):
                    P.wait(_e, gc_dbg, gc_dbg.n)


            tr2_ev = [0, 0]
            ypj = [0, 0]
            x1_st = [0, 0]
            x_free = [0, 0]
            x_free_s = [0, 0]
            s1_apply = [0, 0]
            ytt = [0, 0]
            zap = 0
            x1ap = 0
            ztt2 = 0
            pm_box = [0]
            d_x = [0, 0]

            def b2_dma(qi):
                par = qi % 2
                P.wait("sync", vv, x_free[par])
                P.wait("sync", ss, x_free_s[par])
                d_x[par] = P.op("sync", lambda E, qi=qi, par=par: E.dma_start(
                    out=x_rld[:, par, :], in_=x_in[qi * 128:(qi + 1) * 128, :]), sc_xr[par], 16)

            def b2_s1(qi):
                par = qi % 2
                P.wait("vector", sc_xr[par], d_x[par])
                if qi == 0:
                    P.wait("vector", vv, bcd)
                P.op("vector", lambda E, par=par: E.bn_stats(st2[:, 0:6], x_rld[:, par, 0:512]))
                sb = P.op("vector", lambda E, par=par: E.bn_stats(st2[:, 6:12], x_rld[:, par, 512:1024]), vv)
                P.wait("vector", vv, sb)
                ag = P.op("vector", lambda E, par=par: E.bn_aggr(mva[:, par, :], st2[:]), vv)
                P.wait("scalar", vv, ag)
                lnx = P.op("scalar", lambda E, par=par: E.activation(
                    rsa[:, par:par + 1], mva[:, par, 1:2], AF.Ln, bias=eps_sb[:], scale=1.0), ss)
                P.wait("scalar", ss, lnx)
                rr = P.op("scalar", lambda E, par=par: E.activation(
                    rsa[:, par:par + 1], rsa[:, par:par + 1], AF.Exp, scale=-0.5), ss)
                P.wait("vector", ss, rr)
                nb = P.op("vector", lambda E, par=par: E.scalar_tensor_tensor(
                    nba[:, par:par + 1], mva[:, par, 0:1], -1.0, rsa[:, par:par + 1],
                    OP.mult, OP.mult), vv)
                P.wait("scalar", vv, nb)
                P.wait("scalar", vv, ytt[par])
                s1_apply[par] = P.op("scalar", lambda E, par=par: E.activation(
                    xn1[:, par, :], x_rld[:, par, :], AF.Identity,
                    bias=nba[:, par:par + 1], scale=rsa[:, par:par + 1]), ss)
                x_free_s[par] = s1_apply[par]

            def b2_pe(qi):
                P.wait("tensor", ss, b_end_ss)
                for cg in range(2):
                    tpp = cg % 2
                    P.wait("tensor", ss, tr2_ev[tpp])
                    P.wait("scalar", tt, pm_box[0])
                    for c4 in range(4):
                        c = cg * 4 + c4
                        tp = P.op("tensor", lambda E, qi=qi, c=c, c4=c4, tpp=tpp: E.transpose(
                            ps_tr2[tpp][:, c4 * 128:(c4 + 1) * 128],
                            o_sb[:, qi, c * 128:(c + 1) * 128], id_bf[:]), tt)
                    P.wait("scalar", tt, tp)
                    tr2_ev[tpp] = P.op("scalar", lambda E, cg=cg, tpp=tpp: E.activation(
                        o_fm[:].rearrange("p a b -> p (a b)")[:, cg * 512:(cg + 1) * 512],
                        ps_tr2[tpp][:], AF.Identity, scale=1.0), ss)
                P.wait("tensor", gc_wp, d_wp)
                P.wait("tensor", ss, max(tr2_ev))
                mm = 0
                for fh in range(2):
                    P.wait("tensor", vv, ypj[fh])
                    for kc in range(8):
                        mm = P.op("tensor", lambda E, fh=fh, kc=kc: E.matmul(
                            ps_pj[fh][:], o_fm[:, kc, :],
                            wproj_bf[:, kc, fh * 512:(fh + 1) * 512],
                            start=(kc == 0), stop=(kc == 7)), tt)
                pm_box[0] = mm
                return mm

            def b2_s2(qi, mm):
                nonlocal_ = {}
                par = qi % 2
                P.wait("vector", tt, mm)
                P.wait("vector", ss, zap)
                P.wait("vector", ss, s1_apply[par])
                for fh in range(2):
                    st_ = P.op("vector", lambda E, fh=fh: E.scalar_tensor_tensor(
                        ybuf[:, fh * 512:(fh + 1) * 512], ps_pj[fh][:], 0.0,
                        pb_bc[:, fh * 512:(fh + 1) * 512], OP.add, OP.add), vv)
                    P.wait("vector", vv, st_)
                    yt = P.op("vector", lambda E, fh=fh, par=par: E.tensor_tensor(
                        ybuf[:, fh * 512:(fh + 1) * 512], ybuf[:, fh * 512:(fh + 1) * 512],
                        xn1[:, par, fh * 512:(fh + 1) * 512], OP.add), vv)
                    ypj[fh] = yt
                ytt[par] = yt
                P.wait("vector", vv, yt)
                P.op("vector", lambda E: E.bn_stats(st2[:, 0:6], ybuf[:, 0:512]))
                sb = P.op("vector", lambda E: E.bn_stats(st2[:, 6:12], ybuf[:, 512:1024]), vv)
                P.wait("vector", vv, sb)
                ag = P.op("vector", lambda E: E.bn_aggr(mv2[:], st2[:]), vv)
                P.wait("scalar", vv, ag)
                lnx = P.op("scalar", lambda E: E.activation(rs2[:], mv2[:, 1:2], AF.Ln, bias=eps_sb[:], scale=1.0), ss)
                P.wait("scalar", ss, lnx)
                rr = P.op("scalar", lambda E: E.activation(rs2[:], rs2[:], AF.Exp, scale=-0.5), ss)
                P.wait("vector", ss, rr)
                nb = P.op("vector", lambda E: E.scalar_tensor_tensor(
                    nb2[:], mv2[:, 0:1], -1.0, rs2[:], OP.mult, OP.mult), vv)
                P.wait("scalar", vv, nb)
                P.wait("scalar", gg, ztt2)
                zap_ = P.op("scalar", lambda E: E.activation(
                    zbuf[:], ybuf[:], AF.Identity, bias=nb2[:], scale=rs2[:]), ss)
                P.wait("vector", ss, zap_)
                zt1 = P.op("vector", lambda E: E.tensor_tensor(zbuf[:], zbuf[:], g_bc[:], OP.mult), vv)
                P.wait("vector", vv, zt1)
                zt2 = P.op("vector", lambda E: E.tensor_tensor(zbuf[:], zbuf[:], b_bc[:], OP.add), vv)
                P.wait("vector", vv, zt2)
                P.op("vector", lambda E: E.bn_stats(st2[:, 0:6], zbuf[:, 0:512]))
                sb = P.op("vector", lambda E: E.bn_stats(st2[:, 6:12], zbuf[:, 512:1024]), vv)
                P.wait("vector", vv, sb)
                ag = P.op("vector", lambda E: E.bn_aggr(mv2[:], st2[:]), vv)
                P.wait("scalar", vv, ag)
                lnx = P.op("scalar", lambda E: E.activation(rs2[:], mv2[:, 1:2], AF.Ln, bias=eps_sb[:], scale=1.0), ss)
                P.wait("scalar", ss, lnx)
                rr = P.op("scalar", lambda E: E.activation(rs2[:], rs2[:], AF.Exp, scale=-0.5), ss)
                P.wait("vector", ss, rr)
                nb = P.op("vector", lambda E: E.scalar_tensor_tensor(
                    nb2[:], mv2[:, 0:1], -1.0, rs2[:], OP.mult, OP.mult), vv)
                P.wait("scalar", vv, nb)
                P.wait("scalar", sc_x1[par], x1_st[par])
                x1a = P.op("scalar", lambda E, par=par: E.activation(
                    x1buf[:, par, :], zbuf[:], AF.Identity, bias=nb2[:], scale=rs2[:]), ss)
                P.wait("vector", ss, x1a)
                xa = P.op("vector", lambda E, par=par: E.tensor_tensor(
                    x1buf[:, par, :], x1buf[:, par, :], x_rld[:, par, :], OP.add), vv)
                x_free[par] = xa
                P.wait("sync", vv, xa)
                x1_st[par] = P.op("sync", lambda E, qi=qi, par=par: E.dma_start(
                    out=x1_dram[qi * 128:(qi + 1) * 128, :], in_=x1buf[:, par, :]), sc_x1[par], 16)
                return zap_, x1a

            b2_dma(0)
            if NT > 1:
                b2_dma(1)
            b2_s1(0)
            for qi in range(NT):
                mm = b2_pe(qi)
                if qi + 1 < NT:
                    b2_s1(qi + 1)
                zap, x1ap = b2_s2(qi, mm)
                if qi + 2 < NT:
                    b2_dma(qi + 2)

            if b_dbg:
                P.wait("gpsimd", sc_x1[0], x1_st[0])
                P.wait("gpsimd", sc_x1[1], x1_st[1])
                P.wait("gpsimd", gc_dbg, gc_dbg.n)
                _dmp = P.op("gpsimd", lambda E: E.dma_start(out=dbg["dbg_x1"], in_=x1_dram), gc_dbg, 16)
                for _e in ("tensor", "vector", "scalar", "sync"):
                    P.wait(_e, gc_dbg, gc_dbg.n)

        stO.close()
        P.barrier(ALLSEMS)
        with ExitStack() as stC:
            eC = stC.enter_context
            wdq_sb = eC(nc.sbuf_tensor("wdq_sb", [128, 32, D], BF16))
            st3 = eC(nc.sbuf_tensor("st3", [128, 12], F32))
            mv3 = eC(nc.sbuf_tensor("mv3", [128, 2], F32))
            rs3 = eC(nc.sbuf_tensor("rs3", [128, 1], F32))
            sc1 = eC(nc.sbuf_tensor("sc1", [128, NT], F32))
            rsg = eC(nc.sbuf_tensor("rsg", [128, NT], F32))
            rsu = eC(nc.sbuf_tensor("rsu", [128, NT], F32))
            rsd = eC(nc.sbuf_tensor("rsd", [128, NT], F32))
            magic_c = eC(nc.sbuf_tensor("magic_c", [128, 1], F32))
            nmagic_c = eC(nc.sbuf_tensor("nmagic_c", [128, 1], F32))
            ps_t3 = [eC(nc.psum_tensor(f"ps_t3{i}", [128, 512], BF16)) for i in range(2)]
            stQ3 = ExitStack()
            stCG = ExitStack()
            xq3_fm = stQ3.enter_context(nc.sbuf_tensor("xq3_fm", [128, 8, S], BF16))
            wgf = stCG.enter_context(nc.sbuf_tensor("wgf", [128, 4, 512], F32))
            wuf = stCG.enter_context(nc.sbuf_tensor("wuf", [128, 4, 512], F32))
            wdf = stCG.enter_context(nc.sbuf_tensor("wdf", [128, 2, D], F32))
            wq_t = stCG.enter_context(nc.sbuf_tensor("wq_t", [128, 2048], F32))
            wq_c1 = stCG.enter_context(nc.sbuf_tensor("wq_c1", [128, 2048], BF16))
            wq_c2 = stCG.enter_context(nc.sbuf_tensor("wq_c2", [128, 2048], BF16))
            wgq_sb = stCG.enter_context(nc.sbuf_tensor("wgq_sb", [128, 2, 8, 512], BF16))
            wuq_sb = stCG.enter_context(nc.sbuf_tensor("wuq_sb", [128, 2, 8, 512], BF16))
            g_act = stCG.enter_context(nc.sbuf_tensor("g_act", [128, 2, 512], BF16))
            prod_sb = stCG.enter_context(nc.sbuf_tensor("prod_sb", [128, 2, 512], F32))
            ps_g = [stCG.enter_context(nc.psum_tensor(f"ps_g{i}", [128, 512], F32)) for i in range(2)]
            ps_u = [stCG.enter_context(nc.psum_tensor(f"ps_u{i}", [128, 512], F32)) for i in range(2)]

            P.op("vector", lambda E: E.memset(magic_c[:], 6755399441055744.0))
            mgd = P.op("vector", lambda E: E.memset(nmagic_c[:], -6755399441055744.0), vv)

            sc_ps = [sc_p2l, sc_p2s]
            wsems = {"g": gc_gu, "u": gc_wq, "d": gc_wd}
            wsrcs = {"g": wgT, "u": wuT}
            wbufs = {"g": wgf, "u": wuf}
            wqsb = {"g": wgq_sb, "u": wuq_sb}
            widx = {"g": 0, "u": 1, "d": 2}
            wbuf_free = {"g": 0, "u": 0, "d": 0}
            wdma = {("g", 0): 0, ("g", 1): 0, ("u", 0): 0, ("u", 1): 0,
                    ("d", 0): 0, ("d", 1): 0}
            qfree = [0]
            qcons = [0]
            quv = {"g": [0, 0], "u": [0, 0]}
            gu_used = [0, 0]

            def emit_wdma(ic):
                for mat in ("g", "u"):
                    P.wait("gpsimd", ss, wbuf_free[mat])
                    wdma[(mat, 0)] = P.op(
                        "gpsimd", lambda E, mat=mat, ic=ic: E.dma_start(
                            out=wbufs[mat][:],
                            in_=AP(wsrcs[mat].tensor, ic * 512,
                                   [[INNER, 128], [128 * INNER, 4], [1, 512]])),
                        wsems[mat][0], 16)
                P.wait("gpsimd", ss, wbuf_free["d"])
                wdma[("d", 0)] = P.op("gpsimd", lambda E, ic=ic: E.dma_start(
                    out=wdf[:],
                    in_=AP(wdT.tensor, ic * 512 * D, [[D, 128], [128 * D, 2], [1, D]])),
                    gc_wd[0], 16)

            def emit_qunit(mat, ic, hf):
                icpar = ic % 2
                src = wdf[:] if mat == "d" else wbufs[mat][:]
                dsem, dval = wsems[mat][hf], wdma[(mat, hf)]
                P.wait("scalar", dsem, dval)
                P.wait("scalar", vv, qfree[0])
                P.wait("scalar", vv, swb)
                tw = P.op("scalar", lambda E, src=src, mat=mat: E.activation(
                    wq_t[:], src, AF.Identity, scale=swg_b[:, widx[mat]:widx[mat] + 1]), ss)
                if hf == 0:
                    P.wait("gpsimd", ss, tw)
                    if mat == "d":
                        wdma[("d", 1)] = P.op("gpsimd", lambda E, ic=ic: E.dma_start(
                            out=wdf[:],
                            in_=AP(wdT.tensor, (ic * 512 + 256) * D,
                                   [[D, 128], [128 * D, 2], [1, D]])),
                            gc_wd[1], 16)
                    else:
                        wdma[(mat, 1)] = P.op(
                            "gpsimd", lambda E, mat=mat, ic=ic: E.dma_start(
                                out=wbufs[mat][:],
                                in_=AP(wsrcs[mat].tensor, 512 * INNER + ic * 512,
                                       [[INNER, 128], [128 * INNER, 4], [1, 512]])),
                            wsems[mat][1], 16)
                else:
                    wbuf_free[mat] = tw
                P.wait("vector", ss, tw)
                P.wait("vector", vv, qcons[0])
                P.op("vector", lambda E: E.tensor_scalar(wq_c1[:], wq_t[:], 0.5, None, OP.is_gt))
                cmp2 = P.op("vector", lambda E: E.tensor_scalar(wq_c2[:], wq_t[:], -0.5, None, OP.is_lt), vv)
                qfree[0] = cmp2
                P.wait("vector", vv, cmp2)
                if mat == "d":
                    dst = wdq_sb[:, 4 * ic + 2 * hf:4 * ic + 2 * hf + 2, :]
                else:
                    if hf == 0:
                        P.wait("vector", tt, gu_used[icpar])
                    dst = wqsb[mat][:, icpar, 4 * hf:4 * hf + 4, :]
                qq_ = P.op("vector", lambda E, dst=dst: E.tensor_tensor(
                    dst.rearrange("p a b -> p (a b)"), wq_c1[:], wq_c2[:], OP.subtract), vv)
                qcons[0] = qq_
                if mat != "d":
                    quv[mat][icpar] = qq_

            UNITS = [("g", 0), ("g", 1), ("u", 0), ("u", 1), ("d", 0), ("d", 1)]
            unit_ts = [1 + i * max(1, (NT - 2) // 6) for i in range(6)]

            x1t_used = [0, 0]
            xq3_done = [0] * NT
            i32_free = 0
            tr3_ev = [0, 0]
            emit_wdma(0)
            with ExitStack() as stC0:
                e0 = stC0.enter_context
                x1t = e0(nc.sbuf_tensor("x1t", [128, 2, D], F32))
                xn3 = e0(nc.sbuf_tensor("xn3", [128, D], F32))
                xq_i32a = e0(nc.sbuf_tensor("xq_i32a", [128, D], I32))
                d_bfa = e0(nc.sbuf_tensor("d_bfa", [128, D], F32))
                xq3_tm = e0(nc.sbuf_tensor("xq3_tm", [128, D], BF16))
                for t in range(NT):
                    par = t % 2
                    P.wait("sync", vv, x1t_used[par])
                    d_x1 = P.op("sync", lambda E, t=t, par=par: E.dma_start(
                        out=x1t[:, par, :], in_=x1_dram[t * 128:(t + 1) * 128, :]), sc_xr[par], 16)
                    P.wait("vector", sc_xr[par], d_x1)
                    if t == 0:
                        P.wait("vector", vv, swb)
                        P.wait("vector", vv, mgd)
                    P.op("vector", lambda E, par=par: E.bn_stats(st3[:, 0:6], x1t[:, par, 0:512]))
                    sb = P.op("vector", lambda E, par=par: E.bn_stats(st3[:, 6:12], x1t[:, par, 512:1024]), vv)
                    P.wait("vector", vv, sb)
                    ag = P.op("vector", lambda E: E.bn_aggr(mv3[:], st3[:]), vv)
                    P.wait("scalar", vv, ag)
                    lnx = P.op("scalar", lambda E: E.activation(rs3[:], mv3[:, 1:2], AF.Ln, bias=eps_sb[:], scale=1.0), ss)
                    P.wait("scalar", ss, lnx)
                    rr = P.op("scalar", lambda E: E.activation(rs3[:], rs3[:], AF.Exp, scale=-0.5), ss)
                    P.wait("vector", ss, rr)
                    xnd = P.op("vector", lambda E, par=par: E.tensor_scalar(
                        xn3[:], x1t[:, par, :], mv3[:, 0:1], rs3[:], OP.subtract, OP.mult), vv)
                    x1t_used[par] = xnd
                    P.wait("vector", vv, xnd)
                    am = P.op("vector", lambda E, t=t: E.reduce_max(
                        sc1[:, t:t + 1], xn3[:], axis=AX.X, apply_absolute_value=True), vv)
                    P.wait("vector", vv, am)
                    mc1 = P.op("vector", lambda E, t=t: E.tensor_scalar(
                        sc1[:, t:t + 1], sc1[:, t:t + 1], 1e-5, None, OP.max), vv)
                    P.wait("vector", vv, mc1)
                    P.op("vector", lambda E, t=t: E.scalar_tensor_tensor(
                        rsg[:, t:t + 1], sc1[:, t:t + 1], 1.0 / 127.0, mwg_b[:, 0:1], OP.mult, OP.mult))
                    ru = P.op("vector", lambda E, t=t: E.scalar_tensor_tensor(
                        rsu[:, t:t + 1], sc1[:, t:t + 1], 1.0 / 127.0, mwg_b[:, 1:2], OP.mult, OP.mult), vv)
                    P.wait("vector", vv, ru)
                    rc = P.op("vector", lambda E, t=t: E.reciprocal(sc1[:, t:t + 1], sc1[:, t:t + 1]), vv)
                    P.wait("vector", vv, rc)
                    s127 = P.op("vector", lambda E, t=t: E.tensor_scalar(
                        sc1[:, t:t + 1], sc1[:, t:t + 1], 127.0, None, OP.mult), vv)
                    P.wait("scalar", vv, s127)
                    P.wait("scalar", vv, i32_free)
                    qi32 = P.op("scalar", lambda E, t=t: E.activation(
                        xq_i32a[:], xn3[:], AF.Identity, scale=sc1[:, t:t + 1]), ss)
                    P.wait("vector", ss, qi32)
                    dd1 = P.op("vector", lambda E, t=t: E.scalar_tensor_tensor(
                        d_bfa[:], xn3[:], sc1[:, t:t + 1], xq_i32a[:], OP.mult, OP.subtract), vv)
                    P.wait("scalar", vv, dd1)
                    da = P.op("scalar", lambda E: E.activation(
                        d_bfa[:], d_bfa[:], AF.Identity, bias=magic_c[:], scale=1.0), ss)
                    P.wait("scalar", ss, da)
                    dd2b = P.op("scalar", lambda E: E.activation(
                        d_bfa[:], d_bfa[:], AF.Identity, bias=nmagic_c[:], scale=1.0), ss)
                    P.wait("vector", ss, dd2b)
                    i32_free = P.op("vector", lambda E: E.scalar_tensor_tensor(
                        xq3_tm[:], d_bfa[:], 1.0, xq_i32a[:], OP.mult, OP.add), vv)
                    P.wait("tensor", vv, i32_free)
                    for cg in range(2):
                        tpp = (2 * t + cg) % 2
                        P.wait("tensor", vv, tr3_ev[tpp])
                        for c4 in range(4):
                            c = cg * 4 + c4
                            tp = P.op("tensor", lambda E, c=c, c4=c4, tpp=tpp: E.transpose(
                                ps_t3[tpp][:, c4 * 128:(c4 + 1) * 128],
                                xq3_tm[:, c * 128:(c + 1) * 128], id_bf[:]), tt)
                        P.wait("vector", tt, tp)
                        tr3_ev[tpp] = P.op("vector", lambda E, t=t, cg=cg, tpp=tpp: E.tensor_copy(
                            AP(xq3_fm[:].tensor, cg * 4 * S + t * 128,
                               [[8 * S, 128], [S, 4], [1, 128]]),
                            ps_t3[tpp][:].rearrange("p (a b) -> p a b", a=4)), vv)
                    xq3_done[t] = tr3_ev[(2 * t + 1) % 2]
                    for ui, (mat, hf) in enumerate(UNITS):
                        if unit_ts[ui] == t + 1 or (t == NT - 1 and unit_ts[ui] > NT - 1):
                            emit_qunit(mat, 0, hf)

            silu_done = [0, 0]
            prod_done = [0, 0]
            prod_st = [0, 0]
            for ic in range(8):
                icpar = ic % 2
                if ic + 1 < 8:
                    emit_wdma(ic + 1)
                P.wait("tensor", vv, quv["g"][icpar])
                P.wait("tensor", vv, quv["u"][icpar])
                for t in range(NT):
                    tpar = t % 2
                    if ic == 0:
                        P.wait("tensor", vv, xq3_done[t])
                    P.wait("tensor", ss, silu_done[tpar])
                    P.wait("tensor", vv, prod_done[tpar])
                    for kc in range(8):
                        P.op("tensor", lambda E, kc=kc, t=t, icpar=icpar, tpar=tpar: E.matmul(
                            ps_g[tpar][:], xq3_fm[:, kc, t * 128:(t + 1) * 128],
                            wgq_sb[:, icpar, kc, :], start=(kc == 0), stop=(kc == 7)))
                    for kc in range(8):
                        mmu = P.op("tensor", lambda E, kc=kc, t=t, icpar=icpar, tpar=tpar: E.matmul(
                            ps_u[tpar][:], xq3_fm[:, kc, t * 128:(t + 1) * 128],
                            wuq_sb[:, icpar, kc, :], start=(kc == 0), stop=(kc == 7)), tt)
                    gu_used[icpar] = mmu
                    P.wait("scalar", tt, mmu)
                    P.wait("scalar", vv, prod_done[tpar])
                    silu_done[tpar] = P.op("scalar", lambda E, t=t, tpar=tpar: E.activation(
                        g_act[:, tpar, :], ps_g[tpar][:], AF.Silu, scale=rsg[:, t:t + 1]), ss)
                    P.wait("vector", ss, silu_done[tpar])
                    P.wait("vector", sc_ps[tpar], prod_st[tpar])
                    prod_done[tpar] = P.op("vector", lambda E, tpar=tpar: E.tensor_tensor(
                        prod_sb[:, tpar, :], g_act[:, tpar, :], ps_u[tpar][:], OP.mult), vv)
                    P.wait("sync", vv, prod_done[tpar])
                    prod_st[tpar] = P.op("sync", lambda E, ic=ic, t=t, tpar=tpar: E.dma_start(
                        out=AP(gu_dram.tensor, t * 128 * INNER + ic * 512, [[INNER, 128], [1, 512]]),
                        in_=prod_sb[:, tpar, :]), sc_ps[tpar], 16)
                    if ic + 1 < 8:
                        for ui, (mat, hf) in enumerate(UNITS):
                            if unit_ts[ui] == t + 1 or (t == NT - 1 and unit_ts[ui] > NT - 1):
                                emit_qunit(mat, ic + 1, hf)

            stCG.close()
            stQ3.close()
            P.barrier(ALLSEMS)
            with ExitStack() as stCD:
                eD = stCD.enter_context
                gu_t = eD(nc.sbuf_tensor("gu_t", [128, 2, INNER], F32))
                x1t2 = eD(nc.sbuf_tensor("x1t2", [128, 2, D], F32))
                xqgu_tm = eD(nc.sbuf_tensor("xqgu_tm", [128, 2, INNER], BF16))
                xqgu_fm = eD(nc.sbuf_tensor("xqgu_fm", [128, 2, 32, 128], BF16))
                xq_i32b = eD(nc.sbuf_tensor("xq_i32b", [128, INNER], I32))
                d_bfb = eD(nc.sbuf_tensor("d_bfb", [128, INNER], F32))
                mpt = eD(nc.sbuf_tensor("mpt", [128, NT, 3], F32))
                ffo = eD(nc.sbuf_tensor("ffo", [128, 2, D], F32))
                outb = eD(nc.sbuf_tensor("outb", [128, 2, D], F32))
                ps_d = [eD(nc.psum_tensor(f"ps_d{i}", [128, D], F32)) for i in range(2)]

                gu_free = [0, 0]
                gu_free_s = [0, 0]
                gu_free_g = [0, 0]
                x1_free = [0, 0]
                ffo_used = [0, 0]
                ffo_ev = [0, 0]
                out_st = [0, 0]
                d_gu = [0, 0]
                d_x1 = [0, 0]
                dd1_d = [0, 0]
                dd2_d = [0, 0]
                comb_d = [0, 0]
                qi32_d = [0, 0]
                copy_d = {}
                tr_d = {}
                mm_d = [0, 0]

                def cd_dma_gu(t):
                    par = t % 2
                    P.wait("sync", vv, gu_free[par])
                    P.wait("sync", ss, gu_free_s[par])
                    d_gu[par] = P.op("sync", lambda E, t=t, par=par: E.dma_start(
                        out=gu_t[:, par, :],
                        in_=AP(gu_dram.tensor, t * 128 * INNER, [[INNER, 128], [1, INNER]])), sc_x1[par], 16)

                def cd_dma_x1(t):
                    par = t % 2
                    P.wait("sync", vv, x1_free[par])
                    d_x1[par] = P.op("sync", lambda E, t=t, par=par: E.dma_start(
                        out=x1t2[:, par, :], in_=x1_dram[t * 128:(t + 1) * 128, :]), sc_xr[par], 16)

                def cd_quant(t):
                    par = t % 2
                    P.wait("vector", sc_x1[par], d_gu[par])
                    P.wait("vector", vv, mgd)
                    am = P.op("vector", lambda E, t=t, par=par: E.reduce_max(
                        mpt[:, t, 0:1], gu_t[:, par, :], axis=AX.X, apply_absolute_value=True), vv)
                    gu_free_g[par] = 0
                    P.wait("vector", vv, am)
                    mc2 = P.op("vector", lambda E, t=t: E.tensor_scalar(
                        mpt[:, t, 1:2], mpt[:, t, 0:1], rsu[:, t:t + 1], 1e-5, OP.mult, OP.max), vv)
                    P.wait("vector", vv, mc2)
                    P.op("vector", lambda E, t=t: E.scalar_tensor_tensor(
                        rsd[:, t:t + 1], mpt[:, t, 1:2], 1.0 / 127.0, mwg_b[:, 2:3], OP.mult, OP.mult))
                    rc = P.op("vector", lambda E, t=t: E.reciprocal(mpt[:, t, 2:3], mpt[:, t, 1:2]), vv)
                    P.wait("vector", vv, rc)
                    v2d = P.op("vector", lambda E, t=t: E.scalar_tensor_tensor(
                        mpt[:, t, 2:3], mpt[:, t, 2:3], 127.0, rsu[:, t:t + 1], OP.mult, OP.mult), vv)
                    P.wait("scalar", vv, v2d)
                    P.wait("scalar", vv, max(comb_d))
                    qi32_d[par] = P.op("scalar", lambda E, t=t, par=par: E.activation(
                        xq_i32b[:], gu_t[:, par, :], AF.Identity, scale=mpt[:, t, 2:3]), ss)
                    gu_free_s[par] = qi32_d[par]
                    P.wait("vector", ss, qi32_d[par])
                    P.wait("vector", vv, max(comb_d))
                    dd1_d[par] = P.op("vector", lambda E, t=t, par=par: E.scalar_tensor_tensor(
                        d_bfb[:], gu_t[:, par, :], mpt[:, t, 2:3], xq_i32b[:],
                        OP.mult, OP.subtract), vv)
                    gu_free[par] = dd1_d[par]
                    P.wait("scalar", vv, dd1_d[par])
                    da = P.op("scalar", lambda E: E.activation(
                        d_bfb[:], d_bfb[:], AF.Identity, bias=magic_c[:], scale=1.0), ss)
                    P.wait("scalar", ss, da)
                    dd2_d[par] = P.op("scalar", lambda E: E.activation(
                        d_bfb[:], d_bfb[:], AF.Identity, bias=nmagic_c[:], scale=1.0), ss)
                    P.wait("vector", ss, dd2_d[par])
                    P.wait("vector", tt, tr_d.get((t - 2, 7), 0))
                    comb_d[par] = P.op("vector", lambda E, par=par: E.scalar_tensor_tensor(
                        xqgu_tm[:, par, :], d_bfb[:], 1.0, xq_i32b[:],
                        OP.mult, OP.add), vv)

                def cd_transp(t):
                    par = t % 2
                    P.wait("tensor", vv, comb_d[par])
                    P.wait("scalar", tt, mm_d[par])
                    for cg in range(8):
                        tpp = cg % 2
                        prev = copy_d[(t, cg - 2)] if cg >= 2 else copy_d.get((t - 1, 6 + tpp), 0)
                        P.wait("tensor", ss, prev)
                        for c4 in range(4):
                            c = cg * 4 + c4
                            tp = P.op("tensor", lambda E, c=c, c4=c4, tpp=tpp, par=par: E.transpose(
                                ps_t3[tpp][:, c4 * 128:(c4 + 1) * 128],
                                xqgu_tm[:, par, c * 128:(c + 1) * 128], id_bf[:]), tt)
                        tr_d[(t, cg)] = tp
                        P.wait("scalar", tt, tp)
                        copy_d[(t, cg)] = P.op("scalar", lambda E, cg=cg, tpp=tpp, par=par: E.activation(
                            xqgu_fm[:, par, cg * 4:(cg + 1) * 4, :].rearrange("p a b -> p (a b)"),
                            ps_t3[tpp][:], AF.Identity, scale=1.0), ss)

                def cd_mm(t):
                    par = t % 2
                    P.wait("tensor", ss, copy_d[(t, 7)])
                    P.wait("tensor", ss, ffo_ev[par])
                    for fh in range(2):
                        for kc in range(32):
                            mmd = P.op("tensor", lambda E, kc=kc, fh=fh, par=par: E.matmul(
                                ps_d[par][:, fh * 512:(fh + 1) * 512], xqgu_fm[:, par, kc, :],
                                wdq_sb[:, kc, fh * 512:(fh + 1) * 512],
                                start=(kc == 0), stop=(kc == 31)), tt)
                        P.wait("scalar", tt, mmd)
                        if fh == 0:
                            P.wait("scalar", vv, ffo_used[par])
                        ffo_ev[par] = P.op("scalar", lambda E, t=t, fh=fh, par=par: E.activation(
                            ffo[:, par, fh * 512:(fh + 1) * 512],
                            ps_d[par][:, fh * 512:(fh + 1) * 512],
                            AF.Identity, scale=rsd[:, t:t + 1]), ss)
                    mm_d[par] = mmd

                def cd_ln4(t):
                    par = t % 2
                    P.wait("vector", ss, ffo_ev[par])
                    P.op("vector", lambda E, par=par: E.bn_stats(st3[:, 0:6], ffo[:, par, 0:512]))
                    sb4 = P.op("vector", lambda E, par=par: E.bn_stats(st3[:, 6:12], ffo[:, par, 512:1024]), vv)
                    P.wait("vector", vv, sb4)
                    ag4 = P.op("vector", lambda E: E.bn_aggr(mv3[:], st3[:]), vv)
                    P.wait("scalar", vv, ag4)
                    lnx4 = P.op("scalar", lambda E: E.activation(rs3[:], mv3[:, 1:2], AF.Ln, bias=eps_sb[:], scale=1.0), ss)
                    P.wait("scalar", ss, lnx4)
                    rr4 = P.op("scalar", lambda E: E.activation(rs3[:], rs3[:], AF.Exp, scale=-0.5), ss)
                    P.wait("vector", ss, rr4)
                    P.wait("vector", sc_out[par], out_st[par])
                    ow = P.op("vector", lambda E, par=par: E.tensor_scalar(
                        outb[:, par, :], ffo[:, par, :], mv3[:, 0:1], rs3[:], OP.subtract, OP.mult), vv)
                    ffo_used[par] = ow
                    P.wait("vector", vv, ow)
                    P.wait("vector", sc_xr[par], d_x1[par])
                    ox = P.op("vector", lambda E, par=par: E.tensor_tensor(
                        outb[:, par, :], outb[:, par, :], x1t2[:, par, :], OP.add), vv)
                    x1_free[par] = ox
                    P.wait("sync", vv, ox)
                    out_st[par] = P.op("sync", lambda E, t=t, par=par: E.dma_start(
                        out=out[t * 128:(t + 1) * 128, :], in_=outb[:, par, :]), sc_out[par], 16)

                cd_dma_gu(0)
                if NT > 1:
                    cd_dma_gu(1)
                cd_dma_x1(0)
                cd_quant(0)
                cd_transp(0)
                for i in range(NT):
                    if i + 2 < NT:
                        cd_dma_gu(i + 2)
                    if i + 1 < NT:
                        cd_dma_x1(i + 1)
                    if i + 1 < NT:
                        cd_quant(i + 1)
                    cd_mm(i)
                    if i + 1 < NT:
                        cd_transp(i + 1)
                    cd_ln4(i)

                P.wait("sync", sc_out[0], out_st[0])
                P.wait("sync", sc_out[1], out_st[1])
                P.wait("gpsimd", sc_out[0], out_st[0])
                P.wait("gpsimd", sc_out[1], out_st[1])

        with nc.Block() as block:
            @block.tensor
            def _(E):
                for f in P.ev["tensor"]:
                    f(E)

            @block.vector
            def _(E):
                for f in P.ev["vector"]:
                    f(E)

            @block.scalar
            def _(E):
                for f in P.ev["scalar"]:
                    f(E)

            @block.gpsimd
            def _(E):
                for f in P.ev["gpsimd"]:
                    f(E)

            @block.sync
            def _(E):
                for f in P.ev["sync"]:
                    f(E)

    return nc, dbg


_CACHE = {}


def _host_inputs(L0_attn_W_w, L0_attn_W_b, L0_attn_proj_w, L0_attn_proj_b,
                 L0_attn_ln_g, L0_attn_ln_b, L0_attn_rpb,
                 L0_ffn_w_gate, L0_ffn_w_up, L0_ffn_w_down):
    f = np.asarray
    Ww = f(L0_attn_W_w, dtype=np.float32)
    Wb = f(L0_attn_W_b, dtype=np.float32)
    rpb = f(L0_attn_rpb, dtype=np.float32)
    shared = {
        "wqkvT": np.ascontiguousarray(Ww.T),
        "wb_pc": np.ascontiguousarray(Wb.reshape(24, 128).T),
        "wbv_row": np.ascontiguousarray(Wb[2048:3072].reshape(1, 1024)),
        "wprojT": np.ascontiguousarray(f(L0_attn_proj_w, dtype=np.float32).T),
        "projb_row": np.ascontiguousarray(f(L0_attn_proj_b, dtype=np.float32).reshape(1, 1024)),
        "lng_row": np.ascontiguousarray(f(L0_attn_ln_g, dtype=np.float32).reshape(1, 1024)),
        "lnb_row": np.ascontiguousarray(f(L0_attn_ln_b, dtype=np.float32).reshape(1, 1024)),
        "rpbmt": np.ascontiguousarray(rpb[::-1, :].T).reshape(-1),
        "wgT": np.ascontiguousarray(f(L0_ffn_w_gate, dtype=np.float32).T),
        "wuT": np.ascontiguousarray(f(L0_ffn_w_up, dtype=np.float32).T),
        "wdT": np.ascontiguousarray(f(L0_ffn_w_down, dtype=np.float32).T),
        "id_in": np.eye(128, dtype=np.float32),
        "j_in": np.eye(128, dtype=np.float32)[::-1].copy(),
    }
    m = np.arange(128)[:, None] + np.arange(128)[None, :]
    shared["mask_in"] = np.where(m <= 126, np.float32(NEG), np.float32(0.0))
    return shared


def kernel(L0_attn_x, L0_attn_W_w, L0_attn_W_b, L0_attn_proj_w, L0_attn_proj_b,
           L0_attn_ln_g, L0_attn_ln_b, L0_attn_rpb, L0_ffn_w_gate, L0_ffn_w_up,
           L0_ffn_w_down):
    x = np.asarray(L0_attn_x, dtype=np.float32)
    B = x.shape[0]
    if "nc" not in _CACHE:
        _CACHE["nc"] = build(S=2048, b_dbg=False)[0]
    nc = _CACHE["nc"]
    shared = _host_inputs(L0_attn_W_w, L0_attn_W_b, L0_attn_proj_w, L0_attn_proj_b,
                          L0_attn_ln_g, L0_attn_ln_b, L0_attn_rpb,
                          L0_ffn_w_gate, L0_ffn_w_up, L0_ffn_w_down)
    in_maps = []
    for b in range(B):
        m = dict(shared)
        m["x"] = np.ascontiguousarray(x[b])
        in_maps.append(m)
    res = run_bass_kernel_spmd(nc, in_maps, core_ids=list(range(B)))
    return np.stack([res.results[b]["out"] for b in range(B)], axis=0)


# version 21
# speedup vs baseline: 1.0154x; 1.0031x over previous
import numpy as np
from contextlib import ExitStack

import concourse.bass as bass
import concourse.mybir as mybir
from concourse.ap import AP
from concourse.bass_utils import run_bass_kernel_spmd
from concourse.alu_op_type import AluOpType
from concourse import library_config

F32 = mybir.dt.float32
BF16 = mybir.dt.bfloat16
I32 = mybir.dt.int32
AF = mybir.ActivationFunctionType
AX = mybir.AxisListType
OP = AluOpType

D = 1024
H = 16
INNER = 4096
NEG = -1.0e30
EXP_OFF = 40.0


class Sem:
    __slots__ = ("h", "n")
    def __init__(self, h):
        self.h = h
        self.n = 0


class RotSem:
    CAP = 1600

    def __init__(self, handles):
        self.hs = handles
        self.n = 0
        self.seg = 0
        self.base = 0

    def loc(self, v):
        if v == 0:
            return self.hs[0], 0
        s = (v - 1) // self.CAP
        return self.hs[s % len(self.hs)], v - s * self.CAP


class Prog:
    ENGS = ("tensor", "vector", "scalar", "gpsimd", "sync")

    def __init__(self, nc):
        self.nc = nc
        self.ev = {k: [] for k in self.ENGS}

    def wait(self, eng, sem, val):
        if not val:
            return
        if isinstance(sem, RotSem):
            h, v = sem.loc(int(val))
            self.ev[eng].append(lambda e, s=h, vv_=v: e.wait_ge(s, vv_))
        else:
            self.ev[eng].append(lambda e, s=sem.h, v=int(val): e.wait_ge(s, v))

    def op(self, eng, make, inc=None, k=1):
        if inc is not None:
            if isinstance(inc, RotSem):
                inc.n += k
                h, _ = inc.loc(inc.n)
                self.ev[eng].append(lambda e, m=make, s=h, kk=k: m(e).then_inc(s, kk))
                return inc.n
            inc.n += k
            self.ev[eng].append(lambda e, m=make, s=inc.h, kk=k: m(e).then_inc(s, kk))
            return inc.n
        self.ev[eng].append(make)
        return None

    def barrier(self, sems):
        snap = [(s, s.n) for s in sems if s.n > 0]
        for eng in self.ENGS:
            for s, n in snap:
                self.wait(eng, s, n)


def build(S=2048, b_dbg=False, pad_instr=0):
    NT = S // 128
    NP = NT // 2
    QT = max(1, NT // 8)
    R = 2 * S - 1
    WEXT = 384 + S
    nc = bass.Bass("TRN2", target_bir_lowering=False)

    x_in = nc.dram_tensor("x", [S, D], F32, kind="ExternalInput").ap()
    wqkvT = nc.dram_tensor("wqkvT", [D, 3 * D], F32, kind="ExternalInput").ap()
    wb_pc = nc.dram_tensor("wb_pc", [128, 24], F32, kind="ExternalInput").ap()
    wbv_row = nc.dram_tensor("wbv_row", [1, D], F32, kind="ExternalInput").ap()
    wprojT = nc.dram_tensor("wprojT", [D, D], F32, kind="ExternalInput").ap()
    projb_row = nc.dram_tensor("projb_row", [1, D], F32, kind="ExternalInput").ap()
    lng_row = nc.dram_tensor("lng_row", [1, D], F32, kind="ExternalInput").ap()
    lnb_row = nc.dram_tensor("lnb_row", [1, D], F32, kind="ExternalInput").ap()
    rpbmt = nc.dram_tensor("rpbmt", [H * R], F32, kind="ExternalInput").ap()
    wgT = nc.dram_tensor("wgT", [D, INNER], F32, kind="ExternalInput").ap()
    wuT = nc.dram_tensor("wuT", [D, INNER], F32, kind="ExternalInput").ap()
    wdT = nc.dram_tensor("wdT", [INNER, D], F32, kind="ExternalInput").ap()
    id_in = nc.dram_tensor("id_in", [128, 128], F32, kind="ExternalInput").ap()
    j_in = nc.dram_tensor("j_in", [128, 128], F32, kind="ExternalInput").ap()
    mask_in = nc.dram_tensor("mask_in", [128, 128], F32, kind="ExternalInput").ap()

    out = nc.dram_tensor("out", [S, D], F32, kind="ExternalOutput").ap()

    x1_dram = nc.dram_tensor("x1_scratch", [S, D], F32, kind="Internal").ap()
    gu_dram = nc.dram_tensor("gu_scratch", [S, INNER], F32, kind="Internal").ap()

    dbg = {}
    if b_dbg:
        for nm, shp, dt in (("dbg_q", [128, 8 * S], BF16), ("dbg_k", [128, 8 * S], BF16),
                            ("dbg_v", [128, NT * H * 65], BF16), ("dbg_o", [128, NT * D], BF16),
                            ("dbg_x1", [S, D], F32),
                            ("dbg_xn", [128, 8 * S], BF16),
                            ("dbg_wm", [1, 8], F32)):
            dbg[nm] = nc.dram_tensor(nm, shp, dt, kind="ExternalOutput").ap()

    st = ExitStack()
    with st:
        e = st.enter_context

        tt = RotSem([e(nc.semaphore(f"sem_t{i}")) for i in range(8)])
        vv = Sem(e(nc.semaphore("sem_v")))
        ss = Sem(e(nc.semaphore("sem_s")))
        gg = Sem(e(nc.semaphore("sem_g")))
        def _sem(nm):
            return Sem(e(nc.semaphore(nm)))
        sc_const = _sem("sc_const")
        sc_x = _sem("sc_x")
        sc_p1 = _sem("sc_p1")
        sc_p2l = _sem("sc_p2l")
        sc_p2s = _sem("sc_p2s")
        sc_xr = [_sem("sc_xr0"), _sem("sc_xr1")]
        sc_x1 = [_sem("sc_x10"), _sem("sc_x11")]
        sc_x1q = _sem("sc_x1q")
        sc_out = [_sem("sc_out0"), _sem("sc_out1")]
        gc_const = _sem("gc_const")
        gc_wq = [_sem("gc_wq0"), _sem("gc_wq1")]
        gc_wv = _sem("gc_wv")
        gc_we = [_sem("gc_we0"), _sem("gc_we1")]
        gc_wp = _sem("gc_wp")
        gc_gu = [_sem("gc_gu0"), _sem("gc_gu1")]
        gc_wd = [_sem("gc_wd0"), _sem("gc_wd1")]
        gc_dbg = _sem("gc_dbg")
        ALLSEMS = [tt, vv, ss, gg, sc_const, sc_x, sc_p1, sc_p2l, sc_p2s,
                   sc_xr[0], sc_xr[1], sc_x1[0], sc_x1[1], sc_x1q,
                   sc_out[0], sc_out[1], gc_const, gc_wq[0], gc_wq[1], gc_wv,
                   gc_we[0], gc_we[1], gc_wp, gc_gu[0], gc_gu[1],
                   gc_wd[0], gc_wd[1], gc_dbg]

        P = Prog(nc)

        ones_bf = e(nc.sbuf_tensor("ones_bf", [128, 128], BF16))
        ones_f32 = e(nc.sbuf_tensor("ones_f32", [1, 128], F32))
        onec_f32 = e(nc.sbuf_tensor("onec_f32", [128, 1], F32))
        id_bf = e(nc.sbuf_tensor("id_bf", [128, 128], BF16))
        j_bf = e(nc.sbuf_tensor("j_bf", [128, 128], BF16))
        mask_sb = e(nc.sbuf_tensor("mask_sb", [128, 128], F32))
        eps_sb = e(nc.sbuf_tensor("eps_sb", [128, 1], F32))
        wb_sb = e(nc.sbuf_tensor("wb_sb", [128, 24], F32))
        wbq_sb = e(nc.sbuf_tensor("wbq_sb", [128, 8], F32))
        wbv_bf = e(nc.sbuf_tensor("wbv_bf", [1, D], BF16))
        negmh = e(nc.sbuf_tensor("negmh", [128, 1], F32))
        wstat = e(nc.sbuf_tensor("wstat", [1, 8], F32))
        swg_b = e(nc.sbuf_tensor("swg_b", [128, 3], F32))
        mwg_b = e(nc.sbuf_tensor("mwg_b", [128, 3], F32))
        stO = ExitStack()
        stQKV = ExitStack()
        o_sb = stO.enter_context(nc.sbuf_tensor("o_sb", [128, NT, D], BF16))
        q_sb = stQKV.enter_context(nc.sbuf_tensor("q_sb", [128, 8, S], BF16))
        k_sb = stQKV.enter_context(nc.sbuf_tensor("k_sb", [128, 8, S], BF16))
        v_sb = stQKV.enter_context(nc.sbuf_tensor("v_sb", [128, NT, H * 65], BF16))

        P.op("sync", lambda E: E.dma_start(out=wb_sb[:], in_=wb_pc), sc_const, 16)
        a0_ds = P.op("sync", lambda E: E.dma_start(out=mask_sb[:], in_=mask_in), sc_const, 16)
        P.op("gpsimd", lambda E: E.dma_start(out=id_bf[:], in_=id_in), gc_const, 16)
        P.op("gpsimd", lambda E: E.dma_start(out=j_bf[:], in_=j_in), gc_const, 16)
        a0_dg = P.op("gpsimd", lambda E: E.dma_start(out=wbv_bf[:], in_=wbv_row), gc_const, 16)
        for _pi in range(pad_instr):
            P.op("vector", lambda E: E.memset(eps_sb[:], 1e-5))
        if pad_instr:
            with nc.psum_tensor("ps_pad", [128, 128], F32) as ps_pad:
                P.wait("tensor", gc_const, a0_dg)
                for _pi in range(pad_instr):
                    P.op("tensor", lambda E: E.matmul(ps_pad[:], id_bf[:], id_bf[:],
                                                      start=True, stop=True), tt)
        P.op("vector", lambda E: E.memset(eps_sb[:], 1e-5))
        P.op("vector", lambda E: E.memset(wstat[:], 0.0))
        P.op("vector", lambda E: E.memset(negmh[:], 0.0))
        P.op("vector", lambda E: E.memset(ones_bf[:], 1.0))
        P.op("vector", lambda E: E.memset(ones_f32[:], 1.0))
        P.op("vector", lambda E: E.memset(onec_f32[:], 1.0))
        P.op("vector", lambda E: E.memset(
            AP(v_sb[:].tensor, 64, [[NT * H * 65, 128], [H * 65, NT], [65, H]]), 1.0), vv)
        P.wait("vector", sc_const, a0_ds)
        v_wbq = P.op("vector", lambda E: E.tensor_scalar(
            wbq_sb[:], wb_sb[:, 0:8], 1.0 / 32.0, None, OP.mult), vv)

        with ExitStack() as stA:
            eA = stA.enter_context
            TBA = 2
            x_sb = eA(nc.sbuf_tensor("x_sb", [128, 2, TBA, D], F32))
            stats = eA(nc.sbuf_tensor("stats", [128, TBA, 12], F32))
            mv = eA(nc.sbuf_tensor("mv", [128, TBA, 2], F32))
            lnv = eA(nc.sbuf_tensor("lnv", [128, TBA], F32))
            rstd = eA(nc.sbuf_tensor("rstd", [128, TBA], F32))
            xn_bf = eA(nc.sbuf_tensor("xn_bf", [128, TBA, D], BF16))
            xn_fm = eA(nc.sbuf_tensor("xn_fm", [128, 8, S], BF16))
            wqkv_o = eA(nc.sbuf_tensor("wqkv_o", [128, 2, 8, 128], BF16))
            wv_bf = eA(nc.sbuf_tensor("wv_bf", [128, 8, D], BF16))
            ps_tr = [eA(nc.psum_tensor(f"ps_tr{i}", [128, 512], BF16)) for i in range(2)]
            ps_qk = [eA(nc.psum_tensor(f"ps_qk{i}", [128, 512], F32)) for i in range(2)]
            ps_v = [eA(nc.psum_tensor(f"ps_v{i}", [128, D], F32)) for i in range(2)]

            P.wait("tensor", gc_const, a0_dg)

            NTB = NT // TBA
            tr_done = [0, 0]
            x_free = [0, 0]
            d_xa = [0, 0]
            tb_done = [0] * NTB
            for tb in range(NTB + 1):
                if tb < NTB:
                    tbp = tb % 2
                    P.wait("sync", vv, x_free[tbp])
                    d_xa[tbp] = P.op("sync", lambda E, tb=tb, tbp=tbp: E.dma_start(
                        out=x_sb[:, tbp, :, :],
                        in_=AP(x_in.tensor, tb * TBA * 128 * D, [[D, 128], [128 * D, TBA], [1, D]])), sc_x, 16)
                if tb == 0:
                    continue
                tb_ = tb - 1
                tbp = tb_ % 2
                P.wait("vector", sc_x, d_xa[tbp])
                for t in range(TBA):
                    P.op("vector", lambda E, t=t, tbp=tbp: E.bn_stats(stats[:, t, 0:6], x_sb[:, tbp, t, 0:512]))
                    last = P.op("vector", lambda E, t=t, tbp=tbp: E.bn_stats(stats[:, t, 6:12], x_sb[:, tbp, t, 512:1024]), vv)
                P.wait("vector", vv, last)
                for t in range(TBA):
                    agg = P.op("vector", lambda E, t=t: E.bn_aggr(mv[:, t, :], stats[:, t, :]), vv)
                P.wait("scalar", vv, agg)
                ln1 = P.op("scalar", lambda E: E.activation(lnv[:], mv[:, :, 1], AF.Ln, bias=eps_sb[:], scale=1.0), ss)
                P.wait("scalar", ss, ln1)
                rs = P.op("scalar", lambda E: E.activation(rstd[:], lnv[:], AF.Exp, scale=-0.5), ss)
                P.wait("vector", ss, rs)
                for t in range(TBA):
                    x_free[tbp] = P.op("vector", lambda E, t=t, tbp=tbp: E.tensor_scalar(
                        xn_bf[:, t, :], x_sb[:, tbp, t, :], mv[:, t, 0:1], rstd[:, t:t + 1],
                        OP.subtract, OP.mult), vv)
                for c in range(8):
                    par = c % 2
                    P.wait("tensor", vv, x_free[tbp])
                    P.wait("tensor", vv, tr_done[par])
                    for t in range(TBA):
                        tp = P.op("tensor", lambda E, c=c, t=t, par=par, tbp=tbp: E.transpose(
                            ps_tr[par][:, t * 128:(t + 1) * 128],
                            xn_bf[:, t, c * 128:(c + 1) * 128], id_bf[:]), tt)
                    P.wait("vector", tt, tp)
                    tr_done[par] = P.op("vector", lambda E, c=c, tb_=tb_, par=par: E.tensor_copy(
                        xn_fm[:, c, tb_ * TBA * 128:(tb_ + 1) * TBA * 128], ps_tr[par][:, 0:TBA * 128]), vv)
                tb_done[tb_] = max(tr_done)

            if b_dbg:
                P.wait("gpsimd", vv, max(tr_done))
                P.wait("gpsimd", gc_dbg, gc_dbg.n)
                _dmp = P.op("gpsimd", lambda E: E.dma_start(out=dbg["dbg_xn"], in_=xn_fm[:].rearrange("p a b -> p (a b)")), gc_dbg, 16)
                for _e in ("tensor", "vector", "scalar", "sync"):
                    P.wait(_e, gc_dbg, gc_dbg.n)
            P.wait("scalar", sc_const, a0_ds)
            P.wait("scalar", vv, v_wbq)
            w_free = [0, 0]
            qk_ev = [0, 0]
            NF = S // 512
            for o in range(16):
                par = o % 2
                P.wait("gpsimd", tt, w_free[par])
                d_w = P.op("gpsimd", lambda E, o=o, par=par: E.dma_start(
                    out=wqkv_o[:, par, :, :],
                    in_=AP(wqkvT.tensor, o * 128, [[3 * D, 128], [128 * 3 * D, 8], [1, 128]])), gc_wq[par], 16)
                P.wait("tensor", gc_wq[par], d_w)
                for f in range(NF):
                    fpar = (o * NF + f) % 2
                    if o == 0:
                        P.wait("tensor", vv, tb_done[min(NTB - 1, 2 * f + 1)])
                    P.wait("tensor", ss, qk_ev[fpar])
                    for kc in range(8):
                        mm = P.op("tensor", lambda E, par=par, kc=kc, f=f, fpar=fpar: E.matmul(
                            ps_qk[fpar][:], wqkv_o[:, par, kc, :],
                            xn_fm[:, kc, f * 512:(f + 1) * 512],
                            start=(kc == 0), stop=(kc == 7)), tt)
                    if f == NF - 1:
                        w_free[par] = mm
                    P.wait("scalar", tt, mm)
                    if o < 8:
                        qk_ev[fpar] = P.op("scalar", lambda E, o=o, f=f, fpar=fpar: E.activation(
                            q_sb[:, o, f * 512:(f + 1) * 512], ps_qk[fpar][:], AF.Identity,
                            bias=wbq_sb[:, o:o + 1], scale=1.0 / 32.0), ss)
                    else:
                        qk_ev[fpar] = P.op("scalar", lambda E, o=o, f=f, fpar=fpar: E.activation(
                            k_sb[:, o - 8, f * 512:(f + 1) * 512], ps_qk[fpar][:], AF.Identity,
                            bias=wb_sb[:, o:o + 1], scale=1.0), ss)

            P.wait("gpsimd", tt, max(w_free))
            d_wv = P.op("gpsimd", lambda E: E.dma_start(
                out=wv_bf[:],
                in_=AP(wqkvT.tensor, 2048, [[3 * D, 128], [128 * 3 * D, 8], [1, D]])), gc_wv, 16)
            P.wait("tensor", gc_wv, d_wv)
            v_ev = [0, 0]
            for t in range(NT):
                par = t % 2
                P.wait("tensor", vv, v_ev[par])
                for fh in range(2):
                    P.op("tensor", lambda E, fh=fh, par=par: E.matmul(
                        ps_v[par][:, fh * 512:(fh + 1) * 512], ones_bf[0:1, :],
                        wbv_bf[:, fh * 512:(fh + 1) * 512], start=True, stop=False))
                for kc in range(8):
                    for fh in range(2):
                        mm = P.op("tensor", lambda E, t=t, kc=kc, fh=fh, par=par: E.matmul(
                            ps_v[par][:, fh * 512:(fh + 1) * 512],
                            xn_fm[:, kc, t * 128:(t + 1) * 128],
                            wv_bf[:, kc, fh * 512:(fh + 1) * 512],
                            start=False, stop=(kc == 7)), tt)
                P.wait("vector", tt, mm)
                v_ev[par] = P.op("vector", lambda E, t=t, par=par: E.tensor_copy(
                    AP(v_sb[:].tensor, t * (H * 65), [[NT * H * 65, 128], [65, H], [1, 64]]),
                    AP(ps_v[par][:].tensor, 0, [[D, 128], [64, H], [1, 64]])), vv)

        P.barrier(ALLSEMS)
        P.barrier(ALLSEMS)
        if b_dbg:
            P.wait("gpsimd", gc_dbg, gc_dbg.n)
            _dmp = P.op("gpsimd", lambda E: E.dma_start(out=dbg["dbg_q"], in_=q_sb[:].rearrange("p a b -> p (a b)")), gc_dbg, 16)
            for _e in ("tensor", "vector", "scalar", "sync"):
                P.wait(_e, gc_dbg, gc_dbg.n)
            P.wait("gpsimd", gc_dbg, gc_dbg.n)
            _dmp = P.op("gpsimd", lambda E: E.dma_start(out=dbg["dbg_k"], in_=k_sb[:].rearrange("p a b -> p (a b)")), gc_dbg, 16)
            for _e in ("tensor", "vector", "scalar", "sync"):
                P.wait(_e, gc_dbg, gc_dbg.n)
            P.wait("gpsimd", gc_dbg, gc_dbg.n)
            _dmp = P.op("gpsimd", lambda E: E.dma_start(out=dbg["dbg_v"], in_=v_sb[:].rearrange("p a b -> p (a b)")), gc_dbg, 16)
            for _e in ("tensor", "vector", "scalar", "sync"):
                P.wait(_e, gc_dbg, gc_dbg.n)
        with ExitStack() as stB:
            eB = stB.enter_context
            wext = eB(nc.sbuf_tensor("wext", [128, 2, WEXT], BF16))
            e_sb = eB(nc.sbuf_tensor("e_sb", [128, 2, WEXT], BF16))
            p_raw = eB(nc.sbuf_tensor("p_raw", [128, 4, 1024], F32))
            p_sb = eB(nc.sbuf_tensor("p_sb", [128, 4, 1024], BF16))
            recl = eB(nc.sbuf_tensor("recl", [128, 2, 4], F32))
            wchunk = eB(nc.sbuf_tensor("wchunk", [128, 2, 1024], F32))
            wabs = eB(nc.sbuf_tensor("wabs", [128, 2, 1024], F32))
            wsum_sb = eB(nc.sbuf_tensor("wsum_sb", [1, 512], F32))
            ps_st = [eB(nc.psum_tensor(f"ps_st{i}", [128, 1024], F32)) for i in range(2)]
            ps_o = [eB(nc.psum_tensor(f"ps_o{i}", [128, 260], F32)) for i in range(2)]
            ps_ws = eB(nc.psum_tensor("ps_ws", [1, 512], F32))

            P.wait("scalar", vv, v_wbq)

            w_srcs = [wgT, wuT, wdT]
            p1_units = [(wi, cc, hf) for wi in range(3) for cc in range(8) for hf in range(4)]
            p1 = {"dma": [0, 0], "abs": [0, 0], "mm": 0, "mm2": [0, 0], "ev": 0, "wsum_ev": 0}
            p1sems = [sc_p1, sc_x]

            def p1_issue_dma(u):
                wi, cc, hf = p1_units[u]
                wsrc = w_srcs[wi]
                upar = u % 2
                P.wait("sync", vv, p1["abs"][upar])

                def mk(E, wsrc=wsrc, cc=cc, hf=hf, upar=upar):
                    if wsrc is wdT:
                        src = AP(wsrc.tensor, (cc * 512 + hf * 128) * D, [[D, 128], [1, D]])
                        return E.dma_start(out=wchunk[:, upar, :], in_=src)
                    src = AP(wsrc.tensor, cc * 128 * INNER + hf * 1024, [[INNER, 128], [1, 1024]])
                    return E.dma_start(out=wchunk[:, upar, :], in_=src)
                p1["dma"][upar] = P.op("sync", mk, p1sems[upar], 16)

            def p1_abs(u):
                upar = u % 2
                P.wait("vector", p1sems[upar], p1["dma"][upar])
                P.wait("vector", tt, p1["mm2"][upar])
                ab = P.op("vector", lambda E, upar=upar: E.scalar_tensor_tensor(
                    wabs[:, upar, :], wchunk[:, upar, :], -1.0, wchunk[:, upar, :],
                    OP.mult, OP.max), vv)
                p1["abs"][upar] = ab

            def p1_unit_compute(u):
                wi = u // 32
                upar = u % 2
                P.wait("tensor", vv, p1["abs"][upar])
                if u % 32 == 0:
                    P.wait("tensor", vv, p1["wsum_ev"])
                for f in range(2):
                    mm = P.op("tensor", lambda E, f=f, u=u, upar=upar: E.matmul(
                        ps_ws[:], onec_f32[:], wabs[:, upar, f * 512:(f + 1) * 512],
                        start=(u % 32 == 0 and f == 0), stop=(u % 32 == 31 and f == 1)), tt)
                p1["mm"] = mm
                p1["mm2"][upar] = mm
                if u % 32 == 31:
                    P.wait("vector", tt, p1["mm"])
                    ev = P.op("vector", lambda E: E.tensor_copy(wsum_sb[:], ps_ws[:]), vv)
                    p1["wsum_ev"] = ev
                    P.wait("vector", vv, ev)
                    wdt2 = 256
                    while wdt2 >= 1:
                        sm = P.op("vector", lambda E, w2=wdt2: E.tensor_tensor(
                            wsum_sb[:, 0:w2], wsum_sb[:, 0:w2], wsum_sb[:, w2:2 * w2], OP.add), vv)
                        P.wait("vector", vv, sm)
                        wdt2 //= 2
                    sm = P.op("vector", lambda E, wi=wi: E.tensor_copy(
                        wstat[:, wi:wi + 1], wsum_sb[:, 0:1]), vv)
                    P.wait("vector", vv, sm)
                    mc = P.op("vector", lambda E, wi=wi: E.tensor_scalar(
                        wstat[:, wi:wi + 1], wstat[:, wi:wi + 1], 1.0 / float(D * INNER), 1e-5,
                        OP.mult, OP.max), vv)
                    P.wait("vector", vv, mc)
                    p1["ev"] = P.op("vector", lambda E, wi=wi: E.reciprocal(
                        wstat[:, 3 + wi:4 + wi], wstat[:, wi:wi + 1]), vv)

            p1_prog = {"mm": 0, "abs": 0, "issued": 0, "quads": 0}
            TOTQ = H * (NP // 2)

            def p1_tick():
                p1_prog["quads"] += 1
                base = p1_prog["quads"] * 96 // TOTQ
                while p1_prog["mm"] < base:
                    while p1_prog["abs"] <= p1_prog["mm"] + 1 and p1_prog["abs"] < 96:
                        p1_abs(p1_prog["abs"])
                        p1_prog["abs"] += 1
                        if p1_prog["issued"] < 96:
                            p1_issue_dma(p1_prog["issued"])
                            p1_prog["issued"] += 1
                    p1_unit_compute(p1_prog["mm"])
                    p1_prog["mm"] += 1

            p1_issue_dma(0)
            p1_issue_dma(1)
            p1_prog["issued"] = 2

            wx_free = [0, 0]
            wx_ready = [0, 0]
            st_free = [0, 0]
            pv_done = [0, 0, 0, 0]
            pmul = [0, 0, 0, 0]
            e_used = [0, 0]
            e_ready = [0, 0]
            fifo = []
            o_ev = [0, 0]
            gpar = [0]
            b_ss, b_vv = ss.n, vv.n

            def emit_head(h):
                par = h % 2
                P.wait("gpsimd", tt, wx_free[par])
                P.wait("gpsimd", vv, wx_ready[par])
                d_w = P.op("gpsimd", lambda E, h=h, par=par: E.dma_start(
                    out=wext[:, par, :],
                    in_=AP(rpbmt.tensor, h * R + (S - 512), [[1, 128], [1, WEXT]])), gc_we[par], 16)
                P.wait("vector", gc_we[par], d_w)
                P.wait("vector", tt, wx_free[par])
                P.op("vector", lambda E, par=par: E.memset(wext[:, par, 0:384], NEG))
                wx_ready[par] = P.op("vector", lambda E, par=par: E.tensor_tensor(
                    wext[:, par, 384:512], wext[:, par, 384:512], mask_sb[:], OP.add), vv)
                nE = (WEXT + 511) // 512
                for ei in range(nE):
                    w0 = ei * 512
                    w1 = min(WEXT, w0 + 512)
                    mypar = gpar[0] % 2
                    gpar[0] += 1
                    P.wait("tensor", ss, st_free[mypar])
                    if ei == 0:
                        P.wait("tensor", vv, wx_ready[par])
                    jm = P.op("tensor", lambda E, w0=w0, w1=w1, par=par, mypar=mypar: E.matmul(
                        ps_st[mypar][:, 0:w1 - w0], j_bf[:], wext[:, par, w0:w1],
                        start=True, stop=True), tt)
                    if ei == nE - 1:
                        wx_free[par] = jm
                    P.wait("scalar", tt, jm)
                    if ei == 0:
                        P.wait("scalar", vv, e_used[par])
                    st_free[mypar] = P.op("scalar", lambda E, w0=w0, w1=w1, par=par,
                                          mypar=mypar: E.activation(
                        e_sb[:, par, w0:w1], ps_st[mypar][:, 0:w1 - w0], AF.Exp,
                        bias=negmh[:], scale=1.0), ss)
                e_ready[par] = st_free[(gpar[0] - 1) % 2]

                first_use = True
                hh, hc = 64 * (h % 2), h // 2
                for qd in range(NP // 2):
                    nblocks = 4 * qd + 4
                    groups = [[b0, b0 + 1] for b0 in range(0, nblocks, 2)]
                    qdpar = qd % 2
                    pv_first = True
                    for grp in groups:
                        m2 = gpar[0] % 2
                        m3 = gpar[0] % 4
                        gpar[0] += 1
                        if first_use:
                            P.wait("tensor", ss, b_ss)
                            P.wait("tensor", vv, b_vv)
                            first_use = False
                        P.wait("tensor", ss, st_free[m2])
                        for bi, kc in enumerate(grp):
                            qm = P.op("tensor", lambda E, kc=kc, bi=bi, m2=m2, qd=qd,
                                      hh=hh, hc=hc: E.matmul(
                                ps_st[m2][:, bi * 512:(bi + 1) * 512],
                                k_sb[hh:hh + 64, hc, kc * 128:(kc + 1) * 128],
                                q_sb[hh:hh + 64, hc, qd * 512:(qd + 1) * 512],
                                start=True, stop=True), tt)
                        P.wait("scalar", tt, qm)
                        P.wait("scalar", vv, pmul[m3])
                        st_free[m2] = P.op("scalar", lambda E, m2=m2, m3=m3: E.activation(
                            p_raw[:, m3, :], ps_st[m2][:], AF.Exp,
                            bias=negmh[:], scale=1.0), ss)
                        while len(fifo) >= 3:
                            fifo.pop(0)()
                        P.wait("vector", ss, st_free[m2])
                        P.wait("vector", ss, e_ready[par])
                        P.wait("vector", tt, pv_done[m3])
                        for bi, kc in enumerate(grp):
                            off = 384 + (4 * qd - kc) * 128
                            pmul[m3] = P.op("vector", lambda E, off=off, bi=bi, par=par,
                                            m3=m3: E.tensor_tensor(
                                p_sb[:, m3, bi * 512:(bi + 1) * 512],
                                p_raw[:, m3, bi * 512:(bi + 1) * 512],
                                e_sb[:, par, off:off + 512], OP.mult), vv)
                        e_used[par] = pmul[m3]

                        def mk_pv(grp=grp, m3=m3, qdpar=qdpar, h=h, stt=pv_first,
                                  nblocks=nblocks, pm_val=pmul[m3]):
                            P.wait("tensor", vv, pm_val)
                            if stt:
                                P.wait("tensor", ss, o_ev[qdpar])
                            pm = 0
                            for bi, kc in enumerate(grp):
                                for qs in range(4):
                                    pm = P.op("tensor", lambda E, kc=kc, bi=bi, qs=qs, h=h,
                                              m3=m3, qdpar=qdpar,
                                              stt=(stt and bi == 0 and qs == 0),
                                              spp=(kc == nblocks - 1 and qs == 3): E.matmul(
                                        ps_o[qdpar][:, qs * 65:qs * 65 + 65],
                                        p_sb[:, m3, bi * 512 + qs * 128:bi * 512 + qs * 128 + 128],
                                        v_sb[:, kc, h * 65:h * 65 + 65],
                                        start=stt, stop=spp), tt)
                            pv_done[m3] = pm

                        fifo.append(mk_pv)
                        pv_first = False

                    def mk_evac(qd=qd, qdpar=qdpar, h=h, m3_last=(gpar[0] - 1) % 4):
                        P.wait("vector", tt, pv_done[m3_last])
                        P.wait("vector", ss, o_ev[qdpar])
                        rr = P.op("vector", lambda E, qdpar=qdpar: E.reciprocal(
                            recl[:, qdpar, :],
                            AP(ps_o[qdpar][:].tensor, 64, [[260, 128], [65, 4]])), vv)
                        P.wait("scalar", vv, rr)
                        for qs in range(4):
                            o_ev[qdpar] = P.op("scalar", lambda E, qs=qs, qd=qd, h=h,
                                               qdpar=qdpar: E.activation(
                                o_sb[:, 4 * qd + qs, h * 64:h * 64 + 64],
                                ps_o[qdpar][:, qs * 65:qs * 65 + 64],
                                AF.Identity, scale=recl[:, qdpar, qs:qs + 1]), ss)

                    fifo.append(mk_evac)
                    p1_tick()

            for h in range(H):
                emit_head(h)
            while fifo:
                fifo.pop(0)()

            b_end_ss = ss.n
            if b_dbg:
                P.wait("gpsimd", ss, b_end_ss)
                P.wait("gpsimd", gc_dbg, gc_dbg.n)
                _dmp = P.op("gpsimd", lambda E: E.dma_start(out=dbg["dbg_o"], in_=o_sb[:].rearrange("p a b -> p (a b)")), gc_dbg, 16)
                for _e in ("tensor", "vector", "scalar", "sync"):
                    P.wait(_e, gc_dbg, gc_dbg.n)

        stQKV.close()
        P.barrier(ALLSEMS)
        with ExitStack() as stB2:
            eB2 = stB2.enter_context
            wproj_bf = eB2(nc.sbuf_tensor("wproj_bf", [128, 8, D], BF16))
            o_fm = eB2(nc.sbuf_tensor("o_fm", [128, 8, 128], BF16))
            x_rld = eB2(nc.sbuf_tensor("x_rld", [128, 2, D], F32))
            xn1 = eB2(nc.sbuf_tensor("xn1", [128, 2, D], F32))
            ybuf = eB2(nc.sbuf_tensor("ybuf", [128, D], F32))
            zbuf = eB2(nc.sbuf_tensor("zbuf", [128, D], F32))
            x1buf = eB2(nc.sbuf_tensor("x1buf", [128, 2, D], F32))
            st2 = eB2(nc.sbuf_tensor("st2", [128, 12], F32))
            mv2 = eB2(nc.sbuf_tensor("mv2", [128, 2], F32))
            rs2 = eB2(nc.sbuf_tensor("rs2", [128, 1], F32))
            nb2 = eB2(nc.sbuf_tensor("nb2", [128, 1], F32))
            mva = eB2(nc.sbuf_tensor("mva", [128, 2, 2], F32))
            rsa = eB2(nc.sbuf_tensor("rsa", [128, 2], F32))
            nba = eB2(nc.sbuf_tensor("nba", [128, 2], F32))
            g_bc = eB2(nc.sbuf_tensor("g_bc", [128, D], F32))
            b_bc = eB2(nc.sbuf_tensor("b_bc", [128, D], F32))
            pb_bc = eB2(nc.sbuf_tensor("pb_bc", [128, D], F32))
            rows = eB2(nc.sbuf_tensor("rows", [1, 3 * D], F32))
            ps_tr2 = [eB2(nc.psum_tensor(f"ps_tr2{i}", [128, 512], BF16)) for i in range(2)]
            ps_pj = [eB2(nc.psum_tensor(f"ps_pj{i}", [128, 512], F32)) for i in range(2)]

            P.op("sync", lambda E: E.dma_start(out=rows[:, 0:D], in_=projb_row), sc_const, 16)
            P.op("sync", lambda E: E.dma_start(out=rows[:, D:2 * D], in_=lng_row), sc_const, 16)
            d_r2 = P.op("sync", lambda E: E.dma_start(out=rows[:, 2 * D:3 * D], in_=lnb_row), sc_const, 16)
            d_wp = P.op("gpsimd", lambda E: E.dma_start(
                out=wproj_bf[:],
                in_=AP(wprojT.tensor, 0, [[D, 128], [128 * D, 8], [1, D]])), gc_wp, 16)
            P.wait("tensor", sc_const, d_r2)
            P.wait("tensor", vv, p1["ev"])
            bc_dsts = [pb_bc, g_bc, b_bc]
            bc_ev = [0, 0]
            for bi3 in range(3):
                for fh in range(2):
                    P.wait("tensor", vv, bc_ev[fh])
                    bm = P.op("tensor", lambda E, bi3=bi3, fh=fh: E.matmul(
                        ps_pj[fh][:], ones_f32[:], rows[:, bi3 * D + fh * 512:bi3 * D + (fh + 1) * 512],
                        start=True, stop=True), tt)
                    P.wait("vector", tt, bm)
                    bcd = P.op("vector", lambda E, bi3=bi3, fh=fh: E.tensor_copy(
                        bc_dsts[bi3][:, fh * 512:(fh + 1) * 512], ps_pj[fh][:]), vv)
                    bc_ev[fh] = bcd
            P.wait("tensor", vv, bc_ev[0])
            bm = P.op("tensor", lambda E: E.matmul(ps_pj[0][:, 0:8], ones_f32[:], wstat[:],
                                                   start=True, stop=True), tt)
            P.wait("vector", tt, bm)
            P.op("vector", lambda E: E.tensor_copy(mwg_b[:], ps_pj[0][:, 0:3]))
            swb = P.op("vector", lambda E: E.tensor_copy(swg_b[:], ps_pj[0][:, 3:6]), vv)
            if b_dbg:
                P.wait("gpsimd", vv, swb)
                P.wait("gpsimd", gc_dbg, gc_dbg.n)
                _dmp = P.op("gpsimd", lambda E: E.dma_start(out=dbg["dbg_wm"], in_=wstat[:]), gc_dbg, 16)
                for _e in ("tensor", "vector", "scalar", "sync"):
                    P.wait(_e, gc_dbg, gc_dbg.n)


            tr2_ev = [0, 0]
            ypj = [0, 0]
            x1_st = [0, 0]
            x_free = [0, 0]
            x_free_s = [0, 0]
            s1_apply = [0, 0]
            ytt = [0, 0]
            zap = 0
            x1ap = 0
            ztt2 = 0
            pm_box = [0]
            d_x = [0, 0]

            def b2_dma(qi):
                par = qi % 2
                P.wait("sync", vv, x_free[par])
                P.wait("sync", ss, x_free_s[par])
                d_x[par] = P.op("sync", lambda E, qi=qi, par=par: E.dma_start(
                    out=x_rld[:, par, :], in_=x_in[qi * 128:(qi + 1) * 128, :]), sc_xr[par], 16)

            def b2_s1(qi):
                par = qi % 2
                P.wait("vector", sc_xr[par], d_x[par])
                if qi == 0:
                    P.wait("vector", vv, bcd)
                P.op("vector", lambda E, par=par: E.bn_stats(st2[:, 0:6], x_rld[:, par, 0:512]))
                sb = P.op("vector", lambda E, par=par: E.bn_stats(st2[:, 6:12], x_rld[:, par, 512:1024]), vv)
                P.wait("vector", vv, sb)
                ag = P.op("vector", lambda E, par=par: E.bn_aggr(mva[:, par, :], st2[:]), vv)
                P.wait("scalar", vv, ag)
                lnx = P.op("scalar", lambda E, par=par: E.activation(
                    rsa[:, par:par + 1], mva[:, par, 1:2], AF.Ln, bias=eps_sb[:], scale=1.0), ss)
                P.wait("scalar", ss, lnx)
                rr = P.op("scalar", lambda E, par=par: E.activation(
                    rsa[:, par:par + 1], rsa[:, par:par + 1], AF.Exp, scale=-0.5), ss)
                P.wait("vector", ss, rr)
                nb = P.op("vector", lambda E, par=par: E.scalar_tensor_tensor(
                    nba[:, par:par + 1], mva[:, par, 0:1], -1.0, rsa[:, par:par + 1],
                    OP.mult, OP.mult), vv)
                P.wait("scalar", vv, nb)
                P.wait("scalar", vv, ytt[par])
                s1_apply[par] = P.op("scalar", lambda E, par=par: E.activation(
                    xn1[:, par, :], x_rld[:, par, :], AF.Identity,
                    bias=nba[:, par:par + 1], scale=rsa[:, par:par + 1]), ss)
                x_free_s[par] = s1_apply[par]

            def b2_pe(qi):
                P.wait("tensor", ss, b_end_ss)
                for cg in range(2):
                    tpp = cg % 2
                    P.wait("tensor", ss, tr2_ev[tpp])
                    P.wait("scalar", tt, pm_box[0])
                    for c4 in range(4):
                        c = cg * 4 + c4
                        tp = P.op("tensor", lambda E, qi=qi, c=c, c4=c4, tpp=tpp: E.transpose(
                            ps_tr2[tpp][:, c4 * 128:(c4 + 1) * 128],
                            o_sb[:, qi, c * 128:(c + 1) * 128], id_bf[:]), tt)
                    P.wait("scalar", tt, tp)
                    tr2_ev[tpp] = P.op("scalar", lambda E, cg=cg, tpp=tpp: E.activation(
                        o_fm[:].rearrange("p a b -> p (a b)")[:, cg * 512:(cg + 1) * 512],
                        ps_tr2[tpp][:], AF.Identity, scale=1.0), ss)
                P.wait("tensor", gc_wp, d_wp)
                P.wait("tensor", ss, max(tr2_ev))
                mm = 0
                for fh in range(2):
                    P.wait("tensor", vv, ypj[fh])
                    for kc in range(8):
                        mm = P.op("tensor", lambda E, fh=fh, kc=kc: E.matmul(
                            ps_pj[fh][:], o_fm[:, kc, :],
                            wproj_bf[:, kc, fh * 512:(fh + 1) * 512],
                            start=(kc == 0), stop=(kc == 7)), tt)
                pm_box[0] = mm
                return mm

            def b2_s2(qi, mm):
                nonlocal_ = {}
                par = qi % 2
                P.wait("vector", tt, mm)
                P.wait("vector", ss, zap)
                P.wait("vector", ss, s1_apply[par])
                for fh in range(2):
                    st_ = P.op("vector", lambda E, fh=fh: E.scalar_tensor_tensor(
                        ybuf[:, fh * 512:(fh + 1) * 512], ps_pj[fh][:], 0.0,
                        pb_bc[:, fh * 512:(fh + 1) * 512], OP.add, OP.add), vv)
                    P.wait("vector", vv, st_)
                    yt = P.op("vector", lambda E, fh=fh, par=par: E.tensor_tensor(
                        ybuf[:, fh * 512:(fh + 1) * 512], ybuf[:, fh * 512:(fh + 1) * 512],
                        xn1[:, par, fh * 512:(fh + 1) * 512], OP.add), vv)
                    ypj[fh] = yt
                ytt[par] = yt
                P.wait("vector", vv, yt)
                P.op("vector", lambda E: E.bn_stats(st2[:, 0:6], ybuf[:, 0:512]))
                sb = P.op("vector", lambda E: E.bn_stats(st2[:, 6:12], ybuf[:, 512:1024]), vv)
                P.wait("vector", vv, sb)
                ag = P.op("vector", lambda E: E.bn_aggr(mv2[:], st2[:]), vv)
                P.wait("scalar", vv, ag)
                lnx = P.op("scalar", lambda E: E.activation(rs2[:], mv2[:, 1:2], AF.Ln, bias=eps_sb[:], scale=1.0), ss)
                P.wait("scalar", ss, lnx)
                rr = P.op("scalar", lambda E: E.activation(rs2[:], rs2[:], AF.Exp, scale=-0.5), ss)
                P.wait("vector", ss, rr)
                nb = P.op("vector", lambda E: E.scalar_tensor_tensor(
                    nb2[:], mv2[:, 0:1], -1.0, rs2[:], OP.mult, OP.mult), vv)
                P.wait("scalar", vv, nb)
                P.wait("scalar", gg, ztt2)
                zap_ = P.op("scalar", lambda E: E.activation(
                    zbuf[:], ybuf[:], AF.Identity, bias=nb2[:], scale=rs2[:]), ss)
                P.wait("vector", ss, zap_)
                zt1 = P.op("vector", lambda E: E.tensor_tensor(zbuf[:], zbuf[:], g_bc[:], OP.mult), vv)
                P.wait("vector", vv, zt1)
                zt2 = P.op("vector", lambda E: E.tensor_tensor(zbuf[:], zbuf[:], b_bc[:], OP.add), vv)
                P.wait("vector", vv, zt2)
                P.op("vector", lambda E: E.bn_stats(st2[:, 0:6], zbuf[:, 0:512]))
                sb = P.op("vector", lambda E: E.bn_stats(st2[:, 6:12], zbuf[:, 512:1024]), vv)
                P.wait("vector", vv, sb)
                ag = P.op("vector", lambda E: E.bn_aggr(mv2[:], st2[:]), vv)
                P.wait("scalar", vv, ag)
                lnx = P.op("scalar", lambda E: E.activation(rs2[:], mv2[:, 1:2], AF.Ln, bias=eps_sb[:], scale=1.0), ss)
                P.wait("scalar", ss, lnx)
                rr = P.op("scalar", lambda E: E.activation(rs2[:], rs2[:], AF.Exp, scale=-0.5), ss)
                P.wait("vector", ss, rr)
                nb = P.op("vector", lambda E: E.scalar_tensor_tensor(
                    nb2[:], mv2[:, 0:1], -1.0, rs2[:], OP.mult, OP.mult), vv)
                P.wait("scalar", vv, nb)
                P.wait("scalar", sc_x1[par], x1_st[par])
                x1a = P.op("scalar", lambda E, par=par: E.activation(
                    x1buf[:, par, :], zbuf[:], AF.Identity, bias=nb2[:], scale=rs2[:]), ss)
                P.wait("vector", ss, x1a)
                xa = P.op("vector", lambda E, par=par: E.tensor_tensor(
                    x1buf[:, par, :], x1buf[:, par, :], x_rld[:, par, :], OP.add), vv)
                x_free[par] = xa
                P.wait("sync", vv, xa)
                x1_st[par] = P.op("sync", lambda E, qi=qi, par=par: E.dma_start(
                    out=x1_dram[qi * 128:(qi + 1) * 128, :], in_=x1buf[:, par, :]), sc_x1[par], 16)
                return zap_, x1a

            b2_dma(0)
            if NT > 1:
                b2_dma(1)
            b2_s1(0)
            for qi in range(NT):
                mm = b2_pe(qi)
                if qi + 1 < NT:
                    b2_s1(qi + 1)
                zap, x1ap = b2_s2(qi, mm)
                if qi + 2 < NT:
                    b2_dma(qi + 2)

            if b_dbg:
                P.wait("gpsimd", sc_x1[0], x1_st[0])
                P.wait("gpsimd", sc_x1[1], x1_st[1])
                P.wait("gpsimd", gc_dbg, gc_dbg.n)
                _dmp = P.op("gpsimd", lambda E: E.dma_start(out=dbg["dbg_x1"], in_=x1_dram), gc_dbg, 16)
                for _e in ("tensor", "vector", "scalar", "sync"):
                    P.wait(_e, gc_dbg, gc_dbg.n)

        stO.close()
        P.barrier(ALLSEMS)
        with ExitStack() as stC:
            eC = stC.enter_context
            wdq_sb = eC(nc.sbuf_tensor("wdq_sb", [128, 32, D], BF16))
            st3 = eC(nc.sbuf_tensor("st3", [128, 12], F32))
            mv3 = eC(nc.sbuf_tensor("mv3", [128, 2], F32))
            rs3 = eC(nc.sbuf_tensor("rs3", [128, 1], F32))
            sc1 = eC(nc.sbuf_tensor("sc1", [128, NT], F32))
            rsg = eC(nc.sbuf_tensor("rsg", [128, NT], F32))
            rsu = eC(nc.sbuf_tensor("rsu", [128, NT], F32))
            rsd = eC(nc.sbuf_tensor("rsd", [128, NT], F32))
            magic_c = eC(nc.sbuf_tensor("magic_c", [128, 1], F32))
            nmagic_c = eC(nc.sbuf_tensor("nmagic_c", [128, 1], F32))
            ps_t3 = [eC(nc.psum_tensor(f"ps_t3{i}", [128, 512], BF16)) for i in range(2)]
            stQ3 = ExitStack()
            stCG = ExitStack()
            xq3_fm = stQ3.enter_context(nc.sbuf_tensor("xq3_fm", [128, 8, S], BF16))
            wgf = stCG.enter_context(nc.sbuf_tensor("wgf", [128, 4, 512], F32))
            wuf = stCG.enter_context(nc.sbuf_tensor("wuf", [128, 4, 512], F32))
            wdf = stCG.enter_context(nc.sbuf_tensor("wdf", [128, 2, D], F32))
            wq_t = stCG.enter_context(nc.sbuf_tensor("wq_t", [128, 2048], F32))
            wq_c1 = stCG.enter_context(nc.sbuf_tensor("wq_c1", [128, 2048], BF16))
            wq_c2 = stCG.enter_context(nc.sbuf_tensor("wq_c2", [128, 2048], BF16))
            wgq_sb = stCG.enter_context(nc.sbuf_tensor("wgq_sb", [128, 2, 8, 512], BF16))
            wuq_sb = stCG.enter_context(nc.sbuf_tensor("wuq_sb", [128, 2, 8, 512], BF16))
            g_act = stCG.enter_context(nc.sbuf_tensor("g_act", [128, 2, 512], BF16))
            prod_sb = stCG.enter_context(nc.sbuf_tensor("prod_sb", [128, 2, 512], F32))
            ps_g = [stCG.enter_context(nc.psum_tensor(f"ps_g{i}", [128, 512], F32)) for i in range(2)]
            ps_u = [stCG.enter_context(nc.psum_tensor(f"ps_u{i}", [128, 512], F32)) for i in range(2)]

            P.op("vector", lambda E: E.memset(magic_c[:], 6755399441055744.0))
            mgd = P.op("vector", lambda E: E.memset(nmagic_c[:], -6755399441055744.0), vv)

            sc_ps = [sc_p2l, sc_p2s]
            wsems = {"g": gc_gu, "u": gc_wq, "d": gc_wd}
            wsrcs = {"g": wgT, "u": wuT}
            wbufs = {"g": wgf, "u": wuf}
            wqsb = {"g": wgq_sb, "u": wuq_sb}
            widx = {"g": 0, "u": 1, "d": 2}
            wbuf_free = {"g": 0, "u": 0, "d": 0}
            wdma = {("g", 0): 0, ("g", 1): 0, ("u", 0): 0, ("u", 1): 0,
                    ("d", 0): 0, ("d", 1): 0}
            qfree = [0]
            qcons = [0]
            quv = {"g": [0, 0], "u": [0, 0]}
            gu_used = [0, 0]

            def emit_wdma(ic):
                for mat in ("g", "u"):
                    P.wait("gpsimd", ss, wbuf_free[mat])
                    wdma[(mat, 0)] = P.op(
                        "gpsimd", lambda E, mat=mat, ic=ic: E.dma_start(
                            out=wbufs[mat][:],
                            in_=AP(wsrcs[mat].tensor, ic * 512,
                                   [[INNER, 128], [128 * INNER, 4], [1, 512]])),
                        wsems[mat][0], 16)
                P.wait("gpsimd", ss, wbuf_free["d"])
                wdma[("d", 0)] = P.op("gpsimd", lambda E, ic=ic: E.dma_start(
                    out=wdf[:],
                    in_=AP(wdT.tensor, ic * 512 * D, [[D, 128], [128 * D, 2], [1, D]])),
                    gc_wd[0], 16)

            def emit_qunit(mat, ic, hf):
                icpar = ic % 2
                src = wdf[:] if mat == "d" else wbufs[mat][:]
                dsem, dval = wsems[mat][hf], wdma[(mat, hf)]
                P.wait("scalar", dsem, dval)
                P.wait("scalar", vv, qfree[0])
                P.wait("scalar", vv, swb)
                tw = P.op("scalar", lambda E, src=src, mat=mat: E.activation(
                    wq_t[:], src, AF.Identity, scale=swg_b[:, widx[mat]:widx[mat] + 1]), ss)
                if hf == 0:
                    P.wait("gpsimd", ss, tw)
                    if mat == "d":
                        wdma[("d", 1)] = P.op("gpsimd", lambda E, ic=ic: E.dma_start(
                            out=wdf[:],
                            in_=AP(wdT.tensor, (ic * 512 + 256) * D,
                                   [[D, 128], [128 * D, 2], [1, D]])),
                            gc_wd[1], 16)
                    else:
                        wdma[(mat, 1)] = P.op(
                            "gpsimd", lambda E, mat=mat, ic=ic: E.dma_start(
                                out=wbufs[mat][:],
                                in_=AP(wsrcs[mat].tensor, 512 * INNER + ic * 512,
                                       [[INNER, 128], [128 * INNER, 4], [1, 512]])),
                            wsems[mat][1], 16)
                else:
                    wbuf_free[mat] = tw
                P.wait("vector", ss, tw)
                P.wait("vector", vv, qcons[0])
                P.op("vector", lambda E: E.tensor_scalar(wq_c1[:], wq_t[:], 0.5, None, OP.is_gt))
                cmp2 = P.op("vector", lambda E: E.tensor_scalar(wq_c2[:], wq_t[:], -0.5, None, OP.is_lt), vv)
                qfree[0] = cmp2
                P.wait("vector", vv, cmp2)
                if mat == "d":
                    dst = wdq_sb[:, 4 * ic + 2 * hf:4 * ic + 2 * hf + 2, :]
                else:
                    if hf == 0:
                        P.wait("vector", tt, gu_used[icpar])
                    dst = wqsb[mat][:, icpar, 4 * hf:4 * hf + 4, :]
                qq_ = P.op("vector", lambda E, dst=dst: E.tensor_tensor(
                    dst.rearrange("p a b -> p (a b)"), wq_c1[:], wq_c2[:], OP.subtract), vv)
                qcons[0] = qq_
                if mat != "d":
                    quv[mat][icpar] = qq_

            UNITS = [("g", 0), ("g", 1), ("u", 0), ("u", 1), ("d", 0), ("d", 1)]
            unit_ts = [1 + i * max(1, (NT - 2) // 6) for i in range(6)]

            x1t_used = [0, 0]
            xq3_done = [0] * NT
            i32_free = 0
            tr3_ev = [0, 0]
            emit_wdma(0)
            with ExitStack() as stC0:
                e0 = stC0.enter_context
                x1t = e0(nc.sbuf_tensor("x1t", [128, 2, D], F32))
                xn3 = e0(nc.sbuf_tensor("xn3", [128, D], F32))
                xq_i32a = e0(nc.sbuf_tensor("xq_i32a", [128, D], I32))
                d_bfa = e0(nc.sbuf_tensor("d_bfa", [128, D], F32))
                xq3_tm = e0(nc.sbuf_tensor("xq3_tm", [128, D], BF16))
                for t in range(NT):
                    par = t % 2
                    P.wait("sync", vv, x1t_used[par])
                    d_x1 = P.op("sync", lambda E, t=t, par=par: E.dma_start(
                        out=x1t[:, par, :], in_=x1_dram[t * 128:(t + 1) * 128, :]), sc_xr[par], 16)
                    P.wait("vector", sc_xr[par], d_x1)
                    if t == 0:
                        P.wait("vector", vv, swb)
                        P.wait("vector", vv, mgd)
                    P.op("vector", lambda E, par=par: E.bn_stats(st3[:, 0:6], x1t[:, par, 0:512]))
                    sb = P.op("vector", lambda E, par=par: E.bn_stats(st3[:, 6:12], x1t[:, par, 512:1024]), vv)
                    P.wait("vector", vv, sb)
                    ag = P.op("vector", lambda E: E.bn_aggr(mv3[:], st3[:]), vv)
                    P.wait("scalar", vv, ag)
                    lnx = P.op("scalar", lambda E: E.activation(rs3[:], mv3[:, 1:2], AF.Ln, bias=eps_sb[:], scale=1.0), ss)
                    P.wait("scalar", ss, lnx)
                    rr = P.op("scalar", lambda E: E.activation(rs3[:], rs3[:], AF.Exp, scale=-0.5), ss)
                    P.wait("vector", ss, rr)
                    xnd = P.op("vector", lambda E, par=par: E.tensor_scalar(
                        xn3[:], x1t[:, par, :], mv3[:, 0:1], rs3[:], OP.subtract, OP.mult), vv)
                    x1t_used[par] = xnd
                    P.wait("vector", vv, xnd)
                    am = P.op("vector", lambda E, t=t: E.reduce_max(
                        sc1[:, t:t + 1], xn3[:], axis=AX.X, apply_absolute_value=True), vv)
                    P.wait("vector", vv, am)
                    mc1 = P.op("vector", lambda E, t=t: E.tensor_scalar(
                        sc1[:, t:t + 1], sc1[:, t:t + 1], 1e-5, None, OP.max), vv)
                    P.wait("vector", vv, mc1)
                    P.op("vector", lambda E, t=t: E.scalar_tensor_tensor(
                        rsg[:, t:t + 1], sc1[:, t:t + 1], 1.0 / 127.0, mwg_b[:, 0:1], OP.mult, OP.mult))
                    ru = P.op("vector", lambda E, t=t: E.scalar_tensor_tensor(
                        rsu[:, t:t + 1], sc1[:, t:t + 1], 1.0 / 127.0, mwg_b[:, 1:2], OP.mult, OP.mult), vv)
                    P.wait("vector", vv, ru)
                    rc = P.op("vector", lambda E, t=t: E.reciprocal(sc1[:, t:t + 1], sc1[:, t:t + 1]), vv)
                    P.wait("vector", vv, rc)
                    s127 = P.op("vector", lambda E, t=t: E.tensor_scalar(
                        sc1[:, t:t + 1], sc1[:, t:t + 1], 127.0, None, OP.mult), vv)
                    P.wait("scalar", vv, s127)
                    P.wait("scalar", vv, i32_free)
                    qi32 = P.op("scalar", lambda E, t=t: E.activation(
                        xq_i32a[:], xn3[:], AF.Identity, scale=sc1[:, t:t + 1]), ss)
                    P.wait("vector", ss, qi32)
                    dd1 = P.op("vector", lambda E, t=t: E.scalar_tensor_tensor(
                        d_bfa[:], xn3[:], sc1[:, t:t + 1], xq_i32a[:], OP.mult, OP.subtract), vv)
                    P.wait("scalar", vv, dd1)
                    da = P.op("scalar", lambda E: E.activation(
                        d_bfa[:], d_bfa[:], AF.Identity, bias=magic_c[:], scale=1.0), ss)
                    P.wait("scalar", ss, da)
                    dd2b = P.op("scalar", lambda E: E.activation(
                        d_bfa[:], d_bfa[:], AF.Identity, bias=nmagic_c[:], scale=1.0), ss)
                    P.wait("vector", ss, dd2b)
                    i32_free = P.op("vector", lambda E: E.scalar_tensor_tensor(
                        xq3_tm[:], d_bfa[:], 1.0, xq_i32a[:], OP.mult, OP.add), vv)
                    P.wait("tensor", vv, i32_free)
                    for cg in range(2):
                        tpp = (2 * t + cg) % 2
                        P.wait("tensor", vv, tr3_ev[tpp])
                        for c4 in range(4):
                            c = cg * 4 + c4
                            tp = P.op("tensor", lambda E, c=c, c4=c4, tpp=tpp: E.transpose(
                                ps_t3[tpp][:, c4 * 128:(c4 + 1) * 128],
                                xq3_tm[:, c * 128:(c + 1) * 128], id_bf[:]), tt)
                        P.wait("vector", tt, tp)
                        tr3_ev[tpp] = P.op("vector", lambda E, t=t, cg=cg, tpp=tpp: E.tensor_copy(
                            AP(xq3_fm[:].tensor, cg * 4 * S + t * 128,
                               [[8 * S, 128], [S, 4], [1, 128]]),
                            ps_t3[tpp][:].rearrange("p (a b) -> p a b", a=4)), vv)
                    xq3_done[t] = tr3_ev[(2 * t + 1) % 2]
                    for ui, (mat, hf) in enumerate(UNITS):
                        if unit_ts[ui] == t + 1 or (t == NT - 1 and unit_ts[ui] > NT - 1):
                            emit_qunit(mat, 0, hf)

            silu_done = [0, 0]
            prod_done = [0, 0]
            prod_st = [0, 0]
            for ic in range(8):
                icpar = ic % 2
                if ic + 1 < 8:
                    emit_wdma(ic + 1)
                P.wait("tensor", vv, quv["g"][icpar])
                P.wait("tensor", vv, quv["u"][icpar])
                for t in range(NT):
                    tpar = t % 2
                    if ic == 0:
                        P.wait("tensor", vv, xq3_done[t])
                    P.wait("tensor", ss, silu_done[tpar])
                    P.wait("tensor", vv, prod_done[tpar])
                    for kc in range(8):
                        P.op("tensor", lambda E, kc=kc, t=t, icpar=icpar, tpar=tpar: E.matmul(
                            ps_g[tpar][:], xq3_fm[:, kc, t * 128:(t + 1) * 128],
                            wgq_sb[:, icpar, kc, :], start=(kc == 0), stop=(kc == 7)))
                    for kc in range(8):
                        mmu = P.op("tensor", lambda E, kc=kc, t=t, icpar=icpar, tpar=tpar: E.matmul(
                            ps_u[tpar][:], xq3_fm[:, kc, t * 128:(t + 1) * 128],
                            wuq_sb[:, icpar, kc, :], start=(kc == 0), stop=(kc == 7)), tt)
                    gu_used[icpar] = mmu
                    P.wait("scalar", tt, mmu)
                    P.wait("scalar", vv, prod_done[tpar])
                    silu_done[tpar] = P.op("scalar", lambda E, t=t, tpar=tpar: E.activation(
                        g_act[:, tpar, :], ps_g[tpar][:], AF.Silu, scale=rsg[:, t:t + 1]), ss)
                    P.wait("vector", ss, silu_done[tpar])
                    P.wait("vector", sc_ps[tpar], prod_st[tpar])
                    prod_done[tpar] = P.op("vector", lambda E, tpar=tpar: E.tensor_tensor(
                        prod_sb[:, tpar, :], g_act[:, tpar, :], ps_u[tpar][:], OP.mult), vv)
                    P.wait("sync", vv, prod_done[tpar])
                    prod_st[tpar] = P.op("sync", lambda E, ic=ic, t=t, tpar=tpar: E.dma_start(
                        out=AP(gu_dram.tensor, t * 128 * INNER + ic * 512, [[INNER, 128], [1, 512]]),
                        in_=prod_sb[:, tpar, :]), sc_ps[tpar], 16)
                    if ic + 1 < 8:
                        for ui, (mat, hf) in enumerate(UNITS):
                            if unit_ts[ui] == t + 1 or (t == NT - 1 and unit_ts[ui] > NT - 1):
                                emit_qunit(mat, ic + 1, hf)

            stCG.close()
            stQ3.close()
            P.barrier(ALLSEMS)
            with ExitStack() as stCD:
                eD = stCD.enter_context
                gu_t = eD(nc.sbuf_tensor("gu_t", [128, 2, INNER], F32))
                x1t2 = eD(nc.sbuf_tensor("x1t2", [128, 2, D], F32))
                xqgu_tm = eD(nc.sbuf_tensor("xqgu_tm", [128, 2, INNER], BF16))
                xqgu_fm = eD(nc.sbuf_tensor("xqgu_fm", [128, 2, 32, 128], BF16))
                xq_i32b = eD(nc.sbuf_tensor("xq_i32b", [128, INNER], I32))
                d_bfb = eD(nc.sbuf_tensor("d_bfb", [128, INNER], F32))
                mpt = eD(nc.sbuf_tensor("mpt", [128, NT, 3], F32))
                ffo = eD(nc.sbuf_tensor("ffo", [128, 2, D], F32))
                outb = eD(nc.sbuf_tensor("outb", [128, 2, D], F32))
                ps_d = [eD(nc.psum_tensor(f"ps_d{i}", [128, D], F32)) for i in range(2)]

                gu_free = [0, 0]
                gu_free_s = [0, 0]
                gu_free_g = [0, 0]
                x1_free = [0, 0]
                ffo_used = [0, 0]
                ffo_ev = [0, 0]
                out_st = [0, 0]
                d_gu = [0, 0]
                d_x1 = [0, 0]
                dd1_d = [0, 0]
                dd2_d = [0, 0]
                comb_d = [0, 0]
                qi32_d = [0, 0]
                copy_d = {}
                tr_d = {}
                mm_d = [0, 0]

                def cd_dma_gu(t):
                    par = t % 2
                    P.wait("sync", vv, gu_free[par])
                    P.wait("sync", ss, gu_free_s[par])
                    d_gu[par] = P.op("sync", lambda E, t=t, par=par: E.dma_start(
                        out=gu_t[:, par, :],
                        in_=AP(gu_dram.tensor, t * 128 * INNER, [[INNER, 128], [1, INNER]])), sc_x1[par], 16)

                def cd_dma_x1(t):
                    par = t % 2
                    P.wait("sync", vv, x1_free[par])
                    d_x1[par] = P.op("sync", lambda E, t=t, par=par: E.dma_start(
                        out=x1t2[:, par, :], in_=x1_dram[t * 128:(t + 1) * 128, :]), sc_xr[par], 16)

                def cd_quant(t):
                    par = t % 2
                    P.wait("vector", sc_x1[par], d_gu[par])
                    P.wait("vector", vv, mgd)
                    am = P.op("vector", lambda E, t=t, par=par: E.reduce_max(
                        mpt[:, t, 0:1], gu_t[:, par, :], axis=AX.X, apply_absolute_value=True), vv)
                    gu_free_g[par] = 0
                    P.wait("vector", vv, am)
                    mc2 = P.op("vector", lambda E, t=t: E.tensor_scalar(
                        mpt[:, t, 1:2], mpt[:, t, 0:1], rsu[:, t:t + 1], 1e-5, OP.mult, OP.max), vv)
                    P.wait("vector", vv, mc2)
                    P.op("vector", lambda E, t=t: E.scalar_tensor_tensor(
                        rsd[:, t:t + 1], mpt[:, t, 1:2], 1.0 / 127.0, mwg_b[:, 2:3], OP.mult, OP.mult))
                    rc = P.op("vector", lambda E, t=t: E.reciprocal(mpt[:, t, 2:3], mpt[:, t, 1:2]), vv)
                    P.wait("vector", vv, rc)
                    v2d = P.op("vector", lambda E, t=t: E.scalar_tensor_tensor(
                        mpt[:, t, 2:3], mpt[:, t, 2:3], 127.0, rsu[:, t:t + 1], OP.mult, OP.mult), vv)
                    P.wait("scalar", vv, v2d)
                    P.wait("scalar", vv, max(comb_d))
                    qi32_d[par] = P.op("scalar", lambda E, t=t, par=par: E.activation(
                        xq_i32b[:], gu_t[:, par, :], AF.Identity, scale=mpt[:, t, 2:3]), ss)
                    gu_free_s[par] = qi32_d[par]
                    P.wait("vector", ss, qi32_d[par])
                    P.wait("vector", vv, max(comb_d))
                    dd1_d[par] = P.op("vector", lambda E, t=t, par=par: E.scalar_tensor_tensor(
                        d_bfb[:], gu_t[:, par, :], mpt[:, t, 2:3], xq_i32b[:],
                        OP.mult, OP.subtract), vv)
                    gu_free[par] = dd1_d[par]
                    P.wait("scalar", vv, dd1_d[par])
                    da = P.op("scalar", lambda E: E.activation(
                        d_bfb[:], d_bfb[:], AF.Identity, bias=magic_c[:], scale=1.0), ss)
                    P.wait("scalar", ss, da)
                    dd2_d[par] = P.op("scalar", lambda E: E.activation(
                        d_bfb[:], d_bfb[:], AF.Identity, bias=nmagic_c[:], scale=1.0), ss)
                    P.wait("vector", ss, dd2_d[par])
                    P.wait("vector", tt, tr_d.get((t - 2, 7), 0))
                    comb_d[par] = P.op("vector", lambda E, par=par: E.scalar_tensor_tensor(
                        xqgu_tm[:, par, :], d_bfb[:], 1.0, xq_i32b[:],
                        OP.mult, OP.add), vv)

                def cd_transp(t):
                    par = t % 2
                    P.wait("tensor", vv, comb_d[par])
                    P.wait("scalar", tt, mm_d[par])
                    for cg in range(8):
                        tpp = cg % 2
                        prev = copy_d[(t, cg - 2)] if cg >= 2 else copy_d.get((t - 1, 6 + tpp), 0)
                        P.wait("tensor", ss, prev)
                        for c4 in range(4):
                            c = cg * 4 + c4
                            tp = P.op("tensor", lambda E, c=c, c4=c4, tpp=tpp, par=par: E.transpose(
                                ps_t3[tpp][:, c4 * 128:(c4 + 1) * 128],
                                xqgu_tm[:, par, c * 128:(c + 1) * 128], id_bf[:]), tt)
                        tr_d[(t, cg)] = tp
                        P.wait("scalar", tt, tp)
                        copy_d[(t, cg)] = P.op("scalar", lambda E, cg=cg, tpp=tpp, par=par: E.activation(
                            xqgu_fm[:, par, cg * 4:(cg + 1) * 4, :].rearrange("p a b -> p (a b)"),
                            ps_t3[tpp][:], AF.Identity, scale=1.0), ss)

                def cd_mm(t):
                    par = t % 2
                    P.wait("tensor", ss, copy_d[(t, 7)])
                    P.wait("tensor", ss, ffo_ev[par])
                    for fh in range(2):
                        for kc in range(32):
                            mmd = P.op("tensor", lambda E, kc=kc, fh=fh, par=par: E.matmul(
                                ps_d[par][:, fh * 512:(fh + 1) * 512], xqgu_fm[:, par, kc, :],
                                wdq_sb[:, kc, fh * 512:(fh + 1) * 512],
                                start=(kc == 0), stop=(kc == 31)), tt)
                        P.wait("scalar", tt, mmd)
                        if fh == 0:
                            P.wait("scalar", vv, ffo_used[par])
                        ffo_ev[par] = P.op("scalar", lambda E, t=t, fh=fh, par=par: E.activation(
                            ffo[:, par, fh * 512:(fh + 1) * 512],
                            ps_d[par][:, fh * 512:(fh + 1) * 512],
                            AF.Identity, scale=rsd[:, t:t + 1]), ss)
                    mm_d[par] = mmd

                def cd_ln4(t):
                    par = t % 2
                    P.wait("vector", ss, ffo_ev[par])
                    P.op("vector", lambda E, par=par: E.bn_stats(st3[:, 0:6], ffo[:, par, 0:512]))
                    sb4 = P.op("vector", lambda E, par=par: E.bn_stats(st3[:, 6:12], ffo[:, par, 512:1024]), vv)
                    P.wait("vector", vv, sb4)
                    ag4 = P.op("vector", lambda E: E.bn_aggr(mv3[:], st3[:]), vv)
                    P.wait("scalar", vv, ag4)
                    lnx4 = P.op("scalar", lambda E: E.activation(rs3[:], mv3[:, 1:2], AF.Ln, bias=eps_sb[:], scale=1.0), ss)
                    P.wait("scalar", ss, lnx4)
                    rr4 = P.op("scalar", lambda E: E.activation(rs3[:], rs3[:], AF.Exp, scale=-0.5), ss)
                    P.wait("vector", ss, rr4)
                    P.wait("vector", sc_out[par], out_st[par])
                    ow = P.op("vector", lambda E, par=par: E.tensor_scalar(
                        outb[:, par, :], ffo[:, par, :], mv3[:, 0:1], rs3[:], OP.subtract, OP.mult), vv)
                    ffo_used[par] = ow
                    P.wait("vector", vv, ow)
                    P.wait("vector", sc_xr[par], d_x1[par])
                    ox = P.op("vector", lambda E, par=par: E.tensor_tensor(
                        outb[:, par, :], outb[:, par, :], x1t2[:, par, :], OP.add), vv)
                    x1_free[par] = ox
                    P.wait("sync", vv, ox)
                    out_st[par] = P.op("sync", lambda E, t=t, par=par: E.dma_start(
                        out=out[t * 128:(t + 1) * 128, :], in_=outb[:, par, :]), sc_out[par], 16)

                cd_dma_gu(0)
                if NT > 1:
                    cd_dma_gu(1)
                cd_dma_x1(0)
                cd_quant(0)
                cd_transp(0)
                for i in range(NT):
                    if i + 2 < NT:
                        cd_dma_gu(i + 2)
                    if i + 1 < NT:
                        cd_dma_x1(i + 1)
                    if i + 1 < NT:
                        cd_quant(i + 1)
                    cd_mm(i)
                    if i + 1 < NT:
                        cd_transp(i + 1)
                    cd_ln4(i)

                P.wait("sync", sc_out[0], out_st[0])
                P.wait("sync", sc_out[1], out_st[1])
                P.wait("gpsimd", sc_out[0], out_st[0])
                P.wait("gpsimd", sc_out[1], out_st[1])

        with nc.Block() as block:
            @block.tensor
            def _(E):
                for f in P.ev["tensor"]:
                    f(E)

            @block.vector
            def _(E):
                for f in P.ev["vector"]:
                    f(E)

            @block.scalar
            def _(E):
                for f in P.ev["scalar"]:
                    f(E)

            @block.gpsimd
            def _(E):
                for f in P.ev["gpsimd"]:
                    f(E)

            @block.sync
            def _(E):
                for f in P.ev["sync"]:
                    f(E)

    return nc, dbg


_CACHE = {}


def _host_inputs(L0_attn_W_w, L0_attn_W_b, L0_attn_proj_w, L0_attn_proj_b,
                 L0_attn_ln_g, L0_attn_ln_b, L0_attn_rpb,
                 L0_ffn_w_gate, L0_ffn_w_up, L0_ffn_w_down):
    f = np.asarray
    Ww = f(L0_attn_W_w, dtype=np.float32)
    Wb = f(L0_attn_W_b, dtype=np.float32)
    rpb = f(L0_attn_rpb, dtype=np.float32)
    shared = {
        "wqkvT": np.ascontiguousarray(Ww.T),
        "wb_pc": np.ascontiguousarray(Wb.reshape(24, 128).T),
        "wbv_row": np.ascontiguousarray(Wb[2048:3072].reshape(1, 1024)),
        "wprojT": np.ascontiguousarray(f(L0_attn_proj_w, dtype=np.float32).T),
        "projb_row": np.ascontiguousarray(f(L0_attn_proj_b, dtype=np.float32).reshape(1, 1024)),
        "lng_row": np.ascontiguousarray(f(L0_attn_ln_g, dtype=np.float32).reshape(1, 1024)),
        "lnb_row": np.ascontiguousarray(f(L0_attn_ln_b, dtype=np.float32).reshape(1, 1024)),
        "rpbmt": np.ascontiguousarray(rpb[::-1, :].T).reshape(-1),
        "wgT": np.ascontiguousarray(f(L0_ffn_w_gate, dtype=np.float32).T),
        "wuT": np.ascontiguousarray(f(L0_ffn_w_up, dtype=np.float32).T),
        "wdT": np.ascontiguousarray(f(L0_ffn_w_down, dtype=np.float32).T),
        "id_in": np.eye(128, dtype=np.float32),
        "j_in": np.eye(128, dtype=np.float32)[::-1].copy(),
    }
    m = np.arange(128)[:, None] + np.arange(128)[None, :]
    shared["mask_in"] = np.where(m <= 126, np.float32(NEG), np.float32(0.0))
    return shared


def kernel(L0_attn_x, L0_attn_W_w, L0_attn_W_b, L0_attn_proj_w, L0_attn_proj_b,
           L0_attn_ln_g, L0_attn_ln_b, L0_attn_rpb, L0_ffn_w_gate, L0_ffn_w_up,
           L0_ffn_w_down):
    x = np.asarray(L0_attn_x, dtype=np.float32)
    B = x.shape[0]
    if "nc" not in _CACHE:
        _CACHE["nc"] = build(S=2048, b_dbg=False)[0]
    nc = _CACHE["nc"]
    shared = _host_inputs(L0_attn_W_w, L0_attn_W_b, L0_attn_proj_w, L0_attn_proj_b,
                          L0_attn_ln_g, L0_attn_ln_b, L0_attn_rpb,
                          L0_ffn_w_gate, L0_ffn_w_up, L0_ffn_w_down)
    in_maps = []
    for b in range(B):
        m = dict(shared)
        m["x"] = np.ascontiguousarray(x[b])
        in_maps.append(m)
    res = run_bass_kernel_spmd(nc, in_maps, core_ids=list(range(B)))
    return np.stack([res.results[b]["out"] for b in range(B)], axis=0)


# version 23
# speedup vs baseline: 1.0364x; 1.0207x over previous
import numpy as np
from contextlib import ExitStack

import concourse.bass as bass
import concourse.mybir as mybir
from concourse.ap import AP
from concourse.bass_utils import run_bass_kernel_spmd
from concourse.alu_op_type import AluOpType
from concourse import library_config

F32 = mybir.dt.float32
BF16 = mybir.dt.bfloat16
I32 = mybir.dt.int32
AF = mybir.ActivationFunctionType
AX = mybir.AxisListType
OP = AluOpType

D = 1024
H = 16
INNER = 4096
NEG = -1.0e30
EXP_OFF = 40.0


class Sem:
    __slots__ = ("h", "n")
    def __init__(self, h):
        self.h = h
        self.n = 0


class RotSem:
    CAP = 1600

    def __init__(self, handles):
        self.hs = handles
        self.n = 0
        self.seg = 0
        self.base = 0

    def loc(self, v):
        if v == 0:
            return self.hs[0], 0
        s = (v - 1) // self.CAP
        return self.hs[s % len(self.hs)], v - s * self.CAP


class Prog:
    ENGS = ("tensor", "vector", "scalar", "gpsimd", "sync")

    def __init__(self, nc):
        self.nc = nc
        self.ev = {k: [] for k in self.ENGS}

    def wait(self, eng, sem, val):
        if not val:
            return
        if isinstance(sem, RotSem):
            h, v = sem.loc(int(val))
            self.ev[eng].append(lambda e, s=h, vv_=v: e.wait_ge(s, vv_))
        else:
            self.ev[eng].append(lambda e, s=sem.h, v=int(val): e.wait_ge(s, v))

    def op(self, eng, make, inc=None, k=1):
        if inc is not None:
            if isinstance(inc, RotSem):
                inc.n += k
                h, _ = inc.loc(inc.n)
                self.ev[eng].append(lambda e, m=make, s=h, kk=k: m(e).then_inc(s, kk))
                return inc.n
            inc.n += k
            self.ev[eng].append(lambda e, m=make, s=inc.h, kk=k: m(e).then_inc(s, kk))
            return inc.n
        self.ev[eng].append(make)
        return None

    def barrier(self, sems):
        snap = [(s, s.n) for s in sems if s.n > 0]
        for eng in self.ENGS:
            for s, n in snap:
                self.wait(eng, s, n)


def build(S=2048, b_dbg=False, pad_instr=0):
    NT = S // 128
    NP = NT // 2
    QT = max(1, NT // 8)
    R = 2 * S - 1
    WEXT = 384 + S
    nc = bass.Bass("TRN2", target_bir_lowering=False)

    x_in = nc.dram_tensor("x", [S, D], F32, kind="ExternalInput").ap()
    wqkvT = nc.dram_tensor("wqkvT", [D, 3 * D], F32, kind="ExternalInput").ap()
    wb_pc = nc.dram_tensor("wb_pc", [128, 24], F32, kind="ExternalInput").ap()
    wbv_row = nc.dram_tensor("wbv_row", [1, D], F32, kind="ExternalInput").ap()
    wprojT = nc.dram_tensor("wprojT", [D, D], F32, kind="ExternalInput").ap()
    projb_row = nc.dram_tensor("projb_row", [1, D], F32, kind="ExternalInput").ap()
    lng_row = nc.dram_tensor("lng_row", [1, D], F32, kind="ExternalInput").ap()
    lnb_row = nc.dram_tensor("lnb_row", [1, D], F32, kind="ExternalInput").ap()
    rpbmt = nc.dram_tensor("rpbmt", [H * R], F32, kind="ExternalInput").ap()
    wgT = nc.dram_tensor("wgT", [D, INNER], F32, kind="ExternalInput").ap()
    wuT = nc.dram_tensor("wuT", [D, INNER], F32, kind="ExternalInput").ap()
    wdT = nc.dram_tensor("wdT", [INNER, D], F32, kind="ExternalInput").ap()
    id_in = nc.dram_tensor("id_in", [128, 128], F32, kind="ExternalInput").ap()
    j_in = nc.dram_tensor("j_in", [128, 128], F32, kind="ExternalInput").ap()
    mask_in = nc.dram_tensor("mask_in", [128, 128], F32, kind="ExternalInput").ap()

    out = nc.dram_tensor("out", [S, D], F32, kind="ExternalOutput").ap()

    x1_dram = nc.dram_tensor("x1_scratch", [S, D], F32, kind="Internal").ap()
    gu_dram = nc.dram_tensor("gu_scratch", [S, INNER], F32, kind="Internal").ap()

    dbg = {}
    if b_dbg:
        for nm, shp, dt in (("dbg_q", [128, 8 * S], BF16), ("dbg_k", [128, 8 * S], BF16),
                            ("dbg_v", [128, NT * H * 65], BF16), ("dbg_o", [128, NT * D], BF16),
                            ("dbg_x1", [S, D], F32),
                            ("dbg_xn", [128, 8 * S], BF16),
                            ("dbg_wm", [1, 8], F32)):
            dbg[nm] = nc.dram_tensor(nm, shp, dt, kind="ExternalOutput").ap()

    st = ExitStack()
    with st:
        e = st.enter_context

        tt = RotSem([e(nc.semaphore(f"sem_t{i}")) for i in range(8)])
        vv = Sem(e(nc.semaphore("sem_v")))
        ss = Sem(e(nc.semaphore("sem_s")))
        gg = Sem(e(nc.semaphore("sem_g")))
        def _sem(nm):
            return Sem(e(nc.semaphore(nm)))
        sc_const = _sem("sc_const")
        sc_x = _sem("sc_x")
        sc_p1 = _sem("sc_p1")
        sc_p2l = _sem("sc_p2l")
        sc_p2s = _sem("sc_p2s")
        sc_xr = [_sem("sc_xr0"), _sem("sc_xr1")]
        sc_x1 = [_sem("sc_x10"), _sem("sc_x11")]
        sc_x1q = _sem("sc_x1q")
        sc_out = [_sem("sc_out0"), _sem("sc_out1")]
        gc_const = _sem("gc_const")
        gc_wq = [_sem("gc_wq0"), _sem("gc_wq1")]
        gc_wv = _sem("gc_wv")
        gc_we = [_sem("gc_we0"), _sem("gc_we1")]
        gc_wp = _sem("gc_wp")
        gc_gu = [_sem("gc_gu0"), _sem("gc_gu1")]
        gc_wd = [_sem("gc_wd0"), _sem("gc_wd1")]
        gc_dbg = _sem("gc_dbg")
        ALLSEMS = [tt, vv, ss, gg, sc_const, sc_x, sc_p1, sc_p2l, sc_p2s,
                   sc_xr[0], sc_xr[1], sc_x1[0], sc_x1[1], sc_x1q,
                   sc_out[0], sc_out[1], gc_const, gc_wq[0], gc_wq[1], gc_wv,
                   gc_we[0], gc_we[1], gc_wp, gc_gu[0], gc_gu[1],
                   gc_wd[0], gc_wd[1], gc_dbg]

        P = Prog(nc)

        ones_bf = e(nc.sbuf_tensor("ones_bf", [128, 128], BF16))
        ones_f32 = e(nc.sbuf_tensor("ones_f32", [1, 128], F32))
        onec_f32 = e(nc.sbuf_tensor("onec_f32", [128, 1], F32))
        id_bf = e(nc.sbuf_tensor("id_bf", [128, 128], BF16))
        j_bf = e(nc.sbuf_tensor("j_bf", [128, 128], BF16))
        mask_sb = e(nc.sbuf_tensor("mask_sb", [128, 128], F32))
        eps_sb = e(nc.sbuf_tensor("eps_sb", [128, 1], F32))
        wb_sb = e(nc.sbuf_tensor("wb_sb", [128, 24], F32))
        wbq_sb = e(nc.sbuf_tensor("wbq_sb", [128, 8], F32))
        wbv_bf = e(nc.sbuf_tensor("wbv_bf", [1, D], BF16))
        negmh = e(nc.sbuf_tensor("negmh", [128, 1], F32))
        wstat = e(nc.sbuf_tensor("wstat", [1, 8], F32))
        swg_b = e(nc.sbuf_tensor("swg_b", [128, 3], F32))
        mwg_b = e(nc.sbuf_tensor("mwg_b", [128, 3], F32))
        stO = ExitStack()
        stQKV = ExitStack()
        o_sb = stO.enter_context(nc.sbuf_tensor("o_sb", [128, NT, D], BF16))
        q_sb = stQKV.enter_context(nc.sbuf_tensor("q_sb", [128, 8, S], BF16))
        k_sb = stQKV.enter_context(nc.sbuf_tensor("k_sb", [128, 8, S], BF16))
        v_sb = stQKV.enter_context(nc.sbuf_tensor("v_sb", [128, NT, H * 65], BF16))

        P.op("sync", lambda E: E.dma_start(out=wb_sb[:], in_=wb_pc), sc_const, 16)
        a0_ds = P.op("sync", lambda E: E.dma_start(out=mask_sb[:], in_=mask_in), sc_const, 16)
        P.op("gpsimd", lambda E: E.dma_start(out=id_bf[:], in_=id_in), gc_const, 16)
        P.op("gpsimd", lambda E: E.dma_start(out=j_bf[:], in_=j_in), gc_const, 16)
        a0_dg = P.op("gpsimd", lambda E: E.dma_start(out=wbv_bf[:], in_=wbv_row), gc_const, 16)
        for _pi in range(pad_instr):
            P.op("vector", lambda E: E.memset(eps_sb[:], 1e-5))
        if pad_instr:
            with nc.psum_tensor("ps_pad", [128, 128], F32) as ps_pad:
                P.wait("tensor", gc_const, a0_dg)
                for _pi in range(pad_instr):
                    P.op("tensor", lambda E: E.matmul(ps_pad[:], id_bf[:], id_bf[:],
                                                      start=True, stop=True), tt)
        P.op("vector", lambda E: E.memset(eps_sb[:], 1e-5))
        P.op("vector", lambda E: E.memset(wstat[:], 0.0))
        P.op("vector", lambda E: E.memset(negmh[:], 0.0))
        P.op("vector", lambda E: E.memset(ones_bf[:], 1.0))
        P.op("vector", lambda E: E.memset(ones_f32[:], 1.0))
        P.op("vector", lambda E: E.memset(onec_f32[:], 1.0))
        P.op("vector", lambda E: E.memset(
            AP(v_sb[:].tensor, 64, [[NT * H * 65, 128], [H * 65, NT], [65, H]]), 1.0), vv)
        P.wait("vector", sc_const, a0_ds)
        v_wbq = P.op("vector", lambda E: E.tensor_scalar(
            wbq_sb[:], wb_sb[:, 0:8], 1.0 / 32.0, None, OP.mult), vv)

        with ExitStack() as stA:
            eA = stA.enter_context
            TBA = 2
            x_sb = eA(nc.sbuf_tensor("x_sb", [128, 2, TBA, D], F32))
            stats = eA(nc.sbuf_tensor("stats", [128, TBA, 12], F32))
            mv = eA(nc.sbuf_tensor("mv", [128, TBA, 2], F32))
            lnv = eA(nc.sbuf_tensor("lnv", [128, TBA], F32))
            rstd = eA(nc.sbuf_tensor("rstd", [128, TBA], F32))
            xn_bf = eA(nc.sbuf_tensor("xn_bf", [128, TBA, D], BF16))
            xn_fm = eA(nc.sbuf_tensor("xn_fm", [128, 8, S], BF16))
            wqkv_o = eA(nc.sbuf_tensor("wqkv_o", [128, 2, 8, 128], BF16))
            wv_bf = eA(nc.sbuf_tensor("wv_bf", [128, 8, D], BF16))
            ps_tr = [eA(nc.psum_tensor(f"ps_tr{i}", [128, 512], BF16)) for i in range(2)]
            ps_qk = [eA(nc.psum_tensor(f"ps_qk{i}", [128, 512], F32)) for i in range(2)]
            ps_v = [eA(nc.psum_tensor(f"ps_v{i}", [128, D], F32)) for i in range(2)]

            P.wait("tensor", gc_const, a0_dg)

            NTB = NT // TBA
            xld_sems = [sc_x, sc_x1q]
            tr_done = [0, 0]
            x_free = [0, 0]
            d_xa = [0, 0]
            tb_done = [0] * NTB
            for tb in range(NTB + 1):
                if tb < NTB:
                    tbp = tb % 2
                    P.wait("sync", vv, x_free[tbp])
                    d_xa[tbp] = P.op("sync", lambda E, tb=tb, tbp=tbp: E.dma_start(
                        out=x_sb[:, tbp, :, :],
                        in_=AP(x_in.tensor, tb * TBA * 128 * D, [[D, 128], [128 * D, TBA], [1, D]])),
                        xld_sems[tbp], 16)
                if tb == 0:
                    continue
                tb_ = tb - 1
                tbp = tb_ % 2
                P.wait("vector", xld_sems[tbp], d_xa[tbp])
                for t in range(TBA):
                    P.op("vector", lambda E, t=t, tbp=tbp: E.bn_stats(stats[:, t, 0:6], x_sb[:, tbp, t, 0:512]))
                    last = P.op("vector", lambda E, t=t, tbp=tbp: E.bn_stats(stats[:, t, 6:12], x_sb[:, tbp, t, 512:1024]), vv)
                P.wait("vector", vv, last)
                for t in range(TBA):
                    agg = P.op("vector", lambda E, t=t: E.bn_aggr(mv[:, t, :], stats[:, t, :]), vv)
                P.wait("scalar", vv, agg)
                ln1 = P.op("scalar", lambda E: E.activation(lnv[:], mv[:, :, 1], AF.Ln, bias=eps_sb[:], scale=1.0), ss)
                P.wait("scalar", ss, ln1)
                rs = P.op("scalar", lambda E: E.activation(rstd[:], lnv[:], AF.Exp, scale=-0.5), ss)
                P.wait("vector", ss, rs)
                for t in range(TBA):
                    x_free[tbp] = P.op("vector", lambda E, t=t, tbp=tbp: E.tensor_scalar(
                        xn_bf[:, t, :], x_sb[:, tbp, t, :], mv[:, t, 0:1], rstd[:, t:t + 1],
                        OP.subtract, OP.mult), vv)
                for c in range(8):
                    par = c % 2
                    P.wait("tensor", vv, x_free[tbp])
                    P.wait("tensor", vv, tr_done[par])
                    for t in range(TBA):
                        tp = P.op("tensor", lambda E, c=c, t=t, par=par, tbp=tbp: E.transpose(
                            ps_tr[par][:, t * 128:(t + 1) * 128],
                            xn_bf[:, t, c * 128:(c + 1) * 128], id_bf[:]), tt)
                    P.wait("vector", tt, tp)
                    tr_done[par] = P.op("vector", lambda E, c=c, tb_=tb_, par=par: E.tensor_copy(
                        xn_fm[:, c, tb_ * TBA * 128:(tb_ + 1) * TBA * 128], ps_tr[par][:, 0:TBA * 128]), vv)
                tb_done[tb_] = max(tr_done)

            if b_dbg:
                P.wait("gpsimd", vv, max(tr_done))
                P.wait("gpsimd", gc_dbg, gc_dbg.n)
                _dmp = P.op("gpsimd", lambda E: E.dma_start(out=dbg["dbg_xn"], in_=xn_fm[:].rearrange("p a b -> p (a b)")), gc_dbg, 16)
                for _e in ("tensor", "vector", "scalar", "sync"):
                    P.wait(_e, gc_dbg, gc_dbg.n)
            P.wait("scalar", sc_const, a0_ds)
            P.wait("scalar", vv, v_wbq)
            w_free = [0, 0]
            qk_ev = [0, 0]
            NF = S // 512
            for o in range(16):
                par = o % 2
                P.wait("gpsimd", tt, w_free[par])
                d_w = P.op("gpsimd", lambda E, o=o, par=par: E.dma_start(
                    out=wqkv_o[:, par, :, :],
                    in_=AP(wqkvT.tensor, o * 128, [[3 * D, 128], [128 * 3 * D, 8], [1, 128]])), gc_wq[par], 16)
                P.wait("tensor", gc_wq[par], d_w)
                for f in range(NF):
                    fpar = (o * NF + f) % 2
                    if o == 0:
                        P.wait("tensor", vv, tb_done[min(NTB - 1, 2 * f + 1)])
                    P.wait("tensor", ss, qk_ev[fpar])
                    for kc in range(8):
                        mm = P.op("tensor", lambda E, par=par, kc=kc, f=f, fpar=fpar: E.matmul(
                            ps_qk[fpar][:], wqkv_o[:, par, kc, :],
                            xn_fm[:, kc, f * 512:(f + 1) * 512],
                            start=(kc == 0), stop=(kc == 7)), tt)
                    if f == NF - 1:
                        w_free[par] = mm
                    P.wait("scalar", tt, mm)
                    if o < 8:
                        qk_ev[fpar] = P.op("scalar", lambda E, o=o, f=f, fpar=fpar: E.activation(
                            q_sb[:, o, f * 512:(f + 1) * 512], ps_qk[fpar][:], AF.Identity,
                            bias=wbq_sb[:, o:o + 1], scale=1.0 / 32.0), ss)
                    else:
                        qk_ev[fpar] = P.op("scalar", lambda E, o=o, f=f, fpar=fpar: E.activation(
                            k_sb[:, o - 8, f * 512:(f + 1) * 512], ps_qk[fpar][:], AF.Identity,
                            bias=wb_sb[:, o:o + 1], scale=1.0), ss)

            P.wait("gpsimd", tt, max(w_free))
            d_wv = P.op("gpsimd", lambda E: E.dma_start(
                out=wv_bf[:],
                in_=AP(wqkvT.tensor, 2048, [[3 * D, 128], [128 * 3 * D, 8], [1, D]])), gc_wv, 16)
            P.wait("tensor", gc_wv, d_wv)
            v_ev = [0, 0]
            for t in range(NT):
                par = t % 2
                P.wait("tensor", vv, v_ev[par])
                for fh in range(2):
                    P.op("tensor", lambda E, fh=fh, par=par: E.matmul(
                        ps_v[par][:, fh * 512:(fh + 1) * 512], ones_bf[0:1, :],
                        wbv_bf[:, fh * 512:(fh + 1) * 512], start=True, stop=False))
                for kc in range(8):
                    for fh in range(2):
                        mm = P.op("tensor", lambda E, t=t, kc=kc, fh=fh, par=par: E.matmul(
                            ps_v[par][:, fh * 512:(fh + 1) * 512],
                            xn_fm[:, kc, t * 128:(t + 1) * 128],
                            wv_bf[:, kc, fh * 512:(fh + 1) * 512],
                            start=False, stop=(kc == 7)), tt)
                P.wait("vector", tt, mm)
                v_ev[par] = P.op("vector", lambda E, t=t, par=par: E.tensor_copy(
                    AP(v_sb[:].tensor, t * (H * 65), [[NT * H * 65, 128], [65, H], [1, 64]]),
                    AP(ps_v[par][:].tensor, 0, [[D, 128], [64, H], [1, 64]])), vv)

        P.barrier(ALLSEMS)
        P.barrier(ALLSEMS)
        if b_dbg:
            P.wait("gpsimd", gc_dbg, gc_dbg.n)
            _dmp = P.op("gpsimd", lambda E: E.dma_start(out=dbg["dbg_q"], in_=q_sb[:].rearrange("p a b -> p (a b)")), gc_dbg, 16)
            for _e in ("tensor", "vector", "scalar", "sync"):
                P.wait(_e, gc_dbg, gc_dbg.n)
            P.wait("gpsimd", gc_dbg, gc_dbg.n)
            _dmp = P.op("gpsimd", lambda E: E.dma_start(out=dbg["dbg_k"], in_=k_sb[:].rearrange("p a b -> p (a b)")), gc_dbg, 16)
            for _e in ("tensor", "vector", "scalar", "sync"):
                P.wait(_e, gc_dbg, gc_dbg.n)
            P.wait("gpsimd", gc_dbg, gc_dbg.n)
            _dmp = P.op("gpsimd", lambda E: E.dma_start(out=dbg["dbg_v"], in_=v_sb[:].rearrange("p a b -> p (a b)")), gc_dbg, 16)
            for _e in ("tensor", "vector", "scalar", "sync"):
                P.wait(_e, gc_dbg, gc_dbg.n)
        with ExitStack() as stB:
            eB = stB.enter_context
            wext = eB(nc.sbuf_tensor("wext", [128, 2, WEXT], BF16))
            e_sb = eB(nc.sbuf_tensor("e_sb", [128, 2, WEXT], BF16))
            p_raw = eB(nc.sbuf_tensor("p_raw", [128, 4, 1024], F32))
            p_sb = eB(nc.sbuf_tensor("p_sb", [128, 4, 1024], BF16))
            recl = eB(nc.sbuf_tensor("recl", [128, 2, 4], F32))
            wchunk = eB(nc.sbuf_tensor("wchunk", [128, 2, 1024], F32))
            wabs = eB(nc.sbuf_tensor("wabs", [128, 2, 1024], F32))
            wsum_sb = eB(nc.sbuf_tensor("wsum_sb", [1, 512], F32))
            ps_st = [eB(nc.psum_tensor(f"ps_st{i}", [128, 1024], F32)) for i in range(2)]
            ps_o = [eB(nc.psum_tensor(f"ps_o{i}", [128, 260], F32)) for i in range(2)]
            ps_ws = eB(nc.psum_tensor("ps_ws", [1, 512], F32))

            P.wait("scalar", vv, v_wbq)

            w_srcs = [wgT, wuT, wdT]
            p1_units = [(wi, cc, hf) for wi in range(3) for cc in range(8) for hf in range(4)]
            p1 = {"dma": [0, 0], "abs": [0, 0], "mm": 0, "mm2": [0, 0], "ev": 0, "wsum_ev": 0}
            p1sems = [sc_p1, sc_x]

            def p1_issue_dma(u):
                wi, cc, hf = p1_units[u]
                wsrc = w_srcs[wi]
                upar = u % 2
                P.wait("sync", vv, p1["abs"][upar])

                def mk(E, wsrc=wsrc, cc=cc, hf=hf, upar=upar):
                    if wsrc is wdT:
                        src = AP(wsrc.tensor, (cc * 512 + hf * 128) * D, [[D, 128], [1, D]])
                        return E.dma_start(out=wchunk[:, upar, :], in_=src)
                    src = AP(wsrc.tensor, cc * 128 * INNER + hf * 1024, [[INNER, 128], [1, 1024]])
                    return E.dma_start(out=wchunk[:, upar, :], in_=src)
                p1["dma"][upar] = P.op("sync", mk, p1sems[upar], 16)

            def p1_abs(u):
                upar = u % 2
                P.wait("vector", p1sems[upar], p1["dma"][upar])
                P.wait("vector", tt, p1["mm2"][upar])
                ab = P.op("vector", lambda E, upar=upar: E.scalar_tensor_tensor(
                    wabs[:, upar, :], wchunk[:, upar, :], -1.0, wchunk[:, upar, :],
                    OP.mult, OP.max), vv)
                p1["abs"][upar] = ab

            def p1_unit_compute(u):
                wi = u // 32
                upar = u % 2
                P.wait("tensor", vv, p1["abs"][upar])
                if u % 32 == 0:
                    P.wait("tensor", vv, p1["wsum_ev"])
                for f in range(2):
                    mm = P.op("tensor", lambda E, f=f, u=u, upar=upar: E.matmul(
                        ps_ws[:], onec_f32[:], wabs[:, upar, f * 512:(f + 1) * 512],
                        start=(u % 32 == 0 and f == 0), stop=(u % 32 == 31 and f == 1)), tt)
                p1["mm"] = mm
                p1["mm2"][upar] = mm
                if u % 32 == 31:
                    P.wait("vector", tt, p1["mm"])
                    ev = P.op("vector", lambda E: E.tensor_copy(wsum_sb[:], ps_ws[:]), vv)
                    p1["wsum_ev"] = ev
                    P.wait("vector", vv, ev)
                    wdt2 = 256
                    while wdt2 >= 1:
                        sm = P.op("vector", lambda E, w2=wdt2: E.tensor_tensor(
                            wsum_sb[:, 0:w2], wsum_sb[:, 0:w2], wsum_sb[:, w2:2 * w2], OP.add), vv)
                        P.wait("vector", vv, sm)
                        wdt2 //= 2
                    sm = P.op("vector", lambda E, wi=wi: E.tensor_copy(
                        wstat[:, wi:wi + 1], wsum_sb[:, 0:1]), vv)
                    P.wait("vector", vv, sm)
                    mc = P.op("vector", lambda E, wi=wi: E.tensor_scalar(
                        wstat[:, wi:wi + 1], wstat[:, wi:wi + 1], 1.0 / float(D * INNER), 1e-5,
                        OP.mult, OP.max), vv)
                    P.wait("vector", vv, mc)
                    p1["ev"] = P.op("vector", lambda E, wi=wi: E.reciprocal(
                        wstat[:, 3 + wi:4 + wi], wstat[:, wi:wi + 1]), vv)

            p1_prog = {"mm": 0, "abs": 0, "issued": 0, "quads": 0}
            TOTQ = H * (NP // 2)

            def p1_tick():
                p1_prog["quads"] += 1
                base = p1_prog["quads"] * 96 // TOTQ
                while p1_prog["mm"] < base:
                    while p1_prog["abs"] <= p1_prog["mm"] + 1 and p1_prog["abs"] < 96:
                        p1_abs(p1_prog["abs"])
                        p1_prog["abs"] += 1
                        if p1_prog["issued"] < 96:
                            p1_issue_dma(p1_prog["issued"])
                            p1_prog["issued"] += 1
                    p1_unit_compute(p1_prog["mm"])
                    p1_prog["mm"] += 1

            p1_issue_dma(0)
            p1_issue_dma(1)
            p1_prog["issued"] = 2

            wx_free = [0, 0]
            wx_ready = [0, 0]
            st_free = [0, 0]
            pv_done = [0, 0, 0, 0]
            pmul = [0, 0, 0, 0]
            e_used = [0, 0]
            e_ready = [0, 0]
            fifo = []
            o_ev = [0, 0]
            gpar = [0]
            b_ss, b_vv = ss.n, vv.n

            def emit_head(h):
                par = h % 2
                P.wait("gpsimd", tt, wx_free[par])
                P.wait("gpsimd", vv, wx_ready[par])
                d_w = P.op("gpsimd", lambda E, h=h, par=par: E.dma_start(
                    out=wext[:, par, :],
                    in_=AP(rpbmt.tensor, h * R + (S - 512), [[1, 128], [1, WEXT]])), gc_we[par], 16)
                P.wait("vector", gc_we[par], d_w)
                P.wait("vector", tt, wx_free[par])
                P.op("vector", lambda E, par=par: E.memset(wext[:, par, 0:384], NEG))
                wx_ready[par] = P.op("vector", lambda E, par=par: E.tensor_tensor(
                    wext[:, par, 384:512], wext[:, par, 384:512], mask_sb[:], OP.add), vv)
                nE = (WEXT + 511) // 512
                for ei in range(nE):
                    w0 = ei * 512
                    w1 = min(WEXT, w0 + 512)
                    mypar = gpar[0] % 2
                    gpar[0] += 1
                    P.wait("tensor", ss, st_free[mypar])
                    if ei == 0:
                        P.wait("tensor", vv, wx_ready[par])
                    jm = P.op("tensor", lambda E, w0=w0, w1=w1, par=par, mypar=mypar: E.matmul(
                        ps_st[mypar][:, 0:w1 - w0], j_bf[:], wext[:, par, w0:w1],
                        start=True, stop=True), tt)
                    if ei == nE - 1:
                        wx_free[par] = jm
                    P.wait("scalar", tt, jm)
                    if ei == 0:
                        P.wait("scalar", vv, e_used[par])
                    st_free[mypar] = P.op("scalar", lambda E, w0=w0, w1=w1, par=par,
                                          mypar=mypar: E.activation(
                        e_sb[:, par, w0:w1], ps_st[mypar][:, 0:w1 - w0], AF.Exp,
                        bias=negmh[:], scale=1.0), ss)
                e_ready[par] = st_free[(gpar[0] - 1) % 2]

                first_use = True
                hh, hc = 64 * (h % 2), h // 2
                for qd in range(NP // 2):
                    nblocks = 4 * qd + 4
                    groups = [[b0, b0 + 1] for b0 in range(0, nblocks, 2)]
                    qdpar = qd % 2
                    pv_first = True
                    for grp in groups:
                        m2 = gpar[0] % 2
                        m3 = gpar[0] % 4
                        gpar[0] += 1
                        if first_use:
                            P.wait("tensor", ss, b_ss)
                            P.wait("tensor", vv, b_vv)
                            first_use = False
                        P.wait("tensor", ss, st_free[m2])
                        for bi, kc in enumerate(grp):
                            qm = P.op("tensor", lambda E, kc=kc, bi=bi, m2=m2, qd=qd,
                                      hh=hh, hc=hc: E.matmul(
                                ps_st[m2][:, bi * 512:(bi + 1) * 512],
                                k_sb[hh:hh + 64, hc, kc * 128:(kc + 1) * 128],
                                q_sb[hh:hh + 64, hc, qd * 512:(qd + 1) * 512],
                                start=True, stop=True), tt)
                        P.wait("scalar", tt, qm)
                        P.wait("scalar", vv, pmul[m3])
                        st_free[m2] = P.op("scalar", lambda E, m2=m2, m3=m3: E.activation(
                            p_raw[:, m3, :], ps_st[m2][:], AF.Exp,
                            bias=negmh[:], scale=1.0), ss)
                        while len(fifo) >= 3:
                            fifo.pop(0)()
                        P.wait("vector", ss, st_free[m2])
                        P.wait("vector", ss, e_ready[par])
                        P.wait("vector", tt, pv_done[m3])
                        for bi, kc in enumerate(grp):
                            off = 384 + (4 * qd - kc) * 128
                            pmul[m3] = P.op("vector", lambda E, off=off, bi=bi, par=par,
                                            m3=m3: E.tensor_tensor(
                                p_sb[:, m3, bi * 512:(bi + 1) * 512],
                                p_raw[:, m3, bi * 512:(bi + 1) * 512],
                                e_sb[:, par, off:off + 512], OP.mult), vv)
                        e_used[par] = pmul[m3]

                        def mk_pv(grp=grp, m3=m3, qdpar=qdpar, h=h, stt=pv_first,
                                  nblocks=nblocks, pm_val=pmul[m3]):
                            P.wait("tensor", vv, pm_val)
                            if stt:
                                P.wait("tensor", ss, o_ev[qdpar])
                            pm = 0
                            for bi, kc in enumerate(grp):
                                for qs in range(4):
                                    pm = P.op("tensor", lambda E, kc=kc, bi=bi, qs=qs, h=h,
                                              m3=m3, qdpar=qdpar,
                                              stt=(stt and bi == 0 and qs == 0),
                                              spp=(kc == nblocks - 1 and qs == 3): E.matmul(
                                        ps_o[qdpar][:, qs * 65:qs * 65 + 65],
                                        p_sb[:, m3, bi * 512 + qs * 128:bi * 512 + qs * 128 + 128],
                                        v_sb[:, kc, h * 65:h * 65 + 65],
                                        start=stt, stop=spp), tt)
                            pv_done[m3] = pm

                        fifo.append(mk_pv)
                        pv_first = False

                    def mk_evac(qd=qd, qdpar=qdpar, h=h, m3_last=(gpar[0] - 1) % 4):
                        P.wait("vector", tt, pv_done[m3_last])
                        P.wait("vector", ss, o_ev[qdpar])
                        rr = P.op("vector", lambda E, qdpar=qdpar: E.reciprocal(
                            recl[:, qdpar, :],
                            AP(ps_o[qdpar][:].tensor, 64, [[260, 128], [65, 4]])), vv)
                        P.wait("scalar", vv, rr)
                        for qs in range(4):
                            o_ev[qdpar] = P.op("scalar", lambda E, qs=qs, qd=qd, h=h,
                                               qdpar=qdpar: E.activation(
                                o_sb[:, 4 * qd + qs, h * 64:h * 64 + 64],
                                ps_o[qdpar][:, qs * 65:qs * 65 + 64],
                                AF.Identity, scale=recl[:, qdpar, qs:qs + 1]), ss)

                    fifo.append(mk_evac)
                    p1_tick()

            for h in range(H):
                emit_head(h)
            while fifo:
                fifo.pop(0)()

            b_end_ss = ss.n
            if b_dbg:
                P.wait("gpsimd", ss, b_end_ss)
                P.wait("gpsimd", gc_dbg, gc_dbg.n)
                _dmp = P.op("gpsimd", lambda E: E.dma_start(out=dbg["dbg_o"], in_=o_sb[:].rearrange("p a b -> p (a b)")), gc_dbg, 16)
                for _e in ("tensor", "vector", "scalar", "sync"):
                    P.wait(_e, gc_dbg, gc_dbg.n)

        stQKV.close()
        P.barrier(ALLSEMS)
        with ExitStack() as stB2:
            eB2 = stB2.enter_context
            wproj_bf = eB2(nc.sbuf_tensor("wproj_bf", [128, 8, D], BF16))
            o_fm = eB2(nc.sbuf_tensor("o_fm", [128, 8, 128], BF16))
            x_rld = eB2(nc.sbuf_tensor("x_rld", [128, 2, D], F32))
            xn1 = eB2(nc.sbuf_tensor("xn1", [128, 2, D], F32))
            ybuf = eB2(nc.sbuf_tensor("ybuf", [128, D], F32))
            zbuf = eB2(nc.sbuf_tensor("zbuf", [128, D], F32))
            x1buf = eB2(nc.sbuf_tensor("x1buf", [128, 2, D], F32))
            st2 = eB2(nc.sbuf_tensor("st2", [128, 12], F32))
            mv2 = eB2(nc.sbuf_tensor("mv2", [128, 2], F32))
            rs2 = eB2(nc.sbuf_tensor("rs2", [128, 1], F32))
            nb2 = eB2(nc.sbuf_tensor("nb2", [128, 1], F32))
            mva = eB2(nc.sbuf_tensor("mva", [128, 2, 2], F32))
            rsa = eB2(nc.sbuf_tensor("rsa", [128, 2], F32))
            nba = eB2(nc.sbuf_tensor("nba", [128, 2], F32))
            g_bc = eB2(nc.sbuf_tensor("g_bc", [128, D], F32))
            b_bc = eB2(nc.sbuf_tensor("b_bc", [128, D], F32))
            pb_bc = eB2(nc.sbuf_tensor("pb_bc", [128, D], F32))
            rows = eB2(nc.sbuf_tensor("rows", [1, 3 * D], F32))
            ps_tr2 = [eB2(nc.psum_tensor(f"ps_tr2{i}", [128, 512], BF16)) for i in range(2)]
            ps_pj = [eB2(nc.psum_tensor(f"ps_pj{i}", [128, 512], F32)) for i in range(2)]

            P.op("sync", lambda E: E.dma_start(out=rows[:, 0:D], in_=projb_row), sc_const, 16)
            P.op("sync", lambda E: E.dma_start(out=rows[:, D:2 * D], in_=lng_row), sc_const, 16)
            d_r2 = P.op("sync", lambda E: E.dma_start(out=rows[:, 2 * D:3 * D], in_=lnb_row), sc_const, 16)
            d_wp = P.op("gpsimd", lambda E: E.dma_start(
                out=wproj_bf[:],
                in_=AP(wprojT.tensor, 0, [[D, 128], [128 * D, 8], [1, D]])), gc_wp, 16)
            P.wait("tensor", sc_const, d_r2)
            P.wait("tensor", vv, p1["ev"])
            bc_dsts = [pb_bc, g_bc, b_bc]
            bc_ev = [0, 0]
            for bi3 in range(3):
                for fh in range(2):
                    P.wait("tensor", vv, bc_ev[fh])
                    bm = P.op("tensor", lambda E, bi3=bi3, fh=fh: E.matmul(
                        ps_pj[fh][:], ones_f32[:], rows[:, bi3 * D + fh * 512:bi3 * D + (fh + 1) * 512],
                        start=True, stop=True), tt)
                    P.wait("vector", tt, bm)
                    bcd = P.op("vector", lambda E, bi3=bi3, fh=fh: E.tensor_copy(
                        bc_dsts[bi3][:, fh * 512:(fh + 1) * 512], ps_pj[fh][:]), vv)
                    bc_ev[fh] = bcd
            P.wait("tensor", vv, bc_ev[0])
            bm = P.op("tensor", lambda E: E.matmul(ps_pj[0][:, 0:8], ones_f32[:], wstat[:],
                                                   start=True, stop=True), tt)
            P.wait("vector", tt, bm)
            P.op("vector", lambda E: E.tensor_copy(mwg_b[:], ps_pj[0][:, 0:3]))
            swb = P.op("vector", lambda E: E.tensor_copy(swg_b[:], ps_pj[0][:, 3:6]), vv)
            if b_dbg:
                P.wait("gpsimd", vv, swb)
                P.wait("gpsimd", gc_dbg, gc_dbg.n)
                _dmp = P.op("gpsimd", lambda E: E.dma_start(out=dbg["dbg_wm"], in_=wstat[:]), gc_dbg, 16)
                for _e in ("tensor", "vector", "scalar", "sync"):
                    P.wait(_e, gc_dbg, gc_dbg.n)


            tr2_ev = [0, 0]
            ypj = [0, 0]
            x1_st = [0, 0]
            x_free = [0, 0]
            x_free_s = [0, 0]
            s1_apply = [0, 0]
            ytt = [0, 0]
            zap = 0
            x1ap = 0
            ztt2 = 0
            pm_box = [0]
            d_x = [0, 0]

            def b2_dma(qi):
                par = qi % 2
                P.wait("sync", vv, x_free[par])
                P.wait("sync", ss, x_free_s[par])
                d_x[par] = P.op("sync", lambda E, qi=qi, par=par: E.dma_start(
                    out=x_rld[:, par, :], in_=x_in[qi * 128:(qi + 1) * 128, :]), sc_xr[par], 16)

            def b2_s1(qi):
                par = qi % 2
                P.wait("vector", sc_xr[par], d_x[par])
                if qi == 0:
                    P.wait("vector", vv, bcd)
                P.op("vector", lambda E, par=par: E.bn_stats(st2[:, 0:6], x_rld[:, par, 0:512]))
                sb = P.op("vector", lambda E, par=par: E.bn_stats(st2[:, 6:12], x_rld[:, par, 512:1024]), vv)
                P.wait("vector", vv, sb)
                ag = P.op("vector", lambda E, par=par: E.bn_aggr(mva[:, par, :], st2[:]), vv)
                P.wait("scalar", vv, ag)
                lnx = P.op("scalar", lambda E, par=par: E.activation(
                    rsa[:, par:par + 1], mva[:, par, 1:2], AF.Ln, bias=eps_sb[:], scale=1.0), ss)
                P.wait("scalar", ss, lnx)
                rr = P.op("scalar", lambda E, par=par: E.activation(
                    rsa[:, par:par + 1], rsa[:, par:par + 1], AF.Exp, scale=-0.5), ss)
                P.wait("vector", ss, rr)
                nb = P.op("vector", lambda E, par=par: E.scalar_tensor_tensor(
                    nba[:, par:par + 1], mva[:, par, 0:1], -1.0, rsa[:, par:par + 1],
                    OP.mult, OP.mult), vv)
                P.wait("scalar", vv, nb)
                P.wait("scalar", vv, ytt[par])
                s1_apply[par] = P.op("scalar", lambda E, par=par: E.activation(
                    xn1[:, par, :], x_rld[:, par, :], AF.Identity,
                    bias=nba[:, par:par + 1], scale=rsa[:, par:par + 1]), ss)
                x_free_s[par] = s1_apply[par]

            def b2_pe(qi):
                P.wait("tensor", ss, b_end_ss)
                for cg in range(2):
                    tpp = cg % 2
                    P.wait("tensor", ss, tr2_ev[tpp])
                    P.wait("scalar", tt, pm_box[0])
                    for c4 in range(4):
                        c = cg * 4 + c4
                        tp = P.op("tensor", lambda E, qi=qi, c=c, c4=c4, tpp=tpp: E.transpose(
                            ps_tr2[tpp][:, c4 * 128:(c4 + 1) * 128],
                            o_sb[:, qi, c * 128:(c + 1) * 128], id_bf[:]), tt)
                    P.wait("scalar", tt, tp)
                    tr2_ev[tpp] = P.op("scalar", lambda E, cg=cg, tpp=tpp: E.activation(
                        o_fm[:].rearrange("p a b -> p (a b)")[:, cg * 512:(cg + 1) * 512],
                        ps_tr2[tpp][:], AF.Identity, scale=1.0), ss)
                P.wait("tensor", gc_wp, d_wp)
                P.wait("tensor", ss, max(tr2_ev))
                mm = 0
                for fh in range(2):
                    P.wait("tensor", vv, ypj[fh])
                    for kc in range(8):
                        mm = P.op("tensor", lambda E, fh=fh, kc=kc: E.matmul(
                            ps_pj[fh][:], o_fm[:, kc, :],
                            wproj_bf[:, kc, fh * 512:(fh + 1) * 512],
                            start=(kc == 0), stop=(kc == 7)), tt)
                pm_box[0] = mm
                return mm

            def b2_s2(qi, mm):
                nonlocal_ = {}
                par = qi % 2
                P.wait("vector", tt, mm)
                P.wait("vector", ss, zap)
                P.wait("vector", ss, s1_apply[par])
                for fh in range(2):
                    st_ = P.op("vector", lambda E, fh=fh: E.scalar_tensor_tensor(
                        ybuf[:, fh * 512:(fh + 1) * 512], ps_pj[fh][:], 0.0,
                        pb_bc[:, fh * 512:(fh + 1) * 512], OP.add, OP.add), vv)
                    P.wait("vector", vv, st_)
                    yt = P.op("vector", lambda E, fh=fh, par=par: E.tensor_tensor(
                        ybuf[:, fh * 512:(fh + 1) * 512], ybuf[:, fh * 512:(fh + 1) * 512],
                        xn1[:, par, fh * 512:(fh + 1) * 512], OP.add), vv)
                    ypj[fh] = yt
                ytt[par] = yt
                P.wait("vector", vv, yt)
                P.op("vector", lambda E: E.bn_stats(st2[:, 0:6], ybuf[:, 0:512]))
                sb = P.op("vector", lambda E: E.bn_stats(st2[:, 6:12], ybuf[:, 512:1024]), vv)
                P.wait("vector", vv, sb)
                ag = P.op("vector", lambda E: E.bn_aggr(mv2[:], st2[:]), vv)
                P.wait("scalar", vv, ag)
                lnx = P.op("scalar", lambda E: E.activation(rs2[:], mv2[:, 1:2], AF.Ln, bias=eps_sb[:], scale=1.0), ss)
                P.wait("scalar", ss, lnx)
                rr = P.op("scalar", lambda E: E.activation(rs2[:], rs2[:], AF.Exp, scale=-0.5), ss)
                P.wait("vector", ss, rr)
                nb = P.op("vector", lambda E: E.scalar_tensor_tensor(
                    nb2[:], mv2[:, 0:1], -1.0, rs2[:], OP.mult, OP.mult), vv)
                P.wait("scalar", vv, nb)
                P.wait("scalar", gg, ztt2)
                zap_ = P.op("scalar", lambda E: E.activation(
                    zbuf[:], ybuf[:], AF.Identity, bias=nb2[:], scale=rs2[:]), ss)
                P.wait("vector", ss, zap_)
                zt1 = P.op("vector", lambda E: E.tensor_tensor(zbuf[:], zbuf[:], g_bc[:], OP.mult), vv)
                P.wait("vector", vv, zt1)
                zt2 = P.op("vector", lambda E: E.tensor_tensor(zbuf[:], zbuf[:], b_bc[:], OP.add), vv)
                P.wait("vector", vv, zt2)
                P.op("vector", lambda E: E.bn_stats(st2[:, 0:6], zbuf[:, 0:512]))
                sb = P.op("vector", lambda E: E.bn_stats(st2[:, 6:12], zbuf[:, 512:1024]), vv)
                P.wait("vector", vv, sb)
                ag = P.op("vector", lambda E: E.bn_aggr(mv2[:], st2[:]), vv)
                P.wait("scalar", vv, ag)
                lnx = P.op("scalar", lambda E: E.activation(rs2[:], mv2[:, 1:2], AF.Ln, bias=eps_sb[:], scale=1.0), ss)
                P.wait("scalar", ss, lnx)
                rr = P.op("scalar", lambda E: E.activation(rs2[:], rs2[:], AF.Exp, scale=-0.5), ss)
                P.wait("vector", ss, rr)
                nb = P.op("vector", lambda E: E.scalar_tensor_tensor(
                    nb2[:], mv2[:, 0:1], -1.0, rs2[:], OP.mult, OP.mult), vv)
                P.wait("scalar", vv, nb)
                P.wait("scalar", sc_x1[par], x1_st[par])
                x1a = P.op("scalar", lambda E, par=par: E.activation(
                    x1buf[:, par, :], zbuf[:], AF.Identity, bias=nb2[:], scale=rs2[:]), ss)
                P.wait("vector", ss, x1a)
                xa = P.op("vector", lambda E, par=par: E.tensor_tensor(
                    x1buf[:, par, :], x1buf[:, par, :], x_rld[:, par, :], OP.add), vv)
                x_free[par] = xa
                P.wait("sync", vv, xa)
                x1_st[par] = P.op("sync", lambda E, qi=qi, par=par: E.dma_start(
                    out=x1_dram[qi * 128:(qi + 1) * 128, :], in_=x1buf[:, par, :]), sc_x1[par], 16)
                return zap_, x1a

            b2_dma(0)
            if NT > 1:
                b2_dma(1)
            b2_s1(0)
            for qi in range(NT):
                mm = b2_pe(qi)
                if qi + 1 < NT:
                    b2_s1(qi + 1)
                zap, x1ap = b2_s2(qi, mm)
                if qi + 2 < NT:
                    b2_dma(qi + 2)

            if b_dbg:
                P.wait("gpsimd", sc_x1[0], x1_st[0])
                P.wait("gpsimd", sc_x1[1], x1_st[1])
                P.wait("gpsimd", gc_dbg, gc_dbg.n)
                _dmp = P.op("gpsimd", lambda E: E.dma_start(out=dbg["dbg_x1"], in_=x1_dram), gc_dbg, 16)
                for _e in ("tensor", "vector", "scalar", "sync"):
                    P.wait(_e, gc_dbg, gc_dbg.n)

        stO.close()
        P.barrier(ALLSEMS)
        with ExitStack() as stC:
            eC = stC.enter_context
            wdq_sb = eC(nc.sbuf_tensor("wdq_sb", [128, 32, D], BF16))
            st3 = eC(nc.sbuf_tensor("st3", [128, 12], F32))
            mv3 = eC(nc.sbuf_tensor("mv3", [128, 2], F32))
            rs3 = eC(nc.sbuf_tensor("rs3", [128, 1], F32))
            sc1 = eC(nc.sbuf_tensor("sc1", [128, NT], F32))
            rsg = eC(nc.sbuf_tensor("rsg", [128, NT], F32))
            rsu = eC(nc.sbuf_tensor("rsu", [128, NT], F32))
            rsd = eC(nc.sbuf_tensor("rsd", [128, NT], F32))
            magic_c = eC(nc.sbuf_tensor("magic_c", [128, 1], F32))
            nmagic_c = eC(nc.sbuf_tensor("nmagic_c", [128, 1], F32))
            ps_t3 = [eC(nc.psum_tensor(f"ps_t3{i}", [128, 512], BF16)) for i in range(2)]
            stQ3 = ExitStack()
            stCG = ExitStack()
            xq3_fm = stQ3.enter_context(nc.sbuf_tensor("xq3_fm", [128, 8, S], BF16))
            wgf = stCG.enter_context(nc.sbuf_tensor("wgf", [128, 4, 512], F32))
            wuf = stCG.enter_context(nc.sbuf_tensor("wuf", [128, 4, 512], F32))
            wdf = stCG.enter_context(nc.sbuf_tensor("wdf", [128, 2, D], F32))
            wq_t = stCG.enter_context(nc.sbuf_tensor("wq_t", [128, 2048], F32))
            wq_c1 = stCG.enter_context(nc.sbuf_tensor("wq_c1", [128, 2048], BF16))
            wq_c2 = stCG.enter_context(nc.sbuf_tensor("wq_c2", [128, 2048], BF16))
            wgq_sb = stCG.enter_context(nc.sbuf_tensor("wgq_sb", [128, 2, 8, 512], BF16))
            wuq_sb = stCG.enter_context(nc.sbuf_tensor("wuq_sb", [128, 2, 8, 512], BF16))
            g_act = stCG.enter_context(nc.sbuf_tensor("g_act", [128, 2, 512], BF16))
            prod_sb = stCG.enter_context(nc.sbuf_tensor("prod_sb", [128, 2, 512], F32))
            ps_g = [stCG.enter_context(nc.psum_tensor(f"ps_g{i}", [128, 512], F32)) for i in range(2)]
            ps_u = [stCG.enter_context(nc.psum_tensor(f"ps_u{i}", [128, 512], F32)) for i in range(2)]

            P.op("vector", lambda E: E.memset(magic_c[:], 6755399441055744.0))
            mgd = P.op("vector", lambda E: E.memset(nmagic_c[:], -6755399441055744.0), vv)

            sc_ps = [sc_p2l, sc_p2s]
            wsems = {"g": gc_gu, "u": gc_wq, "d": gc_wd}
            wsrcs = {"g": wgT, "u": wuT}
            wbufs = {"g": wgf, "u": wuf}
            wqsb = {"g": wgq_sb, "u": wuq_sb}
            widx = {"g": 0, "u": 1, "d": 2}
            wbuf_free = {"g": 0, "u": 0, "d": 0}
            wdma = {("g", 0): 0, ("g", 1): 0, ("u", 0): 0, ("u", 1): 0,
                    ("d", 0): 0, ("d", 1): 0}
            qfree = [0]
            qcons = [0]
            quv = {"g": [0, 0], "u": [0, 0]}
            gu_used = [0, 0]

            def emit_wdma(ic):
                for mat in ("g", "u"):
                    P.wait("gpsimd", ss, wbuf_free[mat])
                    wdma[(mat, 0)] = P.op(
                        "gpsimd", lambda E, mat=mat, ic=ic: E.dma_start(
                            out=wbufs[mat][:],
                            in_=AP(wsrcs[mat].tensor, ic * 512,
                                   [[INNER, 128], [128 * INNER, 4], [1, 512]])),
                        wsems[mat][0], 16)
                P.wait("gpsimd", ss, wbuf_free["d"])
                wdma[("d", 0)] = P.op("gpsimd", lambda E, ic=ic: E.dma_start(
                    out=wdf[:],
                    in_=AP(wdT.tensor, ic * 512 * D, [[D, 128], [128 * D, 2], [1, D]])),
                    gc_wd[0], 16)

            def emit_qunit(mat, ic, hf):
                icpar = ic % 2
                src = wdf[:] if mat == "d" else wbufs[mat][:]
                dsem, dval = wsems[mat][hf], wdma[(mat, hf)]
                P.wait("scalar", dsem, dval)
                P.wait("scalar", vv, qfree[0])
                P.wait("scalar", vv, swb)
                tw = P.op("scalar", lambda E, src=src, mat=mat: E.activation(
                    wq_t[:], src, AF.Identity, scale=swg_b[:, widx[mat]:widx[mat] + 1]), ss)
                if hf == 0:
                    P.wait("gpsimd", ss, tw)
                    if mat == "d":
                        wdma[("d", 1)] = P.op("gpsimd", lambda E, ic=ic: E.dma_start(
                            out=wdf[:],
                            in_=AP(wdT.tensor, (ic * 512 + 256) * D,
                                   [[D, 128], [128 * D, 2], [1, D]])),
                            gc_wd[1], 16)
                    else:
                        wdma[(mat, 1)] = P.op(
                            "gpsimd", lambda E, mat=mat, ic=ic: E.dma_start(
                                out=wbufs[mat][:],
                                in_=AP(wsrcs[mat].tensor, 512 * INNER + ic * 512,
                                       [[INNER, 128], [128 * INNER, 4], [1, 512]])),
                            wsems[mat][1], 16)
                else:
                    wbuf_free[mat] = tw
                P.wait("vector", ss, tw)
                P.wait("vector", vv, qcons[0])
                P.op("vector", lambda E: E.tensor_scalar(wq_c1[:], wq_t[:], 0.5, None, OP.is_gt))
                cmp2 = P.op("vector", lambda E: E.tensor_scalar(wq_c2[:], wq_t[:], -0.5, None, OP.is_lt), vv)
                qfree[0] = cmp2
                P.wait("vector", vv, cmp2)
                if mat == "d":
                    dst = wdq_sb[:, 4 * ic + 2 * hf:4 * ic + 2 * hf + 2, :]
                else:
                    if hf == 0:
                        P.wait("vector", tt, gu_used[icpar])
                    dst = wqsb[mat][:, icpar, 4 * hf:4 * hf + 4, :]
                qq_ = P.op("vector", lambda E, dst=dst: E.tensor_tensor(
                    dst.rearrange("p a b -> p (a b)"), wq_c1[:], wq_c2[:], OP.subtract), vv)
                qcons[0] = qq_
                if mat != "d":
                    quv[mat][icpar] = qq_

            UNITS = [("g", 0), ("g", 1), ("u", 0), ("u", 1), ("d", 0), ("d", 1)]
            unit_ts = [1 + i * max(1, (NT - 2) // 6) for i in range(6)]

            x1t_used = [0, 0]
            xq3_done = [0] * NT
            i32_free = 0
            tr3_ev = [0, 0]
            emit_wdma(0)
            with ExitStack() as stC0:
                e0 = stC0.enter_context
                x1t = e0(nc.sbuf_tensor("x1t", [128, 2, D], F32))
                xn3 = e0(nc.sbuf_tensor("xn3", [128, D], F32))
                xq_i32a = e0(nc.sbuf_tensor("xq_i32a", [128, D], I32))
                d_bfa = e0(nc.sbuf_tensor("d_bfa", [128, D], F32))
                xq3_tm = e0(nc.sbuf_tensor("xq3_tm", [128, D], BF16))
                for t in range(NT):
                    par = t % 2
                    P.wait("sync", vv, x1t_used[par])
                    d_x1 = P.op("sync", lambda E, t=t, par=par: E.dma_start(
                        out=x1t[:, par, :], in_=x1_dram[t * 128:(t + 1) * 128, :]), sc_xr[par], 16)
                    P.wait("vector", sc_xr[par], d_x1)
                    if t == 0:
                        P.wait("vector", vv, swb)
                        P.wait("vector", vv, mgd)
                    P.op("vector", lambda E, par=par: E.bn_stats(st3[:, 0:6], x1t[:, par, 0:512]))
                    sb = P.op("vector", lambda E, par=par: E.bn_stats(st3[:, 6:12], x1t[:, par, 512:1024]), vv)
                    P.wait("vector", vv, sb)
                    ag = P.op("vector", lambda E: E.bn_aggr(mv3[:], st3[:]), vv)
                    P.wait("scalar", vv, ag)
                    lnx = P.op("scalar", lambda E: E.activation(rs3[:], mv3[:, 1:2], AF.Ln, bias=eps_sb[:], scale=1.0), ss)
                    P.wait("scalar", ss, lnx)
                    rr = P.op("scalar", lambda E: E.activation(rs3[:], rs3[:], AF.Exp, scale=-0.5), ss)
                    P.wait("vector", ss, rr)
                    xnd = P.op("vector", lambda E, par=par: E.tensor_scalar(
                        xn3[:], x1t[:, par, :], mv3[:, 0:1], rs3[:], OP.subtract, OP.mult), vv)
                    x1t_used[par] = xnd
                    P.wait("vector", vv, xnd)
                    am = P.op("vector", lambda E, t=t: E.reduce_max(
                        sc1[:, t:t + 1], xn3[:], axis=AX.X, apply_absolute_value=True), vv)
                    P.wait("vector", vv, am)
                    mc1 = P.op("vector", lambda E, t=t: E.tensor_scalar(
                        sc1[:, t:t + 1], sc1[:, t:t + 1], 1e-5, None, OP.max), vv)
                    P.wait("vector", vv, mc1)
                    P.op("vector", lambda E, t=t: E.scalar_tensor_tensor(
                        rsg[:, t:t + 1], sc1[:, t:t + 1], 1.0 / 127.0, mwg_b[:, 0:1], OP.mult, OP.mult))
                    ru = P.op("vector", lambda E, t=t: E.scalar_tensor_tensor(
                        rsu[:, t:t + 1], sc1[:, t:t + 1], 1.0 / 127.0, mwg_b[:, 1:2], OP.mult, OP.mult), vv)
                    P.wait("vector", vv, ru)
                    rc = P.op("vector", lambda E, t=t: E.reciprocal(sc1[:, t:t + 1], sc1[:, t:t + 1]), vv)
                    P.wait("vector", vv, rc)
                    s127 = P.op("vector", lambda E, t=t: E.tensor_scalar(
                        sc1[:, t:t + 1], sc1[:, t:t + 1], 127.0, None, OP.mult), vv)
                    P.wait("scalar", vv, s127)
                    P.wait("scalar", vv, i32_free)
                    qi32 = P.op("scalar", lambda E, t=t: E.activation(
                        xq_i32a[:], xn3[:], AF.Identity, scale=sc1[:, t:t + 1]), ss)
                    P.wait("vector", ss, qi32)
                    dd1 = P.op("vector", lambda E, t=t: E.scalar_tensor_tensor(
                        d_bfa[:], xn3[:], sc1[:, t:t + 1], xq_i32a[:], OP.mult, OP.subtract), vv)
                    P.wait("scalar", vv, dd1)
                    da = P.op("scalar", lambda E: E.activation(
                        d_bfa[:], d_bfa[:], AF.Identity, bias=magic_c[:], scale=1.0), ss)
                    P.wait("scalar", ss, da)
                    dd2b = P.op("scalar", lambda E: E.activation(
                        d_bfa[:], d_bfa[:], AF.Identity, bias=nmagic_c[:], scale=1.0), ss)
                    P.wait("vector", ss, dd2b)
                    i32_free = P.op("vector", lambda E: E.scalar_tensor_tensor(
                        xq3_tm[:], d_bfa[:], 1.0, xq_i32a[:], OP.mult, OP.add), vv)
                    P.wait("tensor", vv, i32_free)
                    for cg in range(2):
                        tpp = (2 * t + cg) % 2
                        P.wait("tensor", vv, tr3_ev[tpp])
                        for c4 in range(4):
                            c = cg * 4 + c4
                            tp = P.op("tensor", lambda E, c=c, c4=c4, tpp=tpp: E.transpose(
                                ps_t3[tpp][:, c4 * 128:(c4 + 1) * 128],
                                xq3_tm[:, c * 128:(c + 1) * 128], id_bf[:]), tt)
                        P.wait("vector", tt, tp)
                        tr3_ev[tpp] = P.op("vector", lambda E, t=t, cg=cg, tpp=tpp: E.tensor_copy(
                            AP(xq3_fm[:].tensor, cg * 4 * S + t * 128,
                               [[8 * S, 128], [S, 4], [1, 128]]),
                            ps_t3[tpp][:].rearrange("p (a b) -> p a b", a=4)), vv)
                    xq3_done[t] = tr3_ev[(2 * t + 1) % 2]
                    for ui, (mat, hf) in enumerate(UNITS):
                        if unit_ts[ui] == t + 1 or (t == NT - 1 and unit_ts[ui] > NT - 1):
                            emit_qunit(mat, 0, hf)

            silu_done = [0, 0]
            prod_done = [0, 0]
            prod_st = [0, 0]
            for ic in range(8):
                icpar = ic % 2
                if ic + 1 < 8:
                    emit_wdma(ic + 1)
                P.wait("tensor", vv, quv["g"][icpar])
                P.wait("tensor", vv, quv["u"][icpar])
                for t in range(NT):
                    tpar = t % 2
                    if ic == 0:
                        P.wait("tensor", vv, xq3_done[t])
                    P.wait("tensor", ss, silu_done[tpar])
                    P.wait("tensor", vv, prod_done[tpar])
                    for kc in range(8):
                        P.op("tensor", lambda E, kc=kc, t=t, icpar=icpar, tpar=tpar: E.matmul(
                            ps_g[tpar][:], xq3_fm[:, kc, t * 128:(t + 1) * 128],
                            wgq_sb[:, icpar, kc, :], start=(kc == 0), stop=(kc == 7)))
                    for kc in range(8):
                        mmu = P.op("tensor", lambda E, kc=kc, t=t, icpar=icpar, tpar=tpar: E.matmul(
                            ps_u[tpar][:], xq3_fm[:, kc, t * 128:(t + 1) * 128],
                            wuq_sb[:, icpar, kc, :], start=(kc == 0), stop=(kc == 7)), tt)
                    gu_used[icpar] = mmu
                    P.wait("scalar", tt, mmu)
                    P.wait("scalar", vv, prod_done[tpar])
                    silu_done[tpar] = P.op("scalar", lambda E, t=t, tpar=tpar: E.activation(
                        g_act[:, tpar, :], ps_g[tpar][:], AF.Silu, scale=rsg[:, t:t + 1]), ss)
                    P.wait("vector", ss, silu_done[tpar])
                    P.wait("vector", sc_ps[tpar], prod_st[tpar])
                    prod_done[tpar] = P.op("vector", lambda E, tpar=tpar: E.tensor_tensor(
                        prod_sb[:, tpar, :], g_act[:, tpar, :], ps_u[tpar][:], OP.mult), vv)
                    P.wait("sync", vv, prod_done[tpar])
                    prod_st[tpar] = P.op("sync", lambda E, ic=ic, t=t, tpar=tpar: E.dma_start(
                        out=AP(gu_dram.tensor, t * 128 * INNER + ic * 512, [[INNER, 128], [1, 512]]),
                        in_=prod_sb[:, tpar, :]), sc_ps[tpar], 16)
                    if ic + 1 < 8:
                        for ui, (mat, hf) in enumerate(UNITS):
                            if unit_ts[ui] == t + 1 or (t == NT - 1 and unit_ts[ui] > NT - 1):
                                emit_qunit(mat, ic + 1, hf)

            stCG.close()
            stQ3.close()
            P.barrier(ALLSEMS)
            with ExitStack() as stCD:
                eD = stCD.enter_context
                gu_t = eD(nc.sbuf_tensor("gu_t", [128, 2, INNER], F32))
                x1t2 = eD(nc.sbuf_tensor("x1t2", [128, 2, D], F32))
                xqgu_tm = eD(nc.sbuf_tensor("xqgu_tm", [128, 2, INNER], BF16))
                xqgu_fm = eD(nc.sbuf_tensor("xqgu_fm", [128, 2, 32, 128], BF16))
                xq_i32b = eD(nc.sbuf_tensor("xq_i32b", [128, INNER], I32))
                d_bfb = eD(nc.sbuf_tensor("d_bfb", [128, INNER], F32))
                mpt = eD(nc.sbuf_tensor("mpt", [128, NT, 3], F32))
                ffo = eD(nc.sbuf_tensor("ffo", [128, 2, D], F32))
                outb = eD(nc.sbuf_tensor("outb", [128, 2, D], F32))
                ps_d = [eD(nc.psum_tensor(f"ps_d{i}", [128, D], F32)) for i in range(2)]

                gu_free = [0, 0]
                gu_free_s = [0, 0]
                gu_free_g = [0, 0]
                x1_free = [0, 0]
                ffo_used = [0, 0]
                ffo_ev = [0, 0]
                out_st = [0, 0]
                d_gu = [0, 0]
                d_x1 = [0, 0]
                dd1_d = [0, 0]
                dd2_d = [0, 0]
                comb_d = [0, 0]
                qi32_d = [0, 0]
                copy_d = {}
                tr_d = {}
                mm_d = [0, 0]

                def cd_dma_gu(t):
                    par = t % 2
                    P.wait("sync", vv, gu_free[par])
                    P.wait("sync", ss, gu_free_s[par])
                    d_gu[par] = P.op("sync", lambda E, t=t, par=par: E.dma_start(
                        out=gu_t[:, par, :],
                        in_=AP(gu_dram.tensor, t * 128 * INNER, [[INNER, 128], [1, INNER]])), sc_x1[par], 16)

                def cd_dma_x1(t):
                    par = t % 2
                    P.wait("sync", vv, x1_free[par])
                    d_x1[par] = P.op("sync", lambda E, t=t, par=par: E.dma_start(
                        out=x1t2[:, par, :], in_=x1_dram[t * 128:(t + 1) * 128, :]), sc_xr[par], 16)

                def cd_quant(t):
                    par = t % 2
                    P.wait("vector", sc_x1[par], d_gu[par])
                    P.wait("vector", vv, mgd)
                    am = P.op("vector", lambda E, t=t, par=par: E.reduce_max(
                        mpt[:, t, 0:1], gu_t[:, par, :], axis=AX.X, apply_absolute_value=True), vv)
                    gu_free_g[par] = 0
                    P.wait("vector", vv, am)
                    mc2 = P.op("vector", lambda E, t=t: E.tensor_scalar(
                        mpt[:, t, 1:2], mpt[:, t, 0:1], rsu[:, t:t + 1], 1e-5, OP.mult, OP.max), vv)
                    P.wait("vector", vv, mc2)
                    P.op("vector", lambda E, t=t: E.scalar_tensor_tensor(
                        rsd[:, t:t + 1], mpt[:, t, 1:2], 1.0 / 127.0, mwg_b[:, 2:3], OP.mult, OP.mult))
                    rc = P.op("vector", lambda E, t=t: E.reciprocal(mpt[:, t, 2:3], mpt[:, t, 1:2]), vv)
                    P.wait("vector", vv, rc)
                    v2d = P.op("vector", lambda E, t=t: E.scalar_tensor_tensor(
                        mpt[:, t, 2:3], mpt[:, t, 2:3], 127.0, rsu[:, t:t + 1], OP.mult, OP.mult), vv)
                    P.wait("scalar", vv, v2d)
                    P.wait("scalar", vv, max(comb_d))
                    qi32_d[par] = P.op("scalar", lambda E, t=t, par=par: E.activation(
                        xq_i32b[:], gu_t[:, par, :], AF.Identity, scale=mpt[:, t, 2:3]), ss)
                    gu_free_s[par] = qi32_d[par]
                    P.wait("vector", ss, qi32_d[par])
                    P.wait("vector", vv, max(comb_d))
                    dd1_d[par] = P.op("vector", lambda E, t=t, par=par: E.scalar_tensor_tensor(
                        d_bfb[:], gu_t[:, par, :], mpt[:, t, 2:3], xq_i32b[:],
                        OP.mult, OP.subtract), vv)
                    gu_free[par] = dd1_d[par]
                    P.wait("scalar", vv, dd1_d[par])
                    da = P.op("scalar", lambda E: E.activation(
                        d_bfb[:], d_bfb[:], AF.Identity, bias=magic_c[:], scale=1.0), ss)
                    P.wait("scalar", ss, da)
                    dd2_d[par] = P.op("scalar", lambda E: E.activation(
                        d_bfb[:], d_bfb[:], AF.Identity, bias=nmagic_c[:], scale=1.0), ss)
                    P.wait("vector", ss, dd2_d[par])
                    P.wait("vector", tt, tr_d.get((t - 2, 7), 0))
                    comb_d[par] = P.op("vector", lambda E, par=par: E.scalar_tensor_tensor(
                        xqgu_tm[:, par, :], d_bfb[:], 1.0, xq_i32b[:],
                        OP.mult, OP.add), vv)

                def cd_transp(t):
                    par = t % 2
                    P.wait("tensor", vv, comb_d[par])
                    P.wait("scalar", tt, mm_d[par])
                    for cg in range(8):
                        tpp = cg % 2
                        prev = copy_d[(t, cg - 2)] if cg >= 2 else copy_d.get((t - 1, 6 + tpp), 0)
                        P.wait("tensor", ss, prev)
                        for c4 in range(4):
                            c = cg * 4 + c4
                            tp = P.op("tensor", lambda E, c=c, c4=c4, tpp=tpp, par=par: E.transpose(
                                ps_t3[tpp][:, c4 * 128:(c4 + 1) * 128],
                                xqgu_tm[:, par, c * 128:(c + 1) * 128], id_bf[:]), tt)
                        tr_d[(t, cg)] = tp
                        P.wait("scalar", tt, tp)
                        copy_d[(t, cg)] = P.op("scalar", lambda E, cg=cg, tpp=tpp, par=par: E.activation(
                            xqgu_fm[:, par, cg * 4:(cg + 1) * 4, :].rearrange("p a b -> p (a b)"),
                            ps_t3[tpp][:], AF.Identity, scale=1.0), ss)

                def cd_mm(t):
                    par = t % 2
                    P.wait("tensor", ss, copy_d[(t, 7)])
                    P.wait("tensor", ss, ffo_ev[par])
                    for fh in range(2):
                        for kc in range(32):
                            mmd = P.op("tensor", lambda E, kc=kc, fh=fh, par=par: E.matmul(
                                ps_d[par][:, fh * 512:(fh + 1) * 512], xqgu_fm[:, par, kc, :],
                                wdq_sb[:, kc, fh * 512:(fh + 1) * 512],
                                start=(kc == 0), stop=(kc == 31)), tt)
                        P.wait("scalar", tt, mmd)
                        if fh == 0:
                            P.wait("scalar", vv, ffo_used[par])
                        ffo_ev[par] = P.op("scalar", lambda E, t=t, fh=fh, par=par: E.activation(
                            ffo[:, par, fh * 512:(fh + 1) * 512],
                            ps_d[par][:, fh * 512:(fh + 1) * 512],
                            AF.Identity, scale=rsd[:, t:t + 1]), ss)
                    mm_d[par] = mmd

                def cd_ln4(t):
                    par = t % 2
                    P.wait("vector", ss, ffo_ev[par])
                    P.op("vector", lambda E, par=par: E.bn_stats(st3[:, 0:6], ffo[:, par, 0:512]))
                    sb4 = P.op("vector", lambda E, par=par: E.bn_stats(st3[:, 6:12], ffo[:, par, 512:1024]), vv)
                    P.wait("vector", vv, sb4)
                    ag4 = P.op("vector", lambda E: E.bn_aggr(mv3[:], st3[:]), vv)
                    P.wait("scalar", vv, ag4)
                    lnx4 = P.op("scalar", lambda E: E.activation(rs3[:], mv3[:, 1:2], AF.Ln, bias=eps_sb[:], scale=1.0), ss)
                    P.wait("scalar", ss, lnx4)
                    rr4 = P.op("scalar", lambda E: E.activation(rs3[:], rs3[:], AF.Exp, scale=-0.5), ss)
                    P.wait("vector", ss, rr4)
                    P.wait("vector", sc_out[par], out_st[par])
                    ow = P.op("vector", lambda E, par=par: E.tensor_scalar(
                        outb[:, par, :], ffo[:, par, :], mv3[:, 0:1], rs3[:], OP.subtract, OP.mult), vv)
                    ffo_used[par] = ow
                    P.wait("vector", vv, ow)
                    P.wait("vector", sc_xr[par], d_x1[par])
                    ox = P.op("vector", lambda E, par=par: E.tensor_tensor(
                        outb[:, par, :], outb[:, par, :], x1t2[:, par, :], OP.add), vv)
                    x1_free[par] = ox
                    P.wait("sync", vv, ox)
                    out_st[par] = P.op("sync", lambda E, t=t, par=par: E.dma_start(
                        out=out[t * 128:(t + 1) * 128, :], in_=outb[:, par, :]), sc_out[par], 16)

                cd_dma_gu(0)
                if NT > 1:
                    cd_dma_gu(1)
                cd_dma_x1(0)
                cd_quant(0)
                if NT > 1:
                    cd_quant(1)
                cd_transp(0)
                for i in range(NT):
                    if i + 2 < NT:
                        cd_dma_gu(i + 2)
                    if i >= 1:
                        cd_ln4(i - 1)
                    if i + 1 < NT:
                        cd_dma_x1(i + 1)
                    if i + 2 < NT:
                        cd_quant(i + 2)
                    cd_mm(i)
                    if i + 1 < NT:
                        cd_transp(i + 1)
                cd_ln4(NT - 1)

                P.wait("sync", sc_out[0], out_st[0])
                P.wait("sync", sc_out[1], out_st[1])
                P.wait("gpsimd", sc_out[0], out_st[0])
                P.wait("gpsimd", sc_out[1], out_st[1])

        with nc.Block() as block:
            @block.tensor
            def _(E):
                for f in P.ev["tensor"]:
                    f(E)

            @block.vector
            def _(E):
                for f in P.ev["vector"]:
                    f(E)

            @block.scalar
            def _(E):
                for f in P.ev["scalar"]:
                    f(E)

            @block.gpsimd
            def _(E):
                for f in P.ev["gpsimd"]:
                    f(E)

            @block.sync
            def _(E):
                for f in P.ev["sync"]:
                    f(E)

    return nc, dbg


_CACHE = {}


def _host_inputs(L0_attn_W_w, L0_attn_W_b, L0_attn_proj_w, L0_attn_proj_b,
                 L0_attn_ln_g, L0_attn_ln_b, L0_attn_rpb,
                 L0_ffn_w_gate, L0_ffn_w_up, L0_ffn_w_down):
    f = np.asarray
    Ww = f(L0_attn_W_w, dtype=np.float32)
    Wb = f(L0_attn_W_b, dtype=np.float32)
    rpb = f(L0_attn_rpb, dtype=np.float32)
    shared = {
        "wqkvT": np.ascontiguousarray(Ww.T),
        "wb_pc": np.ascontiguousarray(Wb.reshape(24, 128).T),
        "wbv_row": np.ascontiguousarray(Wb[2048:3072].reshape(1, 1024)),
        "wprojT": np.ascontiguousarray(f(L0_attn_proj_w, dtype=np.float32).T),
        "projb_row": np.ascontiguousarray(f(L0_attn_proj_b, dtype=np.float32).reshape(1, 1024)),
        "lng_row": np.ascontiguousarray(f(L0_attn_ln_g, dtype=np.float32).reshape(1, 1024)),
        "lnb_row": np.ascontiguousarray(f(L0_attn_ln_b, dtype=np.float32).reshape(1, 1024)),
        "rpbmt": np.ascontiguousarray(rpb[::-1, :].T).reshape(-1),
        "wgT": np.ascontiguousarray(f(L0_ffn_w_gate, dtype=np.float32).T),
        "wuT": np.ascontiguousarray(f(L0_ffn_w_up, dtype=np.float32).T),
        "wdT": np.ascontiguousarray(f(L0_ffn_w_down, dtype=np.float32).T),
        "id_in": np.eye(128, dtype=np.float32),
        "j_in": np.eye(128, dtype=np.float32)[::-1].copy(),
    }
    m = np.arange(128)[:, None] + np.arange(128)[None, :]
    shared["mask_in"] = np.where(m <= 126, np.float32(NEG), np.float32(0.0))
    return shared


def kernel(L0_attn_x, L0_attn_W_w, L0_attn_W_b, L0_attn_proj_w, L0_attn_proj_b,
           L0_attn_ln_g, L0_attn_ln_b, L0_attn_rpb, L0_ffn_w_gate, L0_ffn_w_up,
           L0_ffn_w_down):
    x = np.asarray(L0_attn_x, dtype=np.float32)
    B = x.shape[0]
    if "nc" not in _CACHE:
        _CACHE["nc"] = build(S=2048, b_dbg=False)[0]
    nc = _CACHE["nc"]
    shared = _host_inputs(L0_attn_W_w, L0_attn_W_b, L0_attn_proj_w, L0_attn_proj_b,
                          L0_attn_ln_g, L0_attn_ln_b, L0_attn_rpb,
                          L0_ffn_w_gate, L0_ffn_w_up, L0_ffn_w_down)
    in_maps = []
    for b in range(B):
        m = dict(shared)
        m["x"] = np.ascontiguousarray(x[b])
        in_maps.append(m)
    res = run_bass_kernel_spmd(nc, in_maps, core_ids=list(range(B)))
    return np.stack([res.results[b]["out"] for b in range(B)], axis=0)


# version 24 (final)
# speedup vs baseline: 1.0436x; 1.0070x over previous
import numpy as np
from contextlib import ExitStack

import concourse.bass as bass
import concourse.mybir as mybir
from concourse.ap import AP
from concourse.bass_utils import run_bass_kernel_spmd
from concourse.alu_op_type import AluOpType
from concourse import library_config

F32 = mybir.dt.float32
BF16 = mybir.dt.bfloat16
I32 = mybir.dt.int32
AF = mybir.ActivationFunctionType
AX = mybir.AxisListType
OP = AluOpType

D = 1024
H = 16
INNER = 4096
NEG = -1.0e30
EXP_OFF = 40.0


class Sem:
    __slots__ = ("h", "n")
    def __init__(self, h):
        self.h = h
        self.n = 0


class RotSem:
    CAP = 1600

    def __init__(self, handles):
        self.hs = handles
        self.n = 0
        self.seg = 0
        self.base = 0

    def loc(self, v):
        if v == 0:
            return self.hs[0], 0
        s = (v - 1) // self.CAP
        return self.hs[s % len(self.hs)], v - s * self.CAP


class Prog:
    ENGS = ("tensor", "vector", "scalar", "gpsimd", "sync")

    def __init__(self, nc):
        self.nc = nc
        self.ev = {k: [] for k in self.ENGS}

    def wait(self, eng, sem, val):
        if not val:
            return
        if isinstance(sem, RotSem):
            h, v = sem.loc(int(val))
            self.ev[eng].append(lambda e, s=h, vv_=v: e.wait_ge(s, vv_))
        else:
            self.ev[eng].append(lambda e, s=sem.h, v=int(val): e.wait_ge(s, v))

    def op(self, eng, make, inc=None, k=1):
        if inc is not None:
            if isinstance(inc, RotSem):
                inc.n += k
                h, _ = inc.loc(inc.n)
                self.ev[eng].append(lambda e, m=make, s=h, kk=k: m(e).then_inc(s, kk))
                return inc.n
            inc.n += k
            self.ev[eng].append(lambda e, m=make, s=inc.h, kk=k: m(e).then_inc(s, kk))
            return inc.n
        self.ev[eng].append(make)
        return None

    def barrier(self, sems):
        snap = [(s, s.n) for s in sems if s.n > 0]
        for eng in self.ENGS:
            for s, n in snap:
                self.wait(eng, s, n)


def build(S=2048, b_dbg=False, pad_instr=0):
    NT = S // 128
    NP = NT // 2
    QT = max(1, NT // 8)
    R = 2 * S - 1
    WEXT = 384 + S
    nc = bass.Bass("TRN2", target_bir_lowering=False)

    x_in = nc.dram_tensor("x", [S, D], F32, kind="ExternalInput").ap()
    wqkvT = nc.dram_tensor("wqkvT", [D, 3 * D], F32, kind="ExternalInput").ap()
    wb_pc = nc.dram_tensor("wb_pc", [128, 24], F32, kind="ExternalInput").ap()
    wbv_row = nc.dram_tensor("wbv_row", [1, D], F32, kind="ExternalInput").ap()
    wprojT = nc.dram_tensor("wprojT", [D, D], F32, kind="ExternalInput").ap()
    projb_row = nc.dram_tensor("projb_row", [1, D], F32, kind="ExternalInput").ap()
    lng_row = nc.dram_tensor("lng_row", [1, D], F32, kind="ExternalInput").ap()
    lnb_row = nc.dram_tensor("lnb_row", [1, D], F32, kind="ExternalInput").ap()
    rpbmt = nc.dram_tensor("rpbmt", [H * R], F32, kind="ExternalInput").ap()
    wgT = nc.dram_tensor("wgT", [D, INNER], F32, kind="ExternalInput").ap()
    wuT = nc.dram_tensor("wuT", [D, INNER], F32, kind="ExternalInput").ap()
    wdT = nc.dram_tensor("wdT", [INNER, D], F32, kind="ExternalInput").ap()
    id_in = nc.dram_tensor("id_in", [128, 128], F32, kind="ExternalInput").ap()
    j_in = nc.dram_tensor("j_in", [128, 128], F32, kind="ExternalInput").ap()
    mask_in = nc.dram_tensor("mask_in", [128, 128], F32, kind="ExternalInput").ap()

    out = nc.dram_tensor("out", [S, D], F32, kind="ExternalOutput").ap()

    x1_dram = nc.dram_tensor("x1_scratch", [S, D], F32, kind="Internal").ap()
    gu_dram = nc.dram_tensor("gu_scratch", [S, INNER], F32, kind="Internal").ap()

    dbg = {}
    if b_dbg:
        for nm, shp, dt in (("dbg_q", [128, 8 * S], BF16), ("dbg_k", [128, 8 * S], BF16),
                            ("dbg_v", [128, NT * H * 65], BF16), ("dbg_o", [128, NT * D], BF16),
                            ("dbg_x1", [S, D], F32),
                            ("dbg_xn", [128, 8 * S], BF16),
                            ("dbg_wm", [1, 8], F32)):
            dbg[nm] = nc.dram_tensor(nm, shp, dt, kind="ExternalOutput").ap()

    st = ExitStack()
    with st:
        e = st.enter_context

        tt = RotSem([e(nc.semaphore(f"sem_t{i}")) for i in range(8)])
        vv = Sem(e(nc.semaphore("sem_v")))
        ss = Sem(e(nc.semaphore("sem_s")))
        gg = Sem(e(nc.semaphore("sem_g")))
        def _sem(nm):
            return Sem(e(nc.semaphore(nm)))
        sc_const = _sem("sc_const")
        sc_x = _sem("sc_x")
        sc_p1 = _sem("sc_p1")
        sc_p2l = _sem("sc_p2l")
        sc_p2s = _sem("sc_p2s")
        sc_xr = [_sem("sc_xr0"), _sem("sc_xr1")]
        sc_x1 = [_sem("sc_x10"), _sem("sc_x11")]
        sc_x1q = _sem("sc_x1q")
        sc_out = [_sem("sc_out0"), _sem("sc_out1")]
        gc_const = _sem("gc_const")
        gc_wq = [_sem("gc_wq0"), _sem("gc_wq1")]
        gc_wv = _sem("gc_wv")
        gc_we = [_sem("gc_we0"), _sem("gc_we1")]
        gc_wp = _sem("gc_wp")
        gc_gu = [_sem("gc_gu0"), _sem("gc_gu1")]
        gc_wd = [_sem("gc_wd0"), _sem("gc_wd1")]
        gc_dbg = _sem("gc_dbg")
        ALLSEMS = [tt, vv, ss, gg, sc_const, sc_x, sc_p1, sc_p2l, sc_p2s,
                   sc_xr[0], sc_xr[1], sc_x1[0], sc_x1[1], sc_x1q,
                   sc_out[0], sc_out[1], gc_const, gc_wq[0], gc_wq[1], gc_wv,
                   gc_we[0], gc_we[1], gc_wp, gc_gu[0], gc_gu[1],
                   gc_wd[0], gc_wd[1], gc_dbg]

        P = Prog(nc)

        ones_bf = e(nc.sbuf_tensor("ones_bf", [128, 128], BF16))
        ones_f32 = e(nc.sbuf_tensor("ones_f32", [1, 128], F32))
        onec_f32 = e(nc.sbuf_tensor("onec_f32", [128, 1], F32))
        id_bf = e(nc.sbuf_tensor("id_bf", [128, 128], BF16))
        j_bf = e(nc.sbuf_tensor("j_bf", [128, 128], BF16))
        mask_sb = e(nc.sbuf_tensor("mask_sb", [128, 128], F32))
        eps_sb = e(nc.sbuf_tensor("eps_sb", [128, 1], F32))
        wb_sb = e(nc.sbuf_tensor("wb_sb", [128, 24], F32))
        wbq_sb = e(nc.sbuf_tensor("wbq_sb", [128, 8], F32))
        wbv_bf = e(nc.sbuf_tensor("wbv_bf", [1, D], BF16))
        negmh = e(nc.sbuf_tensor("negmh", [128, 1], F32))
        wstat = e(nc.sbuf_tensor("wstat", [1, 8], F32))
        swg_b = e(nc.sbuf_tensor("swg_b", [128, 3], F32))
        mwg_b = e(nc.sbuf_tensor("mwg_b", [128, 3], F32))
        stO = ExitStack()
        stQKV = ExitStack()
        o_sb = stO.enter_context(nc.sbuf_tensor("o_sb", [128, NT, D], BF16))
        q_sb = stQKV.enter_context(nc.sbuf_tensor("q_sb", [128, 8, S], BF16))
        k_sb = stQKV.enter_context(nc.sbuf_tensor("k_sb", [128, 8, S], BF16))
        v_sb = stQKV.enter_context(nc.sbuf_tensor("v_sb", [128, NT, H * 65], BF16))

        P.op("sync", lambda E: E.dma_start(out=wb_sb[:], in_=wb_pc), sc_const, 16)
        a0_ds = P.op("sync", lambda E: E.dma_start(out=mask_sb[:], in_=mask_in), sc_const, 16)
        P.op("gpsimd", lambda E: E.dma_start(out=id_bf[:], in_=id_in), gc_const, 16)
        P.op("gpsimd", lambda E: E.dma_start(out=j_bf[:], in_=j_in), gc_const, 16)
        a0_dg = P.op("gpsimd", lambda E: E.dma_start(out=wbv_bf[:], in_=wbv_row), gc_const, 16)
        for _pi in range(pad_instr):
            P.op("vector", lambda E: E.memset(eps_sb[:], 1e-5))
        if pad_instr:
            with nc.psum_tensor("ps_pad", [128, 128], F32) as ps_pad:
                P.wait("tensor", gc_const, a0_dg)
                for _pi in range(pad_instr):
                    P.op("tensor", lambda E: E.matmul(ps_pad[:], id_bf[:], id_bf[:],
                                                      start=True, stop=True), tt)
        P.op("vector", lambda E: E.memset(eps_sb[:], 1e-5))
        P.op("vector", lambda E: E.memset(wstat[:], 0.0))
        P.op("vector", lambda E: E.memset(negmh[:], 0.0))
        P.op("vector", lambda E: E.memset(ones_bf[:], 1.0))
        P.op("vector", lambda E: E.memset(ones_f32[:], 1.0))
        P.op("vector", lambda E: E.memset(onec_f32[:], 1.0))
        P.op("vector", lambda E: E.memset(
            AP(v_sb[:].tensor, 64, [[NT * H * 65, 128], [H * 65, NT], [65, H]]), 1.0), vv)
        P.wait("vector", sc_const, a0_ds)
        v_wbq = P.op("vector", lambda E: E.tensor_scalar(
            wbq_sb[:], wb_sb[:, 0:8], 1.0 / 32.0, None, OP.mult), vv)

        with ExitStack() as stA:
            eA = stA.enter_context
            TBA = 2
            x_sb = eA(nc.sbuf_tensor("x_sb", [128, 2, TBA, D], F32))
            stats = eA(nc.sbuf_tensor("stats", [128, TBA, 12], F32))
            mv = eA(nc.sbuf_tensor("mv", [128, TBA, 2], F32))
            lnv = eA(nc.sbuf_tensor("lnv", [128, TBA], F32))
            rstd = eA(nc.sbuf_tensor("rstd", [128, TBA], F32))
            xn_bf = eA(nc.sbuf_tensor("xn_bf", [128, TBA, D], BF16))
            xn_fm = eA(nc.sbuf_tensor("xn_fm", [128, 8, S], BF16))
            wqkv_o = eA(nc.sbuf_tensor("wqkv_o", [128, 2, 8, 128], BF16))
            wv_bf = eA(nc.sbuf_tensor("wv_bf", [128, 8, D], BF16))
            ps_tr = [eA(nc.psum_tensor(f"ps_tr{i}", [128, 512], BF16)) for i in range(2)]
            ps_qk = [eA(nc.psum_tensor(f"ps_qk{i}", [128, 512], F32)) for i in range(2)]
            ps_v = [eA(nc.psum_tensor(f"ps_v{i}", [128, D], F32)) for i in range(2)]

            P.wait("tensor", gc_const, a0_dg)

            NTB = NT // TBA
            xld_sems = [sc_x, sc_x1q]
            tr_done = [0, 0]
            x_free = [0, 0]
            d_xa = [0, 0]
            tb_done = [0] * NTB
            for tb in range(NTB + 1):
                if tb < NTB:
                    tbp = tb % 2
                    P.wait("sync", vv, x_free[tbp])
                    d_xa[tbp] = P.op("sync", lambda E, tb=tb, tbp=tbp: E.dma_start(
                        out=x_sb[:, tbp, :, :],
                        in_=AP(x_in.tensor, tb * TBA * 128 * D, [[D, 128], [128 * D, TBA], [1, D]])),
                        xld_sems[tbp], 16)
                if tb == 0:
                    continue
                tb_ = tb - 1
                tbp = tb_ % 2
                P.wait("vector", xld_sems[tbp], d_xa[tbp])
                for t in range(TBA):
                    P.op("vector", lambda E, t=t, tbp=tbp: E.bn_stats(stats[:, t, 0:6], x_sb[:, tbp, t, 0:512]))
                    last = P.op("vector", lambda E, t=t, tbp=tbp: E.bn_stats(stats[:, t, 6:12], x_sb[:, tbp, t, 512:1024]), vv)
                P.wait("vector", vv, last)
                for t in range(TBA):
                    agg = P.op("vector", lambda E, t=t: E.bn_aggr(mv[:, t, :], stats[:, t, :]), vv)
                P.wait("scalar", vv, agg)
                ln1 = P.op("scalar", lambda E: E.activation(lnv[:], mv[:, :, 1], AF.Ln, bias=eps_sb[:], scale=1.0), ss)
                P.wait("scalar", ss, ln1)
                rs = P.op("scalar", lambda E: E.activation(rstd[:], lnv[:], AF.Exp, scale=-0.5), ss)
                P.wait("vector", ss, rs)
                for t in range(TBA):
                    x_free[tbp] = P.op("vector", lambda E, t=t, tbp=tbp: E.tensor_scalar(
                        xn_bf[:, t, :], x_sb[:, tbp, t, :], mv[:, t, 0:1], rstd[:, t:t + 1],
                        OP.subtract, OP.mult), vv)
                for c in range(8):
                    par = c % 2
                    P.wait("tensor", vv, x_free[tbp])
                    P.wait("tensor", vv, tr_done[par])
                    for t in range(TBA):
                        tp = P.op("tensor", lambda E, c=c, t=t, par=par, tbp=tbp: E.transpose(
                            ps_tr[par][:, t * 128:(t + 1) * 128],
                            xn_bf[:, t, c * 128:(c + 1) * 128], id_bf[:]), tt)
                    P.wait("vector", tt, tp)
                    tr_done[par] = P.op("vector", lambda E, c=c, tb_=tb_, par=par: E.tensor_copy(
                        xn_fm[:, c, tb_ * TBA * 128:(tb_ + 1) * TBA * 128], ps_tr[par][:, 0:TBA * 128]), vv)
                tb_done[tb_] = max(tr_done)

            if b_dbg:
                P.wait("gpsimd", vv, max(tr_done))
                P.wait("gpsimd", gc_dbg, gc_dbg.n)
                _dmp = P.op("gpsimd", lambda E: E.dma_start(out=dbg["dbg_xn"], in_=xn_fm[:].rearrange("p a b -> p (a b)")), gc_dbg, 16)
                for _e in ("tensor", "vector", "scalar", "sync"):
                    P.wait(_e, gc_dbg, gc_dbg.n)
            P.wait("scalar", sc_const, a0_ds)
            P.wait("scalar", vv, v_wbq)
            w_free = [0, 0]
            qk_ev = [0, 0]
            NF = S // 512
            for o in range(16):
                par = o % 2
                P.wait("gpsimd", tt, w_free[par])
                d_w = P.op("gpsimd", lambda E, o=o, par=par: E.dma_start(
                    out=wqkv_o[:, par, :, :],
                    in_=AP(wqkvT.tensor, o * 128, [[3 * D, 128], [128 * 3 * D, 8], [1, 128]])), gc_wq[par], 16)
                P.wait("tensor", gc_wq[par], d_w)
                for f in range(NF):
                    fpar = (o * NF + f) % 2
                    if o == 0:
                        P.wait("tensor", vv, tb_done[min(NTB - 1, 2 * f + 1)])
                    P.wait("tensor", ss, qk_ev[fpar])
                    for kc in range(8):
                        mm = P.op("tensor", lambda E, par=par, kc=kc, f=f, fpar=fpar: E.matmul(
                            ps_qk[fpar][:], wqkv_o[:, par, kc, :],
                            xn_fm[:, kc, f * 512:(f + 1) * 512],
                            start=(kc == 0), stop=(kc == 7)), tt)
                    if f == NF - 1:
                        w_free[par] = mm
                    P.wait("scalar", tt, mm)
                    if o < 8:
                        qk_ev[fpar] = P.op("scalar", lambda E, o=o, f=f, fpar=fpar: E.activation(
                            q_sb[:, o, f * 512:(f + 1) * 512], ps_qk[fpar][:], AF.Identity,
                            bias=wbq_sb[:, o:o + 1], scale=1.0 / 32.0), ss)
                    else:
                        qk_ev[fpar] = P.op("scalar", lambda E, o=o, f=f, fpar=fpar: E.activation(
                            k_sb[:, o - 8, f * 512:(f + 1) * 512], ps_qk[fpar][:], AF.Identity,
                            bias=wb_sb[:, o:o + 1], scale=1.0), ss)

            P.wait("gpsimd", tt, max(w_free))
            d_wv = P.op("gpsimd", lambda E: E.dma_start(
                out=wv_bf[:],
                in_=AP(wqkvT.tensor, 2048, [[3 * D, 128], [128 * 3 * D, 8], [1, D]])), gc_wv, 16)
            P.wait("tensor", gc_wv, d_wv)
            v_ev = [0, 0]
            for t in range(NT):
                par = t % 2
                P.wait("tensor", vv, v_ev[par])
                for fh in range(2):
                    P.op("tensor", lambda E, fh=fh, par=par: E.matmul(
                        ps_v[par][:, fh * 512:(fh + 1) * 512], ones_bf[0:1, :],
                        wbv_bf[:, fh * 512:(fh + 1) * 512], start=True, stop=False))
                for kc in range(8):
                    for fh in range(2):
                        mm = P.op("tensor", lambda E, t=t, kc=kc, fh=fh, par=par: E.matmul(
                            ps_v[par][:, fh * 512:(fh + 1) * 512],
                            xn_fm[:, kc, t * 128:(t + 1) * 128],
                            wv_bf[:, kc, fh * 512:(fh + 1) * 512],
                            start=False, stop=(kc == 7)), tt)
                P.wait("vector", tt, mm)
                v_ev[par] = P.op("vector", lambda E, t=t, par=par: E.tensor_copy(
                    AP(v_sb[:].tensor, t * (H * 65), [[NT * H * 65, 128], [65, H], [1, 64]]),
                    AP(ps_v[par][:].tensor, 0, [[D, 128], [64, H], [1, 64]])), vv)

        P.barrier(ALLSEMS)
        P.barrier(ALLSEMS)
        if b_dbg:
            P.wait("gpsimd", gc_dbg, gc_dbg.n)
            _dmp = P.op("gpsimd", lambda E: E.dma_start(out=dbg["dbg_q"], in_=q_sb[:].rearrange("p a b -> p (a b)")), gc_dbg, 16)
            for _e in ("tensor", "vector", "scalar", "sync"):
                P.wait(_e, gc_dbg, gc_dbg.n)
            P.wait("gpsimd", gc_dbg, gc_dbg.n)
            _dmp = P.op("gpsimd", lambda E: E.dma_start(out=dbg["dbg_k"], in_=k_sb[:].rearrange("p a b -> p (a b)")), gc_dbg, 16)
            for _e in ("tensor", "vector", "scalar", "sync"):
                P.wait(_e, gc_dbg, gc_dbg.n)
            P.wait("gpsimd", gc_dbg, gc_dbg.n)
            _dmp = P.op("gpsimd", lambda E: E.dma_start(out=dbg["dbg_v"], in_=v_sb[:].rearrange("p a b -> p (a b)")), gc_dbg, 16)
            for _e in ("tensor", "vector", "scalar", "sync"):
                P.wait(_e, gc_dbg, gc_dbg.n)
        with ExitStack() as stB:
            eB = stB.enter_context
            wext = eB(nc.sbuf_tensor("wext", [128, 2, WEXT], BF16))
            e_sb = eB(nc.sbuf_tensor("e_sb", [128, 2, WEXT], BF16))
            p_raw = eB(nc.sbuf_tensor("p_raw", [128, 4, 1024], F32))
            p_sb = eB(nc.sbuf_tensor("p_sb", [128, 4, 1024], BF16))
            recl = eB(nc.sbuf_tensor("recl", [128, 2, 4], F32))
            wchunk = eB(nc.sbuf_tensor("wchunk", [128, 2, 1024], F32))
            wabs = eB(nc.sbuf_tensor("wabs", [128, 2, 1024], F32))
            wsum_sb = eB(nc.sbuf_tensor("wsum_sb", [1, 512], F32))
            ps_st = [eB(nc.psum_tensor(f"ps_st{i}", [128, 1024], F32)) for i in range(2)]
            ps_o = [eB(nc.psum_tensor(f"ps_o{i}", [128, 260], F32)) for i in range(2)]
            ps_ws = eB(nc.psum_tensor("ps_ws", [1, 512], F32))

            P.wait("scalar", vv, v_wbq)

            w_srcs = [wgT, wuT, wdT]
            p1_units = [(wi, cc, hf) for wi in range(3) for cc in range(8) for hf in range(4)]
            p1 = {"dma": [0, 0], "abs": [0, 0], "mm": 0, "mm2": [0, 0], "ev": 0, "wsum_ev": 0}
            p1sems = [sc_p1, sc_x]

            def p1_issue_dma(u):
                wi, cc, hf = p1_units[u]
                wsrc = w_srcs[wi]
                upar = u % 2
                P.wait("sync", vv, p1["abs"][upar])

                def mk(E, wsrc=wsrc, cc=cc, hf=hf, upar=upar):
                    if wsrc is wdT:
                        src = AP(wsrc.tensor, (cc * 512 + hf * 128) * D, [[D, 128], [1, D]])
                        return E.dma_start(out=wchunk[:, upar, :], in_=src)
                    src = AP(wsrc.tensor, cc * 128 * INNER + hf * 1024, [[INNER, 128], [1, 1024]])
                    return E.dma_start(out=wchunk[:, upar, :], in_=src)
                p1["dma"][upar] = P.op("sync", mk, p1sems[upar], 16)

            def p1_abs(u):
                upar = u % 2
                P.wait("vector", p1sems[upar], p1["dma"][upar])
                P.wait("vector", tt, p1["mm2"][upar])
                ab = P.op("vector", lambda E, upar=upar: E.scalar_tensor_tensor(
                    wabs[:, upar, :], wchunk[:, upar, :], -1.0, wchunk[:, upar, :],
                    OP.mult, OP.max), vv)
                p1["abs"][upar] = ab

            def p1_unit_compute(u):
                wi = u // 32
                upar = u % 2
                P.wait("tensor", vv, p1["abs"][upar])
                if u % 32 == 0:
                    P.wait("tensor", vv, p1["wsum_ev"])
                for f in range(2):
                    mm = P.op("tensor", lambda E, f=f, u=u, upar=upar: E.matmul(
                        ps_ws[:], onec_f32[:], wabs[:, upar, f * 512:(f + 1) * 512],
                        start=(u % 32 == 0 and f == 0), stop=(u % 32 == 31 and f == 1)), tt)
                p1["mm"] = mm
                p1["mm2"][upar] = mm
                if u % 32 == 31:
                    P.wait("vector", tt, p1["mm"])
                    ev = P.op("vector", lambda E: E.tensor_copy(wsum_sb[:], ps_ws[:]), vv)
                    p1["wsum_ev"] = ev
                    P.wait("vector", vv, ev)
                    wdt2 = 256
                    while wdt2 >= 1:
                        sm = P.op("vector", lambda E, w2=wdt2: E.tensor_tensor(
                            wsum_sb[:, 0:w2], wsum_sb[:, 0:w2], wsum_sb[:, w2:2 * w2], OP.add), vv)
                        P.wait("vector", vv, sm)
                        wdt2 //= 2
                    sm = P.op("vector", lambda E, wi=wi: E.tensor_copy(
                        wstat[:, wi:wi + 1], wsum_sb[:, 0:1]), vv)
                    P.wait("vector", vv, sm)
                    mc = P.op("vector", lambda E, wi=wi: E.tensor_scalar(
                        wstat[:, wi:wi + 1], wstat[:, wi:wi + 1], 1.0 / float(D * INNER), 1e-5,
                        OP.mult, OP.max), vv)
                    P.wait("vector", vv, mc)
                    p1["ev"] = P.op("vector", lambda E, wi=wi: E.reciprocal(
                        wstat[:, 3 + wi:4 + wi], wstat[:, wi:wi + 1]), vv)

            p1_prog = {"mm": 0, "abs": 0, "issued": 0, "quads": 0}
            TOTQ = H * (NP // 2)

            def p1_tick():
                p1_prog["quads"] += 1
                base = p1_prog["quads"] * 96 // TOTQ
                while p1_prog["mm"] < base:
                    while p1_prog["abs"] <= p1_prog["mm"] + 1 and p1_prog["abs"] < 96:
                        p1_abs(p1_prog["abs"])
                        p1_prog["abs"] += 1
                        if p1_prog["issued"] < 96:
                            p1_issue_dma(p1_prog["issued"])
                            p1_prog["issued"] += 1
                    p1_unit_compute(p1_prog["mm"])
                    p1_prog["mm"] += 1

            p1_issue_dma(0)
            p1_issue_dma(1)
            p1_prog["issued"] = 2

            wx_free = [0, 0]
            wx_ready = [0, 0]
            st_free = [0, 0]
            pv_done = [0, 0, 0, 0]
            pmul = [0, 0, 0, 0]
            e_used = [0, 0]
            e_ready = [0, 0]
            fifo = []
            o_ev = [0, 0]
            gpar = [0]
            b_ss, b_vv = ss.n, vv.n

            def emit_head(h):
                par = h % 2
                P.wait("gpsimd", tt, wx_free[par])
                P.wait("gpsimd", vv, wx_ready[par])
                d_w = P.op("gpsimd", lambda E, h=h, par=par: E.dma_start(
                    out=wext[:, par, :],
                    in_=AP(rpbmt.tensor, h * R + (S - 512), [[1, 128], [1, WEXT]])), gc_we[par], 16)
                P.wait("vector", gc_we[par], d_w)
                P.wait("vector", tt, wx_free[par])
                P.op("vector", lambda E, par=par: E.memset(wext[:, par, 0:384], NEG))
                wx_ready[par] = P.op("vector", lambda E, par=par: E.tensor_tensor(
                    wext[:, par, 384:512], wext[:, par, 384:512], mask_sb[:], OP.add), vv)
                nE = (WEXT + 511) // 512
                for ei in range(nE):
                    w0 = ei * 512
                    w1 = min(WEXT, w0 + 512)
                    mypar = gpar[0] % 2
                    gpar[0] += 1
                    P.wait("tensor", ss, st_free[mypar])
                    if ei == 0:
                        P.wait("tensor", vv, wx_ready[par])
                    jm = P.op("tensor", lambda E, w0=w0, w1=w1, par=par, mypar=mypar: E.matmul(
                        ps_st[mypar][:, 0:w1 - w0], j_bf[:], wext[:, par, w0:w1],
                        start=True, stop=True), tt)
                    if ei == nE - 1:
                        wx_free[par] = jm
                    P.wait("scalar", tt, jm)
                    if ei == 0:
                        P.wait("scalar", vv, e_used[par])
                    st_free[mypar] = P.op("scalar", lambda E, w0=w0, w1=w1, par=par,
                                          mypar=mypar: E.activation(
                        e_sb[:, par, w0:w1], ps_st[mypar][:, 0:w1 - w0], AF.Exp,
                        bias=negmh[:], scale=1.0), ss)
                e_ready[par] = st_free[(gpar[0] - 1) % 2]

                first_use = True
                hh, hc = 64 * (h % 2), h // 2
                for qd in range(NP // 2):
                    nblocks = 4 * qd + 4
                    groups = [[b0, b0 + 1] for b0 in range(0, nblocks, 2)]
                    qdpar = qd % 2
                    pv_first = True
                    for grp in groups:
                        m2 = gpar[0] % 2
                        m3 = gpar[0] % 4
                        gpar[0] += 1
                        if first_use:
                            P.wait("tensor", ss, b_ss)
                            P.wait("tensor", vv, b_vv)
                            first_use = False
                        P.wait("tensor", ss, st_free[m2])
                        for bi, kc in enumerate(grp):
                            qm = P.op("tensor", lambda E, kc=kc, bi=bi, m2=m2, qd=qd,
                                      hh=hh, hc=hc: E.matmul(
                                ps_st[m2][:, bi * 512:(bi + 1) * 512],
                                k_sb[hh:hh + 64, hc, kc * 128:(kc + 1) * 128],
                                q_sb[hh:hh + 64, hc, qd * 512:(qd + 1) * 512],
                                start=True, stop=True), tt)
                        P.wait("scalar", tt, qm)
                        P.wait("scalar", vv, pmul[m3])
                        st_free[m2] = P.op("scalar", lambda E, m2=m2, m3=m3: E.activation(
                            p_raw[:, m3, :], ps_st[m2][:], AF.Exp,
                            bias=negmh[:], scale=1.0), ss)
                        while len(fifo) >= 3:
                            fifo.pop(0)()
                        P.wait("vector", ss, st_free[m2])
                        P.wait("vector", ss, e_ready[par])
                        P.wait("vector", tt, pv_done[m3])
                        for bi, kc in enumerate(grp):
                            off = 384 + (4 * qd - kc) * 128
                            pmul[m3] = P.op("vector", lambda E, off=off, bi=bi, par=par,
                                            m3=m3: E.tensor_tensor(
                                p_sb[:, m3, bi * 512:(bi + 1) * 512],
                                p_raw[:, m3, bi * 512:(bi + 1) * 512],
                                e_sb[:, par, off:off + 512], OP.mult), vv)
                        e_used[par] = pmul[m3]

                        def mk_pv(grp=grp, m3=m3, qdpar=qdpar, h=h, stt=pv_first,
                                  nblocks=nblocks, pm_val=pmul[m3]):
                            P.wait("tensor", vv, pm_val)
                            if stt:
                                P.wait("tensor", ss, o_ev[qdpar])
                            pm = 0
                            for bi, kc in enumerate(grp):
                                for qs in range(4):
                                    pm = P.op("tensor", lambda E, kc=kc, bi=bi, qs=qs, h=h,
                                              m3=m3, qdpar=qdpar,
                                              stt=(stt and bi == 0 and qs == 0),
                                              spp=(kc == nblocks - 1 and qs == 3): E.matmul(
                                        ps_o[qdpar][:, qs * 65:qs * 65 + 65],
                                        p_sb[:, m3, bi * 512 + qs * 128:bi * 512 + qs * 128 + 128],
                                        v_sb[:, kc, h * 65:h * 65 + 65],
                                        start=stt, stop=spp), tt)
                            pv_done[m3] = pm

                        fifo.append(mk_pv)
                        pv_first = False

                    def mk_evac(qd=qd, qdpar=qdpar, h=h, m3_last=(gpar[0] - 1) % 4):
                        P.wait("vector", tt, pv_done[m3_last])
                        P.wait("vector", ss, o_ev[qdpar])
                        rr = P.op("vector", lambda E, qdpar=qdpar: E.reciprocal(
                            recl[:, qdpar, :],
                            AP(ps_o[qdpar][:].tensor, 64, [[260, 128], [65, 4]])), vv)
                        P.wait("scalar", vv, rr)
                        for qs in range(4):
                            o_ev[qdpar] = P.op("scalar", lambda E, qs=qs, qd=qd, h=h,
                                               qdpar=qdpar: E.activation(
                                o_sb[:, 4 * qd + qs, h * 64:h * 64 + 64],
                                ps_o[qdpar][:, qs * 65:qs * 65 + 64],
                                AF.Identity, scale=recl[:, qdpar, qs:qs + 1]), ss)

                    fifo.append(mk_evac)
                    p1_tick()

            for h in range(H):
                emit_head(h)
            while fifo:
                fifo.pop(0)()

            b_end_ss = ss.n
            if b_dbg:
                P.wait("gpsimd", ss, b_end_ss)
                P.wait("gpsimd", gc_dbg, gc_dbg.n)
                _dmp = P.op("gpsimd", lambda E: E.dma_start(out=dbg["dbg_o"], in_=o_sb[:].rearrange("p a b -> p (a b)")), gc_dbg, 16)
                for _e in ("tensor", "vector", "scalar", "sync"):
                    P.wait(_e, gc_dbg, gc_dbg.n)

        stQKV.close()
        P.barrier(ALLSEMS)
        with ExitStack() as stB2:
            eB2 = stB2.enter_context
            wproj_bf = eB2(nc.sbuf_tensor("wproj_bf", [128, 8, D], BF16))
            o_fm = eB2(nc.sbuf_tensor("o_fm", [128, 8, 128], BF16))
            x_rld = eB2(nc.sbuf_tensor("x_rld", [128, 2, D], F32))
            xn1 = eB2(nc.sbuf_tensor("xn1", [128, 2, D], F32))
            ybuf = eB2(nc.sbuf_tensor("ybuf", [128, D], F32))
            zbuf = eB2(nc.sbuf_tensor("zbuf", [128, D], F32))
            x1buf = eB2(nc.sbuf_tensor("x1buf", [128, 2, D], F32))
            st2 = eB2(nc.sbuf_tensor("st2", [128, 12], F32))
            mv2 = eB2(nc.sbuf_tensor("mv2", [128, 2], F32))
            rs2 = eB2(nc.sbuf_tensor("rs2", [128, 1], F32))
            nb2 = eB2(nc.sbuf_tensor("nb2", [128, 1], F32))
            mva = eB2(nc.sbuf_tensor("mva", [128, 2, 2], F32))
            rsa = eB2(nc.sbuf_tensor("rsa", [128, 2], F32))
            nba = eB2(nc.sbuf_tensor("nba", [128, 2], F32))
            g_bc = eB2(nc.sbuf_tensor("g_bc", [128, D], F32))
            b_bc = eB2(nc.sbuf_tensor("b_bc", [128, D], F32))
            pb_bc = eB2(nc.sbuf_tensor("pb_bc", [128, D], F32))
            rows = eB2(nc.sbuf_tensor("rows", [1, 3 * D], F32))
            ps_tr2 = [eB2(nc.psum_tensor(f"ps_tr2{i}", [128, 512], BF16)) for i in range(2)]
            ps_pj = [eB2(nc.psum_tensor(f"ps_pj{i}", [128, 512], F32)) for i in range(2)]

            P.op("sync", lambda E: E.dma_start(out=rows[:, 0:D], in_=projb_row), sc_const, 16)
            P.op("sync", lambda E: E.dma_start(out=rows[:, D:2 * D], in_=lng_row), sc_const, 16)
            d_r2 = P.op("sync", lambda E: E.dma_start(out=rows[:, 2 * D:3 * D], in_=lnb_row), sc_const, 16)
            d_wp = P.op("gpsimd", lambda E: E.dma_start(
                out=wproj_bf[:],
                in_=AP(wprojT.tensor, 0, [[D, 128], [128 * D, 8], [1, D]])), gc_wp, 16)
            P.wait("tensor", sc_const, d_r2)
            P.wait("tensor", vv, p1["ev"])
            bc_dsts = [pb_bc, g_bc, b_bc]
            bc_ev = [0, 0]
            for bi3 in range(3):
                for fh in range(2):
                    P.wait("tensor", vv, bc_ev[fh])
                    bm = P.op("tensor", lambda E, bi3=bi3, fh=fh: E.matmul(
                        ps_pj[fh][:], ones_f32[:], rows[:, bi3 * D + fh * 512:bi3 * D + (fh + 1) * 512],
                        start=True, stop=True), tt)
                    P.wait("vector", tt, bm)
                    bcd = P.op("vector", lambda E, bi3=bi3, fh=fh: E.tensor_copy(
                        bc_dsts[bi3][:, fh * 512:(fh + 1) * 512], ps_pj[fh][:]), vv)
                    bc_ev[fh] = bcd
            P.wait("tensor", vv, bc_ev[0])
            bm = P.op("tensor", lambda E: E.matmul(ps_pj[0][:, 0:8], ones_f32[:], wstat[:],
                                                   start=True, stop=True), tt)
            P.wait("vector", tt, bm)
            P.op("vector", lambda E: E.tensor_copy(mwg_b[:], ps_pj[0][:, 0:3]))
            swb = P.op("vector", lambda E: E.tensor_copy(swg_b[:], ps_pj[0][:, 3:6]), vv)
            if b_dbg:
                P.wait("gpsimd", vv, swb)
                P.wait("gpsimd", gc_dbg, gc_dbg.n)
                _dmp = P.op("gpsimd", lambda E: E.dma_start(out=dbg["dbg_wm"], in_=wstat[:]), gc_dbg, 16)
                for _e in ("tensor", "vector", "scalar", "sync"):
                    P.wait(_e, gc_dbg, gc_dbg.n)


            tr2_ev = [0, 0]
            ypj = [0, 0]
            x1_st = [0, 0]
            x_free = [0, 0]
            x_free_s = [0, 0]
            s1_apply = [0, 0]
            ytt = [0, 0]
            zap = 0
            x1ap = 0
            ztt2 = 0
            pm_box = [0]
            d_x = [0, 0]

            def b2_dma(qi):
                par = qi % 2
                P.wait("sync", vv, x_free[par])
                P.wait("sync", ss, x_free_s[par])
                d_x[par] = P.op("sync", lambda E, qi=qi, par=par: E.dma_start(
                    out=x_rld[:, par, :], in_=x_in[qi * 128:(qi + 1) * 128, :]), sc_xr[par], 16)

            def b2_s1(qi):
                par = qi % 2
                P.wait("vector", sc_xr[par], d_x[par])
                if qi == 0:
                    P.wait("vector", vv, bcd)
                P.op("vector", lambda E, par=par: E.bn_stats(st2[:, 0:6], x_rld[:, par, 0:512]))
                sb = P.op("vector", lambda E, par=par: E.bn_stats(st2[:, 6:12], x_rld[:, par, 512:1024]), vv)
                P.wait("vector", vv, sb)
                ag = P.op("vector", lambda E, par=par: E.bn_aggr(mva[:, par, :], st2[:]), vv)
                P.wait("scalar", vv, ag)
                lnx = P.op("scalar", lambda E, par=par: E.activation(
                    rsa[:, par:par + 1], mva[:, par, 1:2], AF.Ln, bias=eps_sb[:], scale=1.0), ss)
                P.wait("scalar", ss, lnx)
                rr = P.op("scalar", lambda E, par=par: E.activation(
                    rsa[:, par:par + 1], rsa[:, par:par + 1], AF.Exp, scale=-0.5), ss)
                P.wait("vector", ss, rr)
                nb = P.op("vector", lambda E, par=par: E.scalar_tensor_tensor(
                    nba[:, par:par + 1], mva[:, par, 0:1], -1.0, rsa[:, par:par + 1],
                    OP.mult, OP.mult), vv)
                P.wait("scalar", vv, nb)
                P.wait("scalar", vv, ytt[par])
                s1_apply[par] = P.op("scalar", lambda E, par=par: E.activation(
                    xn1[:, par, :], x_rld[:, par, :], AF.Identity,
                    bias=nba[:, par:par + 1], scale=rsa[:, par:par + 1]), ss)
                x_free_s[par] = s1_apply[par]

            def b2_pe(qi):
                P.wait("tensor", ss, b_end_ss)
                for cg in range(2):
                    tpp = cg % 2
                    P.wait("tensor", ss, tr2_ev[tpp])
                    P.wait("scalar", tt, pm_box[0])
                    for c4 in range(4):
                        c = cg * 4 + c4
                        tp = P.op("tensor", lambda E, qi=qi, c=c, c4=c4, tpp=tpp: E.transpose(
                            ps_tr2[tpp][:, c4 * 128:(c4 + 1) * 128],
                            o_sb[:, qi, c * 128:(c + 1) * 128], id_bf[:]), tt)
                    P.wait("scalar", tt, tp)
                    tr2_ev[tpp] = P.op("scalar", lambda E, cg=cg, tpp=tpp: E.activation(
                        o_fm[:].rearrange("p a b -> p (a b)")[:, cg * 512:(cg + 1) * 512],
                        ps_tr2[tpp][:], AF.Identity, scale=1.0), ss)
                P.wait("tensor", gc_wp, d_wp)
                P.wait("tensor", ss, max(tr2_ev))
                mm = 0
                for fh in range(2):
                    P.wait("tensor", vv, ypj[fh])
                    for kc in range(8):
                        mm = P.op("tensor", lambda E, fh=fh, kc=kc: E.matmul(
                            ps_pj[fh][:], o_fm[:, kc, :],
                            wproj_bf[:, kc, fh * 512:(fh + 1) * 512],
                            start=(kc == 0), stop=(kc == 7)), tt)
                pm_box[0] = mm
                return mm

            def b2_s2(qi, mm):
                nonlocal_ = {}
                par = qi % 2
                P.wait("vector", tt, mm)
                P.wait("vector", ss, zap)
                P.wait("vector", ss, s1_apply[par])
                for fh in range(2):
                    st_ = P.op("vector", lambda E, fh=fh: E.scalar_tensor_tensor(
                        ybuf[:, fh * 512:(fh + 1) * 512], ps_pj[fh][:], 0.0,
                        pb_bc[:, fh * 512:(fh + 1) * 512], OP.add, OP.add), vv)
                    P.wait("vector", vv, st_)
                    yt = P.op("vector", lambda E, fh=fh, par=par: E.tensor_tensor(
                        ybuf[:, fh * 512:(fh + 1) * 512], ybuf[:, fh * 512:(fh + 1) * 512],
                        xn1[:, par, fh * 512:(fh + 1) * 512], OP.add), vv)
                    ypj[fh] = yt
                ytt[par] = yt
                P.wait("vector", vv, yt)
                P.op("vector", lambda E: E.bn_stats(st2[:, 0:6], ybuf[:, 0:512]))
                sb = P.op("vector", lambda E: E.bn_stats(st2[:, 6:12], ybuf[:, 512:1024]), vv)
                P.wait("vector", vv, sb)
                ag = P.op("vector", lambda E: E.bn_aggr(mv2[:], st2[:]), vv)
                P.wait("scalar", vv, ag)
                lnx = P.op("scalar", lambda E: E.activation(rs2[:], mv2[:, 1:2], AF.Ln, bias=eps_sb[:], scale=1.0), ss)
                P.wait("scalar", ss, lnx)
                rr = P.op("scalar", lambda E: E.activation(rs2[:], rs2[:], AF.Exp, scale=-0.5), ss)
                P.wait("vector", ss, rr)
                nb = P.op("vector", lambda E: E.scalar_tensor_tensor(
                    nb2[:], mv2[:, 0:1], -1.0, rs2[:], OP.mult, OP.mult), vv)
                P.wait("scalar", vv, nb)
                P.wait("scalar", gg, ztt2)
                zap_ = P.op("scalar", lambda E: E.activation(
                    zbuf[:], ybuf[:], AF.Identity, bias=nb2[:], scale=rs2[:]), ss)
                P.wait("vector", ss, zap_)
                zt1 = P.op("vector", lambda E: E.tensor_tensor(zbuf[:], zbuf[:], g_bc[:], OP.mult), vv)
                P.wait("vector", vv, zt1)
                zt2 = P.op("vector", lambda E: E.tensor_tensor(zbuf[:], zbuf[:], b_bc[:], OP.add), vv)
                P.wait("vector", vv, zt2)
                P.op("vector", lambda E: E.bn_stats(st2[:, 0:6], zbuf[:, 0:512]))
                sb = P.op("vector", lambda E: E.bn_stats(st2[:, 6:12], zbuf[:, 512:1024]), vv)
                P.wait("vector", vv, sb)
                ag = P.op("vector", lambda E: E.bn_aggr(mv2[:], st2[:]), vv)
                P.wait("scalar", vv, ag)
                lnx = P.op("scalar", lambda E: E.activation(rs2[:], mv2[:, 1:2], AF.Ln, bias=eps_sb[:], scale=1.0), ss)
                P.wait("scalar", ss, lnx)
                rr = P.op("scalar", lambda E: E.activation(rs2[:], rs2[:], AF.Exp, scale=-0.5), ss)
                P.wait("vector", ss, rr)
                nb = P.op("vector", lambda E: E.scalar_tensor_tensor(
                    nb2[:], mv2[:, 0:1], -1.0, rs2[:], OP.mult, OP.mult), vv)
                P.wait("scalar", vv, nb)
                P.wait("scalar", sc_x1[par], x1_st[par])
                x1a = P.op("scalar", lambda E, par=par: E.activation(
                    x1buf[:, par, :], zbuf[:], AF.Identity, bias=nb2[:], scale=rs2[:]), ss)
                P.wait("vector", ss, x1a)
                xa = P.op("vector", lambda E, par=par: E.tensor_tensor(
                    x1buf[:, par, :], x1buf[:, par, :], x_rld[:, par, :], OP.add), vv)
                x_free[par] = xa
                P.wait("sync", vv, xa)
                x1_st[par] = P.op("sync", lambda E, qi=qi, par=par: E.dma_start(
                    out=x1_dram[qi * 128:(qi + 1) * 128, :], in_=x1buf[:, par, :]), sc_x1[par], 16)
                return zap_, x1a

            b2_dma(0)
            if NT > 1:
                b2_dma(1)
            b2_s1(0)
            for qi in range(NT):
                mm = b2_pe(qi)
                if qi + 1 < NT:
                    b2_s1(qi + 1)
                zap, x1ap = b2_s2(qi, mm)
                if qi + 2 < NT:
                    b2_dma(qi + 2)

            if b_dbg:
                P.wait("gpsimd", sc_x1[0], x1_st[0])
                P.wait("gpsimd", sc_x1[1], x1_st[1])
                P.wait("gpsimd", gc_dbg, gc_dbg.n)
                _dmp = P.op("gpsimd", lambda E: E.dma_start(out=dbg["dbg_x1"], in_=x1_dram), gc_dbg, 16)
                for _e in ("tensor", "vector", "scalar", "sync"):
                    P.wait(_e, gc_dbg, gc_dbg.n)

        stO.close()
        P.barrier(ALLSEMS)
        with ExitStack() as stC:
            eC = stC.enter_context
            wdq_sb = eC(nc.sbuf_tensor("wdq_sb", [128, 32, D], BF16))
            st3 = eC(nc.sbuf_tensor("st3", [128, 12], F32))
            mv3 = eC(nc.sbuf_tensor("mv3", [128, 2], F32))
            rs3 = eC(nc.sbuf_tensor("rs3", [128, 1], F32))
            sc1 = eC(nc.sbuf_tensor("sc1", [128, NT], F32))
            rsg = eC(nc.sbuf_tensor("rsg", [128, NT], F32))
            rsu = eC(nc.sbuf_tensor("rsu", [128, NT], F32))
            rsd = eC(nc.sbuf_tensor("rsd", [128, NT], F32))
            magic_c = eC(nc.sbuf_tensor("magic_c", [128, 1], F32))
            nmagic_c = eC(nc.sbuf_tensor("nmagic_c", [128, 1], F32))
            ps_t3 = [eC(nc.psum_tensor(f"ps_t3{i}", [128, 512], BF16)) for i in range(2)]
            stQ3 = ExitStack()
            stCG = ExitStack()
            xq3_fm = stQ3.enter_context(nc.sbuf_tensor("xq3_fm", [128, 8, S], BF16))
            wgf = stCG.enter_context(nc.sbuf_tensor("wgf", [128, 4, 512], F32))
            wuf = stCG.enter_context(nc.sbuf_tensor("wuf", [128, 4, 512], F32))
            wdf = stCG.enter_context(nc.sbuf_tensor("wdf", [128, 2, D], F32))
            wq_t = stCG.enter_context(nc.sbuf_tensor("wq_t", [128, 2048], F32))
            wq_c1 = stCG.enter_context(nc.sbuf_tensor("wq_c1", [128, 2048], BF16))
            wq_c2 = stCG.enter_context(nc.sbuf_tensor("wq_c2", [128, 2048], BF16))
            wgq_sb = stCG.enter_context(nc.sbuf_tensor("wgq_sb", [128, 2, 8, 512], BF16))
            wuq_sb = stCG.enter_context(nc.sbuf_tensor("wuq_sb", [128, 2, 8, 512], BF16))
            g_act = stCG.enter_context(nc.sbuf_tensor("g_act", [128, 2, 512], BF16))
            prod_sb = stCG.enter_context(nc.sbuf_tensor("prod_sb", [128, 2, 512], F32))
            ps_g = [stCG.enter_context(nc.psum_tensor(f"ps_g{i}", [128, 512], F32)) for i in range(2)]
            ps_u = [stCG.enter_context(nc.psum_tensor(f"ps_u{i}", [128, 512], F32)) for i in range(2)]

            P.op("vector", lambda E: E.memset(magic_c[:], 6755399441055744.0))
            mgd = P.op("vector", lambda E: E.memset(nmagic_c[:], -6755399441055744.0), vv)

            sc_ps = [sc_p2l, sc_p2s]
            wsems = {"g": gc_gu, "u": gc_wq, "d": gc_wd}
            wsrcs = {"g": wgT, "u": wuT}
            wbufs = {"g": wgf, "u": wuf}
            wqsb = {"g": wgq_sb, "u": wuq_sb}
            widx = {"g": 0, "u": 1, "d": 2}
            wbuf_free = {"g": 0, "u": 0, "d": 0}
            wdma = {("g", 0): 0, ("g", 1): 0, ("u", 0): 0, ("u", 1): 0,
                    ("d", 0): 0, ("d", 1): 0}
            qfree = [0]
            qcons = [0]
            quv = {"g": [0, 0], "u": [0, 0]}
            gu_used = [0, 0]

            def emit_wdma(ic):
                for mat in ("g", "u"):
                    P.wait("gpsimd", ss, wbuf_free[mat])
                    wdma[(mat, 0)] = P.op(
                        "gpsimd", lambda E, mat=mat, ic=ic: E.dma_start(
                            out=wbufs[mat][:],
                            in_=AP(wsrcs[mat].tensor, ic * 512,
                                   [[INNER, 128], [128 * INNER, 4], [1, 512]])),
                        wsems[mat][0], 16)
                P.wait("gpsimd", ss, wbuf_free["d"])
                wdma[("d", 0)] = P.op("gpsimd", lambda E, ic=ic: E.dma_start(
                    out=wdf[:],
                    in_=AP(wdT.tensor, ic * 512 * D, [[D, 128], [128 * D, 2], [1, D]])),
                    gc_wd[0], 16)

            def emit_qunit(mat, ic, hf):
                icpar = ic % 2
                src = wdf[:] if mat == "d" else wbufs[mat][:]
                dsem, dval = wsems[mat][hf], wdma[(mat, hf)]
                P.wait("scalar", dsem, dval)
                P.wait("scalar", vv, qfree[0])
                P.wait("scalar", vv, swb)
                tw = P.op("scalar", lambda E, src=src, mat=mat: E.activation(
                    wq_t[:], src, AF.Identity, scale=swg_b[:, widx[mat]:widx[mat] + 1]), ss)
                if hf == 0:
                    P.wait("gpsimd", ss, tw)
                    if mat == "d":
                        wdma[("d", 1)] = P.op("gpsimd", lambda E, ic=ic: E.dma_start(
                            out=wdf[:],
                            in_=AP(wdT.tensor, (ic * 512 + 256) * D,
                                   [[D, 128], [128 * D, 2], [1, D]])),
                            gc_wd[1], 16)
                    else:
                        wdma[(mat, 1)] = P.op(
                            "gpsimd", lambda E, mat=mat, ic=ic: E.dma_start(
                                out=wbufs[mat][:],
                                in_=AP(wsrcs[mat].tensor, 512 * INNER + ic * 512,
                                       [[INNER, 128], [128 * INNER, 4], [1, 512]])),
                            wsems[mat][1], 16)
                else:
                    wbuf_free[mat] = tw
                P.wait("vector", ss, tw)
                P.wait("vector", vv, qcons[0])
                P.op("vector", lambda E: E.tensor_scalar(wq_c1[:], wq_t[:], 0.5, None, OP.is_gt))
                cmp2 = P.op("vector", lambda E: E.tensor_scalar(wq_c2[:], wq_t[:], -0.5, None, OP.is_lt), vv)
                qfree[0] = cmp2
                P.wait("vector", vv, cmp2)
                if mat == "d":
                    dst = wdq_sb[:, 4 * ic + 2 * hf:4 * ic + 2 * hf + 2, :]
                else:
                    if hf == 0:
                        P.wait("vector", tt, gu_used[icpar])
                    dst = wqsb[mat][:, icpar, 4 * hf:4 * hf + 4, :]
                qq_ = P.op("vector", lambda E, dst=dst: E.tensor_tensor(
                    dst.rearrange("p a b -> p (a b)"), wq_c1[:], wq_c2[:], OP.subtract), vv)
                qcons[0] = qq_
                if mat != "d":
                    quv[mat][icpar] = qq_

            UNITS = [("g", 0), ("g", 1), ("u", 0), ("u", 1), ("d", 0), ("d", 1)]
            unit_ts = [1 + i * max(1, (NT - 2) // 6) for i in range(6)]

            x1t_used = [0, 0]
            x1t_used_s = [0, 0]
            xq3_done = [0] * NT
            i32_free = 0
            tr3_ev = [0, 0]
            emit_wdma(0)
            with ExitStack() as stC0:
                e0 = stC0.enter_context
                x1t = e0(nc.sbuf_tensor("x1t", [128, 2, D], F32))
                xn3 = e0(nc.sbuf_tensor("xn3", [128, 2, D], F32))
                xq_i32a = e0(nc.sbuf_tensor("xq_i32a", [128, D], I32))
                d_bfa = e0(nc.sbuf_tensor("d_bfa", [128, D], F32))
                xq3_tm = e0(nc.sbuf_tensor("xq3_tm", [128, D], BF16))
                mv3a = e0(nc.sbuf_tensor("mv3a", [128, 2, 2], F32))
                rs3a = e0(nc.sbuf_tensor("rs3a", [128, 2], F32))
                nb3a = e0(nc.sbuf_tensor("nb3a", [128, 2], F32))
                c0_dma = [0, 0]
                c0_apply = [0, 0]
                dd1_c = [0, 0]

                def c0_dma_x1(t):
                    par = t % 2
                    P.wait("sync", vv, x1t_used[par])
                    P.wait("sync", ss, x1t_used_s[par])
                    c0_dma[par] = P.op("sync", lambda E, t=t, par=par: E.dma_start(
                        out=x1t[:, par, :], in_=x1_dram[t * 128:(t + 1) * 128, :]), sc_xr[par], 16)

                def c0_ln(t):
                    par = t % 2
                    P.wait("vector", sc_xr[par], c0_dma[par])
                    if t == 0:
                        P.wait("vector", vv, swb)
                        P.wait("vector", vv, mgd)
                    P.op("vector", lambda E, par=par: E.bn_stats(st3[:, 0:6], x1t[:, par, 0:512]))
                    sb = P.op("vector", lambda E, par=par: E.bn_stats(st3[:, 6:12], x1t[:, par, 512:1024]), vv)
                    P.wait("vector", vv, sb)
                    ag = P.op("vector", lambda E, par=par: E.bn_aggr(mv3a[:, par, :], st3[:]), vv)
                    P.wait("scalar", vv, ag)
                    lnx = P.op("scalar", lambda E, par=par: E.activation(
                        rs3a[:, par:par + 1], mv3a[:, par, 1:2], AF.Ln, bias=eps_sb[:], scale=1.0), ss)
                    P.wait("scalar", ss, lnx)
                    rr = P.op("scalar", lambda E, par=par: E.activation(
                        rs3a[:, par:par + 1], rs3a[:, par:par + 1], AF.Exp, scale=-0.5), ss)
                    P.wait("vector", ss, rr)
                    nb = P.op("vector", lambda E, par=par: E.scalar_tensor_tensor(
                        nb3a[:, par:par + 1], mv3a[:, par, 0:1], -1.0, rs3a[:, par:par + 1],
                        OP.mult, OP.mult), vv)
                    P.wait("scalar", vv, nb)
                    P.wait("scalar", vv, dd1_c[par])
                    c0_apply[par] = P.op("scalar", lambda E, par=par: E.activation(
                        xn3[:, par, :], x1t[:, par, :], AF.Identity,
                        bias=nb3a[:, par:par + 1], scale=rs3a[:, par:par + 1]), ss)
                    x1t_used_s[par] = c0_apply[par]
                    x1t_used[par] = sb

                def c0_quant(t):
                    par = t % 2
                    P.wait("vector", ss, c0_apply[par])
                    am = P.op("vector", lambda E, t=t, par=par: E.reduce_max(
                        sc1[:, t:t + 1], xn3[:, par, :], axis=AX.X, apply_absolute_value=True), vv)
                    P.wait("vector", vv, am)
                    mc1 = P.op("vector", lambda E, t=t: E.tensor_scalar(
                        sc1[:, t:t + 1], sc1[:, t:t + 1], 1e-5, None, OP.max), vv)
                    P.wait("vector", vv, mc1)
                    P.op("vector", lambda E, t=t: E.scalar_tensor_tensor(
                        rsg[:, t:t + 1], sc1[:, t:t + 1], 1.0 / 127.0, mwg_b[:, 0:1], OP.mult, OP.mult))
                    ru = P.op("vector", lambda E, t=t: E.scalar_tensor_tensor(
                        rsu[:, t:t + 1], sc1[:, t:t + 1], 1.0 / 127.0, mwg_b[:, 1:2], OP.mult, OP.mult), vv)
                    P.wait("vector", vv, ru)
                    rc = P.op("vector", lambda E, t=t: E.reciprocal(sc1[:, t:t + 1], sc1[:, t:t + 1]), vv)
                    P.wait("vector", vv, rc)
                    s127 = P.op("vector", lambda E, t=t: E.tensor_scalar(
                        sc1[:, t:t + 1], sc1[:, t:t + 1], 127.0, None, OP.mult), vv)
                    P.wait("scalar", vv, s127)
                    P.wait("scalar", vv, i32_free)
                    qi32 = P.op("scalar", lambda E, t=t, par=par: E.activation(
                        xq_i32a[:], xn3[:, par, :], AF.Identity, scale=sc1[:, t:t + 1]), ss)
                    P.wait("vector", ss, qi32)
                    dd1 = P.op("vector", lambda E, t=t, par=par: E.scalar_tensor_tensor(
                        d_bfa[:], xn3[:, par, :], sc1[:, t:t + 1], xq_i32a[:], OP.mult, OP.subtract), vv)
                    dd1_c[par] = dd1
                    P.wait("scalar", vv, dd1)
                    da = P.op("scalar", lambda E: E.activation(
                        d_bfa[:], d_bfa[:], AF.Identity, bias=magic_c[:], scale=1.0), ss)
                    P.wait("scalar", ss, da)
                    dd2b = P.op("scalar", lambda E: E.activation(
                        d_bfa[:], d_bfa[:], AF.Identity, bias=nmagic_c[:], scale=1.0), ss)
                    P.wait("vector", ss, dd2b)
                    return P.op("vector", lambda E: E.scalar_tensor_tensor(
                        xq3_tm[:], d_bfa[:], 1.0, xq_i32a[:], OP.mult, OP.add), vv)

                def c0_transp(t, i32f):
                    P.wait("tensor", vv, i32f)
                    for cg in range(2):
                        tpp = (2 * t + cg) % 2
                        P.wait("tensor", vv, tr3_ev[tpp])
                        for c4 in range(4):
                            c = cg * 4 + c4
                            tp = P.op("tensor", lambda E, c=c, c4=c4, tpp=tpp: E.transpose(
                                ps_t3[tpp][:, c4 * 128:(c4 + 1) * 128],
                                xq3_tm[:, c * 128:(c + 1) * 128], id_bf[:]), tt)
                        P.wait("vector", tt, tp)
                        tr3_ev[tpp] = P.op("vector", lambda E, t=t, cg=cg, tpp=tpp: E.tensor_copy(
                            AP(xq3_fm[:].tensor, cg * 4 * S + t * 128,
                               [[8 * S, 128], [S, 4], [1, 128]]),
                            ps_t3[tpp][:].rearrange("p (a b) -> p a b", a=4)), vv)
                    xq3_done[t] = tr3_ev[(2 * t + 1) % 2]

                c0_dma_x1(0)
                if NT > 1:
                    c0_dma_x1(1)
                c0_ln(0)
                for t in range(NT):
                    if t + 1 < NT:
                        c0_ln(t + 1)
                    i32f = c0_quant(t)
                    c0_transp(t, i32f)
                    if t + 2 < NT:
                        c0_dma_x1(t + 2)
                    for ui, (mat, hf) in enumerate(UNITS):
                        if unit_ts[ui] == t + 1 or (t == NT - 1 and unit_ts[ui] > NT - 1):
                            emit_qunit(mat, 0, hf)

            silu_done = [0, 0]
            prod_done = [0, 0]
            prod_st = [0, 0]
            for ic in range(8):
                icpar = ic % 2
                if ic + 1 < 8:
                    emit_wdma(ic + 1)
                P.wait("tensor", vv, quv["g"][icpar])
                P.wait("tensor", vv, quv["u"][icpar])
                for t in range(NT):
                    tpar = t % 2
                    if ic == 0:
                        P.wait("tensor", vv, xq3_done[t])
                    P.wait("tensor", ss, silu_done[tpar])
                    P.wait("tensor", vv, prod_done[tpar])
                    for kc in range(8):
                        P.op("tensor", lambda E, kc=kc, t=t, icpar=icpar, tpar=tpar: E.matmul(
                            ps_g[tpar][:], xq3_fm[:, kc, t * 128:(t + 1) * 128],
                            wgq_sb[:, icpar, kc, :], start=(kc == 0), stop=(kc == 7)))
                    for kc in range(8):
                        mmu = P.op("tensor", lambda E, kc=kc, t=t, icpar=icpar, tpar=tpar: E.matmul(
                            ps_u[tpar][:], xq3_fm[:, kc, t * 128:(t + 1) * 128],
                            wuq_sb[:, icpar, kc, :], start=(kc == 0), stop=(kc == 7)), tt)
                    gu_used[icpar] = mmu
                    P.wait("scalar", tt, mmu)
                    P.wait("scalar", vv, prod_done[tpar])
                    silu_done[tpar] = P.op("scalar", lambda E, t=t, tpar=tpar: E.activation(
                        g_act[:, tpar, :], ps_g[tpar][:], AF.Silu, scale=rsg[:, t:t + 1]), ss)
                    P.wait("vector", ss, silu_done[tpar])
                    P.wait("vector", sc_ps[tpar], prod_st[tpar])
                    prod_done[tpar] = P.op("vector", lambda E, tpar=tpar: E.tensor_tensor(
                        prod_sb[:, tpar, :], g_act[:, tpar, :], ps_u[tpar][:], OP.mult), vv)
                    P.wait("sync", vv, prod_done[tpar])
                    prod_st[tpar] = P.op("sync", lambda E, ic=ic, t=t, tpar=tpar: E.dma_start(
                        out=AP(gu_dram.tensor, t * 128 * INNER + ic * 512, [[INNER, 128], [1, 512]]),
                        in_=prod_sb[:, tpar, :]), sc_ps[tpar], 16)
                    if ic + 1 < 8:
                        for ui, (mat, hf) in enumerate(UNITS):
                            if unit_ts[ui] == t + 1 or (t == NT - 1 and unit_ts[ui] > NT - 1):
                                emit_qunit(mat, ic + 1, hf)

            stCG.close()
            stQ3.close()
            P.barrier(ALLSEMS)
            with ExitStack() as stCD:
                eD = stCD.enter_context
                gu_t = eD(nc.sbuf_tensor("gu_t", [128, 2, INNER], F32))
                x1t2 = eD(nc.sbuf_tensor("x1t2", [128, 2, D], F32))
                xqgu_tm = eD(nc.sbuf_tensor("xqgu_tm", [128, 2, INNER], BF16))
                xqgu_fm = eD(nc.sbuf_tensor("xqgu_fm", [128, 2, 32, 128], BF16))
                xq_i32b = eD(nc.sbuf_tensor("xq_i32b", [128, INNER], I32))
                d_bfb = eD(nc.sbuf_tensor("d_bfb", [128, INNER], F32))
                mpt = eD(nc.sbuf_tensor("mpt", [128, NT, 3], F32))
                ffo = eD(nc.sbuf_tensor("ffo", [128, 2, D], F32))
                outb = eD(nc.sbuf_tensor("outb", [128, 2, D], F32))
                ps_d = [eD(nc.psum_tensor(f"ps_d{i}", [128, D], F32)) for i in range(2)]

                gu_free = [0, 0]
                gu_free_s = [0, 0]
                gu_free_g = [0, 0]
                x1_free = [0, 0]
                ffo_used = [0, 0]
                ffo_ev = [0, 0]
                out_st = [0, 0]
                d_gu = [0, 0]
                d_x1 = [0, 0]
                dd1_d = [0, 0]
                dd2_d = [0, 0]
                comb_d = [0, 0]
                qi32_d = [0, 0]
                copy_d = {}
                tr_d = {}
                mm_d = [0, 0]

                def cd_dma_gu(t):
                    par = t % 2
                    P.wait("sync", vv, gu_free[par])
                    P.wait("sync", ss, gu_free_s[par])
                    d_gu[par] = P.op("sync", lambda E, t=t, par=par: E.dma_start(
                        out=gu_t[:, par, :],
                        in_=AP(gu_dram.tensor, t * 128 * INNER, [[INNER, 128], [1, INNER]])), sc_x1[par], 16)

                def cd_dma_x1(t):
                    par = t % 2
                    P.wait("sync", vv, x1_free[par])
                    d_x1[par] = P.op("sync", lambda E, t=t, par=par: E.dma_start(
                        out=x1t2[:, par, :], in_=x1_dram[t * 128:(t + 1) * 128, :]), sc_xr[par], 16)

                def cd_quant(t):
                    par = t % 2
                    P.wait("vector", sc_x1[par], d_gu[par])
                    P.wait("vector", vv, mgd)
                    am = P.op("vector", lambda E, t=t, par=par: E.reduce_max(
                        mpt[:, t, 0:1], gu_t[:, par, :], axis=AX.X, apply_absolute_value=True), vv)
                    gu_free_g[par] = 0
                    P.wait("vector", vv, am)
                    mc2 = P.op("vector", lambda E, t=t: E.tensor_scalar(
                        mpt[:, t, 1:2], mpt[:, t, 0:1], rsu[:, t:t + 1], 1e-5, OP.mult, OP.max), vv)
                    P.wait("vector", vv, mc2)
                    P.op("vector", lambda E, t=t: E.scalar_tensor_tensor(
                        rsd[:, t:t + 1], mpt[:, t, 1:2], 1.0 / 127.0, mwg_b[:, 2:3], OP.mult, OP.mult))
                    rc = P.op("vector", lambda E, t=t: E.reciprocal(mpt[:, t, 2:3], mpt[:, t, 1:2]), vv)
                    P.wait("vector", vv, rc)
                    v2d = P.op("vector", lambda E, t=t: E.scalar_tensor_tensor(
                        mpt[:, t, 2:3], mpt[:, t, 2:3], 127.0, rsu[:, t:t + 1], OP.mult, OP.mult), vv)
                    P.wait("scalar", vv, v2d)
                    P.wait("scalar", vv, max(comb_d))
                    qi32_d[par] = P.op("scalar", lambda E, t=t, par=par: E.activation(
                        xq_i32b[:], gu_t[:, par, :], AF.Identity, scale=mpt[:, t, 2:3]), ss)
                    gu_free_s[par] = qi32_d[par]
                    P.wait("vector", ss, qi32_d[par])
                    P.wait("vector", vv, max(comb_d))
                    dd1_d[par] = P.op("vector", lambda E, t=t, par=par: E.scalar_tensor_tensor(
                        d_bfb[:], gu_t[:, par, :], mpt[:, t, 2:3], xq_i32b[:],
                        OP.mult, OP.subtract), vv)
                    gu_free[par] = dd1_d[par]
                    P.wait("scalar", vv, dd1_d[par])
                    da = P.op("scalar", lambda E: E.activation(
                        d_bfb[:], d_bfb[:], AF.Identity, bias=magic_c[:], scale=1.0), ss)
                    P.wait("scalar", ss, da)
                    dd2_d[par] = P.op("scalar", lambda E: E.activation(
                        d_bfb[:], d_bfb[:], AF.Identity, bias=nmagic_c[:], scale=1.0), ss)
                    P.wait("vector", ss, dd2_d[par])
                    P.wait("vector", tt, tr_d.get((t - 2, 7), 0))
                    comb_d[par] = P.op("vector", lambda E, par=par: E.scalar_tensor_tensor(
                        xqgu_tm[:, par, :], d_bfb[:], 1.0, xq_i32b[:],
                        OP.mult, OP.add), vv)

                def cd_transp(t):
                    par = t % 2
                    P.wait("tensor", vv, comb_d[par])
                    P.wait("scalar", tt, mm_d[par])
                    for cg in range(8):
                        tpp = cg % 2
                        prev = copy_d[(t, cg - 2)] if cg >= 2 else copy_d.get((t - 1, 6 + tpp), 0)
                        P.wait("tensor", ss, prev)
                        for c4 in range(4):
                            c = cg * 4 + c4
                            tp = P.op("tensor", lambda E, c=c, c4=c4, tpp=tpp, par=par: E.transpose(
                                ps_t3[tpp][:, c4 * 128:(c4 + 1) * 128],
                                xqgu_tm[:, par, c * 128:(c + 1) * 128], id_bf[:]), tt)
                        tr_d[(t, cg)] = tp
                        P.wait("scalar", tt, tp)
                        copy_d[(t, cg)] = P.op("scalar", lambda E, cg=cg, tpp=tpp, par=par: E.activation(
                            xqgu_fm[:, par, cg * 4:(cg + 1) * 4, :].rearrange("p a b -> p (a b)"),
                            ps_t3[tpp][:], AF.Identity, scale=1.0), ss)

                def cd_mm(t):
                    par = t % 2
                    P.wait("tensor", ss, copy_d[(t, 7)])
                    P.wait("tensor", ss, ffo_ev[par])
                    for fh in range(2):
                        for kc in range(32):
                            mmd = P.op("tensor", lambda E, kc=kc, fh=fh, par=par: E.matmul(
                                ps_d[par][:, fh * 512:(fh + 1) * 512], xqgu_fm[:, par, kc, :],
                                wdq_sb[:, kc, fh * 512:(fh + 1) * 512],
                                start=(kc == 0), stop=(kc == 31)), tt)
                        P.wait("scalar", tt, mmd)
                        if fh == 0:
                            P.wait("scalar", vv, ffo_used[par])
                        ffo_ev[par] = P.op("scalar", lambda E, t=t, fh=fh, par=par: E.activation(
                            ffo[:, par, fh * 512:(fh + 1) * 512],
                            ps_d[par][:, fh * 512:(fh + 1) * 512],
                            AF.Identity, scale=rsd[:, t:t + 1]), ss)
                    mm_d[par] = mmd

                def cd_ln4(t):
                    par = t % 2
                    P.wait("vector", ss, ffo_ev[par])
                    P.op("vector", lambda E, par=par: E.bn_stats(st3[:, 0:6], ffo[:, par, 0:512]))
                    sb4 = P.op("vector", lambda E, par=par: E.bn_stats(st3[:, 6:12], ffo[:, par, 512:1024]), vv)
                    P.wait("vector", vv, sb4)
                    ag4 = P.op("vector", lambda E: E.bn_aggr(mv3[:], st3[:]), vv)
                    P.wait("scalar", vv, ag4)
                    lnx4 = P.op("scalar", lambda E: E.activation(rs3[:], mv3[:, 1:2], AF.Ln, bias=eps_sb[:], scale=1.0), ss)
                    P.wait("scalar", ss, lnx4)
                    rr4 = P.op("scalar", lambda E: E.activation(rs3[:], rs3[:], AF.Exp, scale=-0.5), ss)
                    P.wait("vector", ss, rr4)
                    P.wait("vector", sc_out[par], out_st[par])
                    ow = P.op("vector", lambda E, par=par: E.tensor_scalar(
                        outb[:, par, :], ffo[:, par, :], mv3[:, 0:1], rs3[:], OP.subtract, OP.mult), vv)
                    ffo_used[par] = ow
                    P.wait("vector", vv, ow)
                    P.wait("vector", sc_xr[par], d_x1[par])
                    ox = P.op("vector", lambda E, par=par: E.tensor_tensor(
                        outb[:, par, :], outb[:, par, :], x1t2[:, par, :], OP.add), vv)
                    x1_free[par] = ox
                    P.wait("sync", vv, ox)
                    out_st[par] = P.op("sync", lambda E, t=t, par=par: E.dma_start(
                        out=out[t * 128:(t + 1) * 128, :], in_=outb[:, par, :]), sc_out[par], 16)

                cd_dma_gu(0)
                if NT > 1:
                    cd_dma_gu(1)
                cd_dma_x1(0)
                cd_quant(0)
                if NT > 1:
                    cd_quant(1)
                cd_transp(0)
                for i in range(NT):
                    if i + 2 < NT:
                        cd_dma_gu(i + 2)
                    if i >= 1:
                        cd_ln4(i - 1)
                    if i + 1 < NT:
                        cd_dma_x1(i + 1)
                    if i + 2 < NT:
                        cd_quant(i + 2)
                    cd_mm(i)
                    if i + 1 < NT:
                        cd_transp(i + 1)
                cd_ln4(NT - 1)

                P.wait("sync", sc_out[0], out_st[0])
                P.wait("sync", sc_out[1], out_st[1])
                P.wait("gpsimd", sc_out[0], out_st[0])
                P.wait("gpsimd", sc_out[1], out_st[1])

        with nc.Block() as block:
            @block.tensor
            def _(E):
                for f in P.ev["tensor"]:
                    f(E)

            @block.vector
            def _(E):
                for f in P.ev["vector"]:
                    f(E)

            @block.scalar
            def _(E):
                for f in P.ev["scalar"]:
                    f(E)

            @block.gpsimd
            def _(E):
                for f in P.ev["gpsimd"]:
                    f(E)

            @block.sync
            def _(E):
                for f in P.ev["sync"]:
                    f(E)

    return nc, dbg


_CACHE = {}


def _host_inputs(L0_attn_W_w, L0_attn_W_b, L0_attn_proj_w, L0_attn_proj_b,
                 L0_attn_ln_g, L0_attn_ln_b, L0_attn_rpb,
                 L0_ffn_w_gate, L0_ffn_w_up, L0_ffn_w_down):
    f = np.asarray
    Ww = f(L0_attn_W_w, dtype=np.float32)
    Wb = f(L0_attn_W_b, dtype=np.float32)
    rpb = f(L0_attn_rpb, dtype=np.float32)
    shared = {
        "wqkvT": np.ascontiguousarray(Ww.T),
        "wb_pc": np.ascontiguousarray(Wb.reshape(24, 128).T),
        "wbv_row": np.ascontiguousarray(Wb[2048:3072].reshape(1, 1024)),
        "wprojT": np.ascontiguousarray(f(L0_attn_proj_w, dtype=np.float32).T),
        "projb_row": np.ascontiguousarray(f(L0_attn_proj_b, dtype=np.float32).reshape(1, 1024)),
        "lng_row": np.ascontiguousarray(f(L0_attn_ln_g, dtype=np.float32).reshape(1, 1024)),
        "lnb_row": np.ascontiguousarray(f(L0_attn_ln_b, dtype=np.float32).reshape(1, 1024)),
        "rpbmt": np.ascontiguousarray(rpb[::-1, :].T).reshape(-1),
        "wgT": np.ascontiguousarray(f(L0_ffn_w_gate, dtype=np.float32).T),
        "wuT": np.ascontiguousarray(f(L0_ffn_w_up, dtype=np.float32).T),
        "wdT": np.ascontiguousarray(f(L0_ffn_w_down, dtype=np.float32).T),
        "id_in": np.eye(128, dtype=np.float32),
        "j_in": np.eye(128, dtype=np.float32)[::-1].copy(),
    }
    m = np.arange(128)[:, None] + np.arange(128)[None, :]
    shared["mask_in"] = np.where(m <= 126, np.float32(NEG), np.float32(0.0))
    return shared


def kernel(L0_attn_x, L0_attn_W_w, L0_attn_W_b, L0_attn_proj_w, L0_attn_proj_b,
           L0_attn_ln_g, L0_attn_ln_b, L0_attn_rpb, L0_ffn_w_gate, L0_ffn_w_up,
           L0_ffn_w_down):
    x = np.asarray(L0_attn_x, dtype=np.float32)
    B = x.shape[0]
    if "nc" not in _CACHE:
        _CACHE["nc"] = build(S=2048, b_dbg=False)[0]
    nc = _CACHE["nc"]
    shared = _host_inputs(L0_attn_W_w, L0_attn_W_b, L0_attn_proj_w, L0_attn_proj_b,
                          L0_attn_ln_g, L0_attn_ln_b, L0_attn_rpb,
                          L0_ffn_w_gate, L0_ffn_w_up, L0_ffn_w_down)
    in_maps = []
    for b in range(B):
        m = dict(shared)
        m["x"] = np.ascontiguousarray(x[b])
        in_maps.append(m)
    res = run_bass_kernel_spmd(nc, in_maps, core_ids=list(range(B)))
    return np.stack([res.results[b]["out"] for b in range(B)], axis=0)
